# version 10
# speedup vs baseline: 1.1771x; 1.0256x over previous
import numpy as np
from contextlib import ExitStack
import concourse.bass as bass
import concourse.mybir as mybir
from concourse.bass_utils import run_bass_kernel_spmd

F32 = mybir.dt.float32
BF16 = mybir.dt.bfloat16
AF = mybir.ActivationFunctionType
ALU = mybir.AluOpType
AX = mybir.AxisListType

P = 128
D = 2048
NCORES = 8
TOK = 1024
NT = TOK // P
HALO = 512
LT = (TOK + HALO) // P
NCH = D // P
NEG = -3.0e38
W0 = 512
KW = W0 + P
INV_SQRT_HD = 0.08838834764831845
L0_STAGE = 3
MOE_STAGE = 3
L1_STAGE = 3
DBG_TAP = False


def _chunk_stream(w, oc_size=P):
    K, M = w.shape
    n_oc = M // oc_size
    return np.ascontiguousarray(
        w.reshape(K // P, P, n_oc, oc_size).transpose(2, 1, 0, 3).reshape(n_oc, P, -1))


def _rope_perm():
    pm = np.zeros(P, dtype=np.int64)
    pm[0:64] = np.arange(0, P, 2)
    pm[64:P] = np.arange(1, P, 2)
    return pm


def make_host_inputs(inputs):
    f32 = np.float32
    x = np.asarray(inputs["L0_attn_x"], f32).reshape(2 * 4096, D)
    per_core = [dict() for _ in range(NCORES)]
    shared = {}

    for c in range(NCORES):
        blkx = np.zeros((TOK + HALO, D), f32)
        if c % 4 == 0:
            blkx[HALO:] = x[c * TOK:(c + 1) * TOK]
        else:
            blkx[:] = x[c * TOK - HALO:(c + 1) * TOK]
        per_core[c]["x"] = blkx

    pm = _rope_perm()
    wq = np.asarray(inputs["L0_attn_wq"], f32)
    wk = np.asarray(inputs["L0_attn_wk"], f32)
    wv = np.asarray(inputs["L0_attn_wv"], f32)
    wo = np.asarray(inputs["L0_attn_wo"], f32)
    qperm = (np.arange(16)[:, None] * P + pm[None, :]).reshape(-1)
    kperm = (np.arange(4)[:, None] * P + pm[None, :]).reshape(-1)
    shared["wq_s"] = _chunk_stream(wq[:, qperm])
    shared["wk_s"] = _chunk_stream(wk[:, kperm])
    shared["wv_s"] = _chunk_stream(wv)
    shared["wo_s"] = _chunk_stream(wo)
    shared["lnw_row"] = np.asarray(inputs["L0_attn_ln_w"], f32).reshape(1, D)

    inv_freq = (1.0 / 10000.0 ** (np.arange(0, 128, 2) / 128.0)).astype(f32)
    for c in range(NCORES):
        pos0 = (c % 4) * TOK - HALO
        pos = pos0 + np.arange(TOK + HALO, dtype=f32)
        if c % 4 == 0:
            pos = np.maximum(pos, 0.0)
        ang = inv_freq[:, None] * pos[None, :]
        per_core[c]["cs_fm"] = np.concatenate([np.cos(ang), np.cos(ang)], 0).astype(f32)
        per_core[c]["sn_fm"] = np.concatenate([-np.sin(ang), np.sin(ang)], 0).astype(f32)

    r = np.arange(P)[:, None]
    j = np.arange(KW)[None, :]
    band = (j >= r) & (j <= r + W0)
    for c in range(NCORES):
        m = np.zeros((NT, P, KW), f32)
        for t in range(NT):
            ok = band.copy()
            if c % 4 == 0:
                ok = ok & ((t * P + j) >= HALO)
            m[t] = np.where(ok, 0.0, NEG)
        per_core[c]["mask0"] = m

    gw = np.asarray(inputs["L0_ffn_gate_w"], f32)
    shared["gwt"] = np.ascontiguousarray(gw.T.reshape(NCH, P, 32).transpose(1, 0, 2).reshape(P, NCH * 32))
    wg = np.asarray(inputs["L0_ffn_w_gate_proj"], f32)
    wu = np.asarray(inputs["L0_ffn_w_up_proj"], f32)
    wd = np.asarray(inputs["L0_ffn_w_down_proj"], f32)
    shared["wg_t"] = np.ascontiguousarray(wg.transpose(0, 2, 1).reshape(32, NCH, P, 768)
                                          .transpose(0, 2, 1, 3).reshape(32, P, NCH * 768))
    shared["wu_t"] = np.ascontiguousarray(wu.transpose(0, 2, 1).reshape(32, NCH, P, 768)
                                          .transpose(0, 2, 1, 3).reshape(32, P, NCH * 768))
    shared["wd_t"] = np.ascontiguousarray(wd.transpose(0, 2, 1).reshape(32, 6, P, D)
                                          .transpose(0, 2, 1, 3).reshape(32, P, 6 * D))

    wqkv = np.asarray(inputs["L1_attn_w_qkv"], f32)
    wgate1 = np.asarray(inputs["L1_attn_w_gate"], f32)
    wout1 = np.asarray(inputs["L1_attn_w_out"], f32)
    shared["wqkv_s"] = _chunk_stream(np.ascontiguousarray(wqkv.T))
    shared["wgate_s"] = _chunk_stream(np.ascontiguousarray(wgate1.T))
    shared["wot_r"] = np.ascontiguousarray(wout1.T.reshape(NCH, P, D))
    shared["normw_col"] = np.ascontiguousarray(
        np.asarray(inputs["L1_attn_norm_w"], f32).reshape(NCH, P).T)
    shared["slope"] = np.asarray(inputs["L1_attn_slope_rate"], f32).reshape(1, 16)
    cz = np.zeros((P, 512), f32)
    for sub in range(2):
        for khat in range(P):
            kg = sub * P + khat
            m = np.arange(256)
            cz[khat, sub * 256:(sub + 1) * 256] = (m >= kg).astype(f32)
    shared["causal01"] = cz
    shared["arqneg"] = (-(np.arange(256, dtype=f32) + 1.0)).reshape(1, 256)
    ak = np.zeros((P, 2), f32)
    akp = np.zeros((P, 2), f32)
    for sub in range(2):
        ak[:, sub] = -(255.0 - (sub * P + np.arange(P, dtype=f32)))
        akp[:, sub] = (sub * P + np.arange(P, dtype=f32)) + 1.0
    shared["arkneg"] = ak
    shared["arkpos"] = akp
    for c in range(NCORES):
        crel = c % 4
        tc = np.full((1, 8), -1e8, f32)
        for r in range(NCORES):
            if r // 4 == c // 4 and (r % 4) < crel:
                tc[0, r] = -1024.0 * (crel - 1 - (r % 4))
        per_core[c]["tcneg"] = tc

    w1 = np.asarray(inputs["L1_ffn_w1"], f32)
    w2 = np.asarray(inputs["L1_ffn_w2"], f32)
    w1_s = _chunk_stream(np.ascontiguousarray(w1.T))
    w2_s = np.ascontiguousarray(w2.T.reshape(64, P, D))
    shared["w1_s"] = w1_s
    shared["w2_s"] = w2_s
    shared["b1col"] = np.ascontiguousarray(
        np.asarray(inputs["L1_ffn_b1"], f32).reshape(64, P).T)
    shared["b2row"] = np.asarray(inputs["L1_ffn_b2"], f32).reshape(1, D)
    for c in range(NCORES):
        per_core[c]["wpart"] = np.ascontiguousarray(
            np.concatenate([w1_s[c * 8:(c + 1) * 8], w2_s[c * 8:(c + 1) * 8]], axis=0))

    shared["ident"] = np.eye(P, dtype=f32)
    psw = np.zeros((P, P), f32)
    for i in range(P):
        psw[(i + 64) % P, i] = 1.0
    shared["pswap"] = psw
    shared["ones_row"] = np.ones((1, P), f32)
    shared["ones_col"] = np.ones((P, 1), f32)
    return per_core, shared



class Sem:
    def __init__(self, nc, es, name):
        self.h = es.enter_context(nc.semaphore(name))
        self.n = 0

    def inc(self, instr, d=1):
        instr.then_inc(self.h, d)
        self.n += d
        return self.n


class Ctx:
    def __init__(self, nc, es, block):
        self.nc, self.es, self.block = nc, es, block
        self.pe = Sem(nc, es, "s_pe")
        self.dv = Sem(nc, es, "s_dv")
        self.ac = Sem(nc, es, "s_ac")
        self.gp = Sem(nc, es, "s_gp")
        self.ds = Sem(nc, es, "s_ds")
        self.da = Sem(nc, es, "s_da")
        self.dg = Sem(nc, es, "s_dg")
        self.cc = Sem(nc, es, "s_cc")
        self.ins = {}
        self.scr = {}
        self._scope = []

    def sbuf(self, name, shape, dtype):
        t = self._scope[-1].enter_context(self.nc.sbuf_tensor(name, shape, dtype)) \
            if self._scope else self.es.enter_context(self.nc.sbuf_tensor(name, shape, dtype))
        return t

    def dma(self, e, q, out, in_):
        s = {"ds": self.ds, "da": self.da, "dg": self.dg}[q]
        if s.n:
            e.wait_ge(s.h, s.n)
        return s.inc(e.dma_start(out=out, in_=in_), 16)

    def push_scope(self):
        self._scope.append(ExitStack())

    def pop_scope(self):
        self._scope.pop().close()

    def barrier(self):
        C = self
        targets = [(C.pe, "tensor"), (C.dv, "vector"), (C.ac, "scalar"),
                   (C.gp, "gpsimd"), (C.ds, "sync"), (C.da, None), (C.dg, None)]
        vals = [(sem, sem.n) for sem, eng in targets]
        for sem, eng in targets:
            if eng is None:
                continue
            def mk2(eng=eng, sem=sem):
                def f(e):
                    for s2, n2 in vals:
                        if s2 is not sem and n2:
                            e.wait_ge(s2.h, n2)
                return f
            getattr(C.block, eng)(mk2())



def dbuf(C, name, cols, dtype, n=2):
    t = C.sbuf(name, [P, n * cols], dtype)
    return [t[:, i * cols:(i + 1) * cols] for i in range(n)]


def emit_ln_tile(C, src, xn_out, scr, l0_rms=False, lnw_bc=None, pre_wait=None):
    block = C.block
    def w(e):
        if pre_wait:
            for s, n in pre_wait:
                e.wait_ge(s.h, n)

    def step(e, instr):
        C.dv.inc(instr)
        e.wait_ge(C.dv.h, C.dv.n)

    @block.scalar
    def _(e):
        w(e)
        e.activation(out=scr["dump"], in_=src, func=AF.Square, accum_out=scr["sumsq"][:])
        C.ac.inc(e.activation(out=scr["dump"], in_=src, func=AF.Identity,
                              accum_out=scr["sums"][:]))
    ac_stats = C.ac.n

    @block.vector
    def _(e):
        e.wait_ge(C.ac.h, ac_stats)
        step(e, e.tensor_scalar(out=scr["mean"][:], in0=scr["sums"][:], scalar1=1.0 / D,
                                scalar2=None, op0=ALU.mult))
        step(e, e.tensor_scalar(out=scr["t0"][:], in0=scr["mean"][:], scalar1=scr["mean"][:],
                                scalar2=None, op0=ALU.mult))
        step(e, e.tensor_scalar(out=scr["var"][:], in0=scr["sumsq"][:], scalar1=1.0 / D,
                                scalar2=scr["t0"][:], op0=ALU.mult, op1=ALU.subtract))
        C.dv.inc(e.tensor_scalar(out=scr["vare"][:], in0=scr["var"][:], scalar1=1e-5,
                                 scalar2=None, op0=ALU.add))
    dv_var = C.dv.n

    @block.scalar
    def _(e):
        e.wait_ge(C.dv.h, dv_var)
        C.ac.inc(e.activation(out=scr["std"][:], in_=scr["vare"][:], func=AF.Sqrt))
    ac_std = C.ac.n

    @block.vector
    def _(e):
        e.wait_ge(C.ac.h, ac_std)
        C.dv.inc(e.reciprocal(scr["rstd"][:], scr["std"][:]))
        e.wait_ge(C.dv.h, C.dv.n)

    if l0_rms:
        @block.vector
        def _(e):
            step(e, e.tensor_scalar(out=scr["t0"][:], in0=scr["var"][:], scalar1=scr["rstd"][:],
                                    scalar2=scr["rstd"][:], op0=ALU.mult, op1=ALU.mult))
            C.dv.inc(e.tensor_scalar(out=scr["t0"][:], in0=scr["t0"][:], scalar1=1e-6,
                                     scalar2=None, op0=ALU.add))
        dv_t = C.dv.n

        @block.scalar
        def _(e):
            e.wait_ge(C.dv.h, dv_t)
            C.ac.inc(e.activation(out=scr["std"][:], in_=scr["t0"][:], func=AF.Sqrt))
        ac2 = C.ac.n

        @block.vector
        def _(e):
            e.wait_ge(C.ac.h, ac2)
            step(e, e.reciprocal(scr["t0"][:], scr["std"][:]))
            step(e, e.tensor_scalar(out=scr["s_t"][:], in0=scr["rstd"][:],
                                    scalar1=scr["t0"][:], scalar2=None, op0=ALU.mult))
        scale_ap = scr["s_t"]
    else:
        scale_ap = scr["rstd"]

    if lnw_bc is None:
        @block.vector
        def _(e):
            w(e)
            C.dv.inc(e.tensor_scalar(out=xn_out, in0=src, scalar1=scr["mean"][:],
                                     scalar2=scale_ap[:], op0=ALU.subtract, op1=ALU.mult))
        return ("dv", C.dv.n)
    else:
        @block.vector
        def _(e):
            w(e)
            C.dv.inc(e.tensor_scalar(out=scr["xtmp"][:][:, 0:D] if False else scr["xtmp"][:], in0=src, scalar1=scr["mean"][:],
                                     scalar2=scale_ap[:], op0=ALU.subtract, op1=ALU.mult))
        dvn = C.dv.n

        @block.gpsimd
        def _(e):
            e.wait_ge(C.dv.h, dvn)
            C.gp.inc(e.tensor_tensor(out=xn_out, in0=scr["xtmp"][:], in1=lnw_bc, op=ALU.mult))
        return ("gp", C.gp.n)


def mk_ln_scr(C, tag, dump_ap=None, need_xtmp=True):
    dump = dump_ap if dump_ap is not None else C.sbuf(tag + "dump", [P, D], BF16)[:]
    xtmp = C.sbuf(tag + "xtmp", [P, D], BF16) if need_xtmp else None
    out = []
    for i in range(2):
        scr = {k: C.sbuf(tag + k + str(i), [P, 1], F32) for k in
               ["sums", "sumsq", "mean", "var", "vare", "t0", "std", "rstd", "s_t"]}
        scr["dump"] = dump
        scr["xtmp"] = xtmp
        out.append(scr)
    return out


def emit_bcast_row(C, PS, bank, onesr, row_ap, out_ap, n):
    block = C.block
    @block.tensor
    def _(e):
        nb = 0
        for off in range(0, n, 512):
            w_ = min(512, n - off)
            ins_ = e.matmul(PS[:, bank * 512 + off: bank * 512 + off + w_],
                            onesr[:], row_ap[:, off:off + w_], start=True, stop=True)
        C.pe.inc(ins_)
    pn = C.pe.n

    @block.scalar
    def _(e):
        e.wait_ge(C.pe.h, pn)
        C.ac.inc(e.copy(out=out_ap, in_=PS[:, bank * 512: bank * 512 + n]))
    return C.ac.n



def emit_l0attn(C, X, PS):
    nc, block = C.nc, C.block
    ins = C.ins
    C.push_scope()
    kT = C.sbuf("l0kT", [P, 4 * 1536], BF16)
    vtok = C.sbuf("l0vtok", [P, LT * 512], BF16)
    identb = C.sbuf("identb", [P, P], BF16)
    pswapb = C.sbuf("pswapb", [P, P], BF16)
    onesr = C.sbuf("onesr", [1, P], BF16)
    C.push_scope()

    big0 = C.sbuf("big0", [P, NCH * 1536], BF16)
    wst = dbuf(C, "l0wst", NCH * P, BF16)
    csf = C.sbuf("l0csf", [P, 1536], BF16)
    snf = C.sbuf("l0snf", [P, 1536], BF16)
    lnwb = C.sbuf("l0lnwb", [P, D], BF16)
    lnwrow = C.sbuf("lnwrow", [1, D], BF16)
    xhalo = dbuf(C, "l0xhalo", D, F32)
    xn2 = dbuf(C, "l0xn2", D, BF16)
    qsb = dbuf(C, "l0qsb", 512, BF16)
    rsw = dbuf(C, "l0rsw", 512, BF16)
    rcs = dbuf(C, "l0rcs", 512, BF16)
    qasm = dbuf(C, "l0qasm", TOK, BF16)
    scr2 = mk_ln_scr(C, "l0ln")

    qT_d = C.scr["l0_qT"]

    @block.gpsimd
    def _(e):
        C.dma(e, "dg", out=csf[:], in_=ins["cs_fm"])
        C.dma(e, "dg", out=snf[:], in_=ins["sn_fm"])
        C.dma(e, "dg", out=identb[:], in_=ins["ident"])
        C.dma(e, "dg", out=pswapb[:], in_=ins["pswap"])
        C.dma(e, "dg", out=onesr[:], in_=ins["ones_row"])
        C.dma(e, "dg", out=lnwrow[:], in_=ins["lnw_row"])
    dg_const = C.dg.n

    @block.tensor
    def _(e):
        e.wait_ge(C.dg.h, dg_const)
    @block.scalar
    def _(e):
        e.wait_ge(C.dg.h, dg_const)
    ac_lnw = emit_bcast_row(C, PS, 0, onesr, lnwrow, lnwb[:], D)

    halo_ld = []
    @block.sync
    def _(e):
        for i in range(2):
            halo_ld.append(C.dma(e, "ds", out=xhalo[i], in_=ins["x"][i * P:(i + 1) * P, :]))

    pe_tr = []
    ac_ev = []
    rdy_hist = []
    for lt in range(LT):
        if lt < 4:
            src = xhalo[lt % 2]
            pre = [(C.ds, halo_ld[lt])]
        else:
            src = X[:, (lt - 4) * D:(lt - 3) * D]
            pre = []
        if lt >= 2:
            k2, n2 = rdy_hist[lt - 2]
            pre.append(({"dv": C.dv, "gp": C.gp}[k2], n2))
            k1, n1 = rdy_hist[lt - 1]
            pre.append(({"dv": C.dv, "gp": C.gp}[k1], n1))
            pv = pe_tr[lt - 2]
            @block.vector
            def _(e, pv=pv):
                e.wait_ge(C.pe.h, pv)
            @block.gpsimd
            def _(e, pv=pv):
                e.wait_ge(C.pe.h, pv)
        kind, n = emit_ln_tile(C, src, xn2[lt % 2], scr2[lt % 2], l0_rms=True, lnw_bc=lnwb[:],
                               pre_wait=pre)
        rdy_hist.append((kind, n))
        sem = {"dv": C.dv, "gp": C.gp}[kind]
        if lt + 2 < 4:
            @block.sync
            def _(e, sem=sem, n=n, lt=lt):
                e.wait_ge(sem.h, n)
                halo_ld.append(C.dma(e, "ds", out=xhalo[lt % 2],
                                                    in_=ins["x"][(lt + 2) * P:(lt + 3) * P, :]))
        bank = (lt % 2) * 2
        prev_ev = ac_ev[lt - 2] if lt >= 2 else ac_lnw
        @block.tensor
        def _(e, sem=sem, n=n, bank=bank, prev_ev=prev_ev, lt=lt):
            e.wait_ge(sem.h, n)
            e.wait_ge(C.ac.h, prev_ev)
            for dc in range(NCH):
                bb = bank + dc // 8
                view = PS[:, bb * 512:(bb + 1) * 512].bitcast(BF16)[:, (dc % 8) * P:(dc % 8 + 1) * P]
                ins_ = e.transpose(view, xn2[lt % 2][:, dc * P:(dc + 1) * P], identb[:])
            C.pe.inc(ins_)
        pe_tr.append(C.pe.n)
        pn = C.pe.n

        @block.scalar
        def _(e, bank=bank, lt=lt, pn=pn):
            e.wait_ge(C.pe.h, pn)
            for half in range(2):
                bb = bank + half
                out = big0[:].rearrange("p (c n) -> p c n", c=NCH)[:, half * 8:(half + 1) * 8,
                                                                   lt * P:(lt + 1) * P]
                i2 = PS[:, bb * 512:(bb + 1) * 512].bitcast(BF16).rearrange("p (c n) -> p c n", c=8)
                ins_ = e.copy(out=out, in_=i2)
            C.ac.inc(ins_)
        ac_ev.append(C.ac.n)
    ac_xnT = C.ac.n

    stream_ld = []
    bank_guard = {b: ("ac", 0) for b in range(8)}

    def guard_wait(e, b):
        kind, n = bank_guard[b]
        if n:
            e.wait_ge({"ac": C.ac, "dv": C.dv, "gp": C.gp, "ds": C.ds, "da": C.da}[kind].h, n)

    jobs = []
    for oc in range(4):
        jobs.append(("k", "wk_s", oc, 0, 1536))
    for oc in range(4):
        jobs.append(("v", "wv_s", oc, 0, 1536))
    for oc in range(NCH):
        jobs.append(("q", "wq_s", oc, HALO, TOK))

    @block.gpsimd
    def _(e):
        e.wait_ge(C.ac.h, ac_xnT)
        for i in range(2):
            tag, wname, oc, _, _ = jobs[i]
            stream_ld.append(C.dma(e, "dg", out=wst[i % 2], in_=ins[wname][oc]))

    vT_sb = dbuf(C, "l0vTsb", 1536, BF16)
    git = [0]
    rope_gp_hist = []
    qasm_spill = {0: 0, 1: 0}

    for ji, (tag, wname, oc, off, cols) in enumerate(jobs):
        need = stream_ld[ji]
        ngrp = cols // 512
        for g in range(ngrp):
            it = git[0]; git[0] += 1
            pb = 4 + (it % 2)
            @block.tensor
            def _(e, ji=ji, need=need, oc=oc, off=off, g=g, pb=pb, tag=tag):
                e.wait_ge(C.dg.h, need)
                if ji == 0 and g == 0:
                    e.wait_ge(C.ac.h, ac_xnT)
                guard_wait(e, pb)
                for dc in range(NCH):
                    lhsT = wst[ji % 2][:, dc * P:(dc + 1) * P]
                    rhs = big0[:, dc * 1536 + off + g * 512: dc * 1536 + off + (g + 1) * 512]
                    ins_ = e.matmul(PS[:, pb * 512:(pb + 1) * 512], lhsT, rhs,
                                    start=(dc == 0), stop=(dc == NCH - 1))
                C.pe.inc(ins_)
            pn = C.pe.n

            if tag == "v":
                @block.scalar
                def _(e, pn=pn, pb=pb, g=g, oc=oc):
                    e.wait_ge(C.pe.h, pn)
                    C.ac.inc(e.copy(out=vT_sb[oc % 2][:, g * 512:(g + 1) * 512],
                                    in_=PS[:, pb * 512:(pb + 1) * 512]))
                bank_guard[pb] = ("ac", C.ac.n)
            else:
                sb2 = 6 + (it % 2)
                ropehist = rope_gp_hist
                gguard = ropehist[-2] if len(ropehist) >= 2 else 0
                @block.scalar
                def _(e, pn=pn, pb=pb, it=it, gguard=gguard):
                    e.wait_ge(C.pe.h, pn)
                    if gguard:
                        e.wait_ge(C.gp.h, gguard)
                    C.ac.inc(e.copy(out=qsb[it % 2], in_=PS[:, pb * 512:(pb + 1) * 512]))
                an = C.ac.n
                bank_guard[pb] = ("ac", an)

                @block.tensor
                def _(e, an=an, sb2=sb2, it=it):
                    e.wait_ge(C.ac.h, an)
                    guard_wait(e, sb2)
                    C.pe.inc(e.matmul(PS[:, sb2 * 512:(sb2 + 1) * 512], pswapb[:],
                                      qsb[it % 2], start=True, stop=True))
                pn2 = C.pe.n

                cs_off = off + g * 512
                @block.vector
                def _(e, pn2=pn2, sb2=sb2, it=it, cs_off=cs_off, gguard=gguard):
                    e.wait_ge(C.pe.h, pn2)
                    if gguard:
                        e.wait_ge(C.gp.h, gguard)
                    C.dv.inc(e.tensor_tensor(out=rsw[it % 2],
                                             in0=PS[:, sb2 * 512:(sb2 + 1) * 512],
                                             in1=snf[:, cs_off:cs_off + 512], op=ALU.mult))
                dvn = C.dv.n
                bank_guard[sb2] = ("dv", dvn)

                if tag == "q":
                    out = qasm[oc % 2][:, g * 512:(g + 1) * 512]
                else:
                    out = kT[:, oc * 1536 + g * 512: oc * 1536 + (g + 1) * 512]
                qg = qasm_spill[oc % 2] if tag == "q" else 0
                @block.gpsimd
                def _(e, it=it, cs_off=cs_off, dvn=dvn, out=out, an=an, qg=qg):
                    e.wait_ge(C.ac.h, an)
                    if qg:
                        e.wait_ge(C.ds.h, qg)
                    e.tensor_tensor(out=rcs[it % 2], in0=qsb[it % 2],
                                    in1=csf[:, cs_off:cs_off + 512], op=ALU.mult)
                    e.wait_ge(C.dv.h, dvn)
                    C.gp.inc(e.tensor_tensor(out=out, in0=rcs[it % 2],
                                             in1=rsw[it % 2], op=ALU.add))
                rope_gp_hist.append(C.gp.n)

        if tag == "v":
            an_v = C.ac.n
            @block.tensor
            def _(e, an_v=an_v, oc=oc):
                e.wait_ge(C.ac.h, an_v)
                guard_wait(e, 2)
                guard_wait(e, 3)
                for lt in range(LT):
                    bb = 2 + lt // 8
                    view = PS[:, bb * 512:(bb + 1) * 512].bitcast(BF16)[:, (lt % 8) * P:(lt % 8 + 1) * P]
                    ins_ = e.transpose(view, vT_sb[oc % 2][:, lt * P:(lt + 1) * P], identb[:])
                C.pe.inc(ins_)
            pnv = C.pe.n

            @block.scalar
            def _(e, pnv=pnv, oc=oc):
                e.wait_ge(C.pe.h, pnv)
                out = vtok[:].rearrange("p (t n) -> p t n", t=LT)[:, 0:8, oc * P:(oc + 1) * P]
                C.ac.inc(e.copy(out=out, in_=PS[:, 2 * 512:3 * 512].bitcast(BF16).rearrange(
                    "p (t n) -> p t n", t=8)))
                out2 = vtok[:].rearrange("p (t n) -> p t n", t=LT)[:, 8:12, oc * P:(oc + 1) * P]
                C.ac.inc(e.copy(out=out2, in_=PS[:, 3 * 512:3 * 512 + 512].bitcast(BF16)[:, 0:4 * P]
                                .rearrange("p (t n) -> p t n", t=4)))
            bank_guard[2] = ("ac", C.ac.n)
            bank_guard[3] = ("ac", C.ac.n)
        elif tag == "q":
            gn = C.gp.n
            @block.sync
            def _(e, gn=gn, oc=oc):
                e.wait_ge(C.gp.h, gn)
                C.dma(e, "ds", out=qT_d[oc], in_=qasm[oc % 2])
            qasm_spill[oc % 2] = C.ds.n

        if ji + 2 < len(jobs):
            tag2, wname2, oc2, _, _ = jobs[ji + 2]
            pn_free = C.pe.n
            @block.gpsimd
            def _(e, pn_free=pn_free, wname2=wname2, oc2=oc2, ji=ji):
                e.wait_ge(C.pe.h, pn_free)
                stream_ld.append(C.dma(e, "dg", out=wst[ji % 2], in_=ins[wname2][oc2]))

    if L0_STAGE < 1.5 and "dbgk" in C.ins:
        ac_all0 = C.ac.n
        @block.gpsimd
        def _(e):
            e.wait_ge(C.ac.h, ac_all0)
            C.dma(e, "dg", out=C.ins["dbgq"].rearrange("p (c n) -> p c n", c=NCH),
                  in_=big0[:].rearrange("p (c n) -> p c n", c=NCH)[:, :, 0:TOK])
            C.dma(e, "dg", out=C.ins["dbgk"], in_=big0[:, 0:4 * 1536])
    if L0_STAGE < 2 and "dbgk" in C.ins:
        gn_all = C.gp.n; ac_all = C.ac.n
        @block.gpsimd
        def _(e):
            e.wait_ge(C.gp.h, gn_all)
            e.wait_ge(C.ac.h, ac_all)
            C.dma(e, "dg", out=C.ins["dbgk"], in_=kT[:])
            C.dma(e, "dg", out=C.ins["dbgv"], in_=vtok[:])
            e.wait_ge(C.ds.h, C.ds.n)
            C.dma(e, "dg", out=C.ins["dbgq"].rearrange("p (c n) -> p c n", c=NCH), in_=C.scr["l0_qT"].rearrange("c p n -> p c n"))
    C.barrier()
    C.pop_scope()
    if L0_STAGE < 2:
        return

    C.push_scope()
    aoT = C.sbuf("l0aoT", [P, NCH * TOK], BF16)
    qt_sb = dbuf(C, "l0qt", NCH * P, BF16)
    masks = dbuf(C, "l0masks", KW, F32)
    ssc = dbuf(C, "l0ssc", KW, F32)
    sat = dbuf(C, "l0sat", KW, BF16)
    satT = dbuf(C, "l0satT", KW, BF16)
    oev = dbuf(C, "l0oev", 1024, BF16)
    wst2 = dbuf(C, "l0wst2", NCH * P, BF16)
    nmax = dbuf(C, "l0nmax", 1, F32)
    ssum = dbuf(C, "l0ssum", 1, F32)
    srec = dbuf(C, "l0srec", 1, F32)

    qt_ld = []
    mk_ld = []
    @block.scalar
    def _(e):
        for t in range(2):
            qt_ld.append(C.dma(e, "da", 
                out=qt_sb[t % 2].rearrange("p (c n) -> p c n", c=NCH),
                in_=C.scr["l0_qT"].rearrange("c p n -> p c n")[:, :, t * P:(t + 1) * P]))
    @block.sync
    def _(e):
        for t in range(2):
            mk_ld.append(C.dma(e, "ds", out=masks[t % 2],
                                              in_=ins["mask0"][t]))

    bank_guard2 = {b: ("ac", 0) for b in range(8)}
    ac_exp_hist = []
    gp_nrm_hist = []
    pe_trans_hist = []
    def guard2(e, b):
        kind, n = bank_guard2[b]
        if n:
            e.wait_ge({"ac": C.ac, "dv": C.dv, "gp": C.gp}[kind].h, n)

    p0_pend = {}

    def p0_A(t, h):
        it = t * 16 + h
        sb = (it % 2) * 2
        @block.tensor
        def _(e, t=t, h=h, sb=sb, qn=qt_ld[t], mn=mk_ld[t]):
            e.wait_ge(C.da.h, qn)
            guard2(e, sb)
            guard2(e, sb + 1)
            lhsT = qt_sb[t % 2][:, h * P:(h + 1) * P]
            kvh = h // 4
            base = kvh * 1536 + t * P
            e.matmul(PS[:, sb * 512:(sb + 1) * 512], lhsT, kT[:, base:base + 512],
                     start=True, stop=True)
            C.pe.inc(e.matmul(PS[:, (sb + 1) * 512:(sb + 1) * 512 + P], lhsT,
                              kT[:, base + 512:base + 640], start=True, stop=True))
        pn = C.pe.n

        acg = ac_exp_hist[it - 2] if it >= 2 else 0
        gng = gp_nrm_hist[it - 2] if it >= 2 else 0
        @block.vector
        def _(e, t=t, it=it, sb=sb, pn=pn, mn=mk_ld[t], acg=acg, gng=gng):
            e.wait_ge(C.pe.h, pn)
            e.wait_ge(C.ds.h, mn)
            if acg:
                e.wait_ge(C.ac.h, acg)
            if gng:
                e.wait_ge(C.dv.h, gng)
            C.dv.inc(e.scalar_tensor_tensor(out=ssc[it % 2], in0=PS[:, sb * 512: sb * 512 + KW],
                                   scalar=INV_SQRT_HD, in1=masks[t % 2],
                                   op0=ALU.mult, op1=ALU.add))
            e.wait_ge(C.dv.h, C.dv.n)
            C.dv.inc(e.reduce_max(nmax[it % 2], ssc[it % 2], axis=AX.X, negate=True))
        dvn = C.dv.n
        bank_guard2[sb] = ("dv", dvn)
        bank_guard2[sb + 1] = ("dv", dvn)

        peg = pe_trans_hist[it - 2] if it >= 2 else 0
        @block.scalar
        def _(e, it=it, dvn=dvn, peg=peg):
            e.wait_ge(C.dv.h, dvn)
            if peg:
                e.wait_ge(C.pe.h, peg)
            C.ac.inc(e.activation(out=sat[it % 2], in_=ssc[it % 2], func=AF.Exp,
                                  bias=nmax[it % 2], scale=1.0, accum_out=ssum[it % 2]))
        acn = C.ac.n
        ac_exp_hist.append(acn)

        @block.vector
        def _(e, acn=acn, it=it):
            e.wait_ge(C.ac.h, acn)
            C.dv.inc(e.reciprocal(srec[it % 2], ssum[it % 2]))

        @block.vector
        def _(e, it=it):
            C.dv.inc(e.tensor_scalar(out=sat[it % 2], in0=sat[it % 2],
                                     scalar1=srec[it % 2], scalar2=None, op0=ALU.mult))
            e.wait_ge(C.dv.h, C.dv.n)
        gn = C.dv.n
        gp_nrm_hist.append(gn)
        p0_pend[it] = (pn, dvn, gn)

    def p0_B(t, h):
        it = t * 16 + h
        atb = 4 + it % 2
        avb = 6 + (it // 4) % 2
        pn, dvn, gn = p0_pend[it]

        @block.tensor
        def _(e, it=it, atb=atb, gn=gn):
            e.wait_ge(C.dv.h, gn)
            guard2(e, atb)
            for kc in range(5):
                view = PS[:, atb * 512:(atb + 1) * 512].bitcast(BF16)[:, kc * P:(kc + 1) * P]
                ins_ = e.transpose(view, sat[it % 2][:, kc * P:(kc + 1) * P], identb[:])
            C.pe.inc(ins_)
        pn2 = C.pe.n
        pe_trans_hist.append(pn2)

        @block.vector
        def _(e, it=it, atb=atb, pn2=pn2):
            e.wait_ge(C.pe.h, pn2)
            C.dv.inc(e.tensor_copy(satT[it % 2],
                                   PS[:, atb * 512:(atb + 1) * 512].bitcast(BF16)[:, 0:KW]))
        bank_guard2[atb] = ("dv", C.dv.n)
        an = C.dv.n

        @block.tensor
        def _(e, t=t, h=h, it=it, avb=avb, an=an):
            e.wait_ge(C.dv.h, an)
            if h % 4 == 0:
                guard2(e, avb)
            kvh = h // 4
            for kc in range(5):
                lhsT = vtok[:, (t + kc) * 512 + kvh * P: (t + kc) * 512 + (kvh + 1) * P]
                ins_ = e.matmul(PS[:, avb * 512 + (h % 4) * P: avb * 512 + (h % 4 + 1) * P],
                                lhsT, satT[it % 2][:, kc * P:(kc + 1) * P],
                                start=(kc == 0), stop=(kc == 4))
            C.pe.inc(ins_)
        pn3 = C.pe.n

        if h % 4 == 3:
            @block.vector
            def _(e, t=t, h=h, avb=avb, pn3=pn3):
                e.wait_ge(C.pe.h, pn3)
                hb = h - 3
                out = aoT[:].rearrange("p (c n) -> p c n", c=NCH)[:, hb:hb + 4, t * P:(t + 1) * P]
                C.dv.inc(e.tensor_copy(out, PS[:, avb * 512:(avb + 1) * 512].rearrange(
                    "p (c n) -> p c n", c=4)))
            bank_guard2[avb] = ("dv", C.dv.n)
        if h == 15 and t + 2 <= NT - 1:
            @block.scalar
            def _(e, t=t, pn=pn, dvn=dvn):
                e.wait_ge(C.pe.h, pn)
                qt_ld.append(C.dma(e, "da",
                    out=qt_sb[t % 2].rearrange("p (c n) -> p c n", c=NCH),
                    in_=C.scr["l0_qT"].rearrange("c p n -> p c n")[:, :, (t + 2) * P:(t + 3) * P]))
            @block.sync
            def _(e, t=t, dvn=dvn):
                e.wait_ge(C.dv.h, dvn)
                mk_ld.append(C.dma(e, "ds", out=masks[t % 2], in_=ins["mask0"][t + 2]))

    its = [(t, h) for t in range(NT) for h in range(16)]
    L0_PIPE = 1
    for i in range(len(its) + L0_PIPE):
        if i < len(its):
            p0_A(*its[i])
        if i >= L0_PIPE:
            p0_B(*its[i - L0_PIPE])
    ac_attn = C.ac.n
    dv_attn = C.dv.n
    wo_ld = []
    @block.gpsimd
    def _(e):
        e.wait_ge(C.dv.h, dv_attn)
        for i in range(2):
            wo_ld.append(C.dma(e, "dg", out=wst2[i % 2], in_=ins["wo_s"][i]))

    bg = {b: ("ac", 0) for b in range(8)}
    def guard3(e, b):
        kind, n = bg[b]
        if n:
            e.wait_ge({"ac": C.ac, "dv": C.dv}[kind].h, n)

    for oc in range(NCH):
        need = wo_ld[oc]
        pb = (oc % 2) * 2
        tb = 4 + (oc % 2)
        @block.tensor
        def _(e, oc=oc, need=need, pb=pb):
            e.wait_ge(C.dg.h, need)
            if oc == 0:
                e.wait_ge(C.dv.h, dv_attn)
            guard3(e, pb)
            guard3(e, pb + 1)
            for dc in range(NCH):
                lhsT = wst2[oc % 2][:, dc * P:(dc + 1) * P]
                for g in range(2):
                    rhs = aoT[:, dc * TOK + g * 512: dc * TOK + (g + 1) * 512]
                    ins_ = e.matmul(PS[:, (pb + g) * 512:(pb + g + 1) * 512], lhsT, rhs,
                                    start=(dc == 0), stop=(dc == NCH - 1))
            C.pe.inc(ins_)
        pn = C.pe.n
        if oc + 2 < NCH:
            @block.gpsimd
            def _(e, oc=oc, pn=pn):
                e.wait_ge(C.pe.h, pn)
                wo_ld.append(C.dma(e, "dg", out=wst2[oc % 2], in_=ins["wo_s"][oc + 2]))

        @block.scalar
        def _(e, oc=oc, pb=pb, pn=pn):
            e.wait_ge(C.pe.h, pn)
            C.ac.inc(e.copy(out=oev[oc % 2], in_=PS[:, pb * 512: pb * 512 + 1024]))
        bg[pb] = ("ac", C.ac.n)
        bg[pb + 1] = ("ac", C.ac.n)
        an = C.ac.n

        if L0_STAGE < 2.8:
            bg[pb] = ("ac", C.ac.n)
            continue
        @block.tensor
        def _(e, oc=oc, tb=tb, an=an):
            e.wait_ge(C.ac.h, an)
            guard3(e, tb)
            for tt in range(NT):
                view = PS[:, tb * 512:(tb + 1) * 512].bitcast(BF16)[:, tt * P:(tt + 1) * P]
                ins_ = e.transpose(view, oev[oc % 2][:, tt * P:(tt + 1) * P], identb[:])
            C.pe.inc(ins_)
        pn2 = C.pe.n

        @block.vector
        def _(e, oc=oc, tb=tb, pn2=pn2):
            e.wait_ge(C.pe.h, pn2)
            out = X[:].rearrange("p (t n) -> p t n", t=NT)[:, :, oc * P:(oc + 1) * P]
            src = PS[:, tb * 512:(tb + 1) * 512].bitcast(BF16).rearrange("p (t n) -> p t n", t=NT)
            if L0_STAGE < 2.9:
                C.dv.inc(e.tensor_copy(scr["xtmp"][:, 0:TOK], src.rearrange("p t n -> p (t n)")))
            else:
                C.dv.inc(e.tensor_tensor(out=out, in0=out, in1=src, op=ALU.add))
        bg[tb] = ("dv", C.dv.n)

    C.barrier()
    C.pop_scope()
    C.pop_scope()



def emit_moe(C, X, PS):
    nc, block = C.nc, C.block
    ins = C.ins
    C.push_scope()

    XNT1 = C.sbuf("m_xnT", [P, NCH * TOK], BF16)
    xnf = C.sbuf("m_xnf", [P, D], F32)
    identb = C.sbuf("m_ident", [P, P], BF16)
    identf = C.sbuf("m_identf", [P, P], F32)
    gwt = C.sbuf("m_gwt", [P, NCH * 32], F32)
    gwh = C.sbuf("m_gwh", [P, NCH * 32], BF16)
    gwl = C.sbuf("m_gwl", [P, NCH * 32], BF16)
    lg = C.sbuf("m_lg", [P, 32], F32)
    cur = C.sbuf("m_cur", [P, 32], F32)
    pex = C.sbuf("m_pex", [P, 32], F32)
    tbig = C.sbuf("m_tbig", [P, 32], F32)
    mi = C.sbuf("m_mi", [P, 1], F32)
    m1n = C.sbuf("m_m1n", [P, 1], F32)
    ssum = C.sbuf("m_ssum", [P, 1], F32)
    srec = C.sbuf("m_srec", [P, 1], F32)
    mw_all = C.sbuf("m_mw", [P, NT * 32], F32)
    mwT = C.sbuf("m_mwT", [32, NT * P], BF16)
    wg_sb = C.sbuf("m_wg", [P, NCH * 768], BF16)
    wu_sb = C.sbuf("m_wu", [P, NCH * 768], BF16)
    wd_sb = C.sbuf("m_wd", [P, 6 * D], BF16)
    sg = C.sbuf("m_sg", [P, 6 * 512], BF16)
    uev = C.sbuf("m_uev", [P, 6 * 512], BF16)
    hT = C.sbuf("m_hT", [P, 6 * 512], BF16)
    wb = C.sbuf("m_wb", [P, TOK], BF16)
    xnb = [sg[:, 0:D], sg[:, 0:D]]
    xnlo = hT[:, 0:D]
    xnloT = wd_sb[:, D:2 * D]
    scr2 = mk_ln_scr(C, "m_ln", dump_ap=wd_sb[:, 0:D], need_xtmp=False)
    mw_d = C.scr["moe_mw"]

    @block.gpsimd
    def _(e):
        C.dma(e, "dg", out=identb[:], in_=ins["ident"])
        C.dma(e, "dg", out=gwh[:], in_=ins["gwt"])
    @block.sync
    def _(e):
        C.dma(e, "ds", out=gwt[:], in_=ins["gwt"])
        C.dma(e, "ds", out=identf[:], in_=ins["ident"])
    dsn0 = C.ds.n
    dgn0 = C.dg.n

    @block.vector
    def _(e):
        e.wait_ge(C.ds.h, dsn0)
        e.wait_ge(C.dg.h, dgn0)
        C.dv.inc(e.tensor_tensor(out=gwl[:], in0=gwt[:], in1=gwh[:], op=ALU.subtract))
    dv_gw = C.dv.n

    rdy_hist = []
    pe_tr = []
    ac_ev = []
    pe_lo = [0] * NT
    ac_lgev = 0
    ac_mwT = 0
    dv_mw_prev = 0
    for t in range(NT):
        pre = []
        if t >= 2:
            k2, n2 = rdy_hist[t - 2]
            pre.append((C.dv, n2))
            pv = pe_tr[t - 2]
            @block.vector
            def _(e, pv=pv):
                e.wait_ge(C.pe.h, pv)
        kind, n = emit_ln_tile(C, X[:, t * D:(t + 1) * D], xnf[:], scr2[t % 2],
                               pre_wait=pre)
        rdy_hist.append((kind, n))
        plo = pe_lo[t - 1] if t >= 1 else 0
        @block.vector
        def _(e, t=t, plo=plo):
            if plo:
                e.wait_ge(C.pe.h, plo)
            C.dv.inc(e.tensor_copy(xnb[t % 2], xnf[:]))
            e.wait_ge(C.dv.h, C.dv.n)
            C.dv.inc(e.tensor_tensor(out=xnlo, in0=xnf[:], in1=xnb[t % 2], op=ALU.subtract))
        dv_hilo = C.dv.n

        prev_ev = ac_ev[t - 1] if t >= 1 else 0
        @block.tensor
        def _(e, t=t, dv_hilo=dv_hilo, prev_ev=prev_ev):
            e.wait_ge(C.dv.h, dv_hilo)
            e.wait_ge(C.ac.h, prev_ev)
            for dc in range(NCH):
                bb = dc // 8
                view = PS[:, bb * 512:(bb + 1) * 512].bitcast(BF16)[:, (dc % 8) * P:(dc % 8 + 1) * P]
                ins_ = e.transpose(view, xnb[t % 2][:, dc * P:(dc + 1) * P], identb[:])
            C.pe.inc(ins_)
            for dc in range(NCH):
                bb = 2 + dc // 8
                view = PS[:, bb * 512:(bb + 1) * 512].bitcast(BF16)[:, (dc % 8) * P:(dc % 8 + 1) * P]
                ins_ = e.transpose(view, xnlo[:, dc * P:(dc + 1) * P], identb[:])
            C.pe.inc(ins_)
        pe_tr.append(C.pe.n - 1)
        pe_lo[t] = C.pe.n
        pn = C.pe.n

        @block.scalar
        def _(e, t=t, pn=pn):
            e.wait_ge(C.pe.h, pn)
            for half in range(2):
                out = XNT1[:].rearrange("p (c n) -> p c n", c=NCH)[:, half * 8:(half + 1) * 8,
                                                                   t * P:(t + 1) * P]
                i2 = PS[:, half * 512:(half + 1) * 512].bitcast(BF16).rearrange("p (c n) -> p c n", c=8)
                ins_ = e.copy(out=out, in_=i2)
            for half in range(2):
                out = xnloT.rearrange("p (c n) -> p c n", c=NCH)[:, half * 8:(half + 1) * 8, :]
                i2 = PS[:, (2 + half) * 512:(3 + half) * 512].bitcast(BF16).rearrange("p (c n) -> p c n", c=8)
                ins_ = e.copy(out=out, in_=i2)
            C.ac.inc(ins_)
        ac_ev.append(C.ac.n)
        an_t = C.ac.n

        @block.tensor
        def _(e, t=t, an_t=an_t, dv_gw=dv_gw, ac_lgev=ac_lgev):
            e.wait_ge(C.ac.h, an_t)
            if t == 0:
                e.wait_ge(C.dv.h, dv_gw)
            if ac_lgev:
                e.wait_ge(C.ac.h, ac_lgev)
            combos = [(0, gwh), (0, gwl), (1, gwh)]
            nmm = len(combos) * NCH
            k = 0
            for (which, gw) in combos:
                for dc in range(NCH):
                    if which == 0:
                        lhsT = XNT1[:, dc * TOK + t * P: dc * TOK + (t + 1) * P]
                    else:
                        lhsT = xnloT[:, dc * P:(dc + 1) * P]
                    ins_ = e.matmul(PS[:, 4 * 512:4 * 512 + 32], lhsT, gw[:, dc * 32:(dc + 1) * 32],
                                    start=(k == 0), stop=(k == nmm - 1))
                    k += 1
            C.pe.inc(ins_)
        pn_r = C.pe.n

        @block.scalar
        def _(e, pn_r=pn_r, dvp=dv_mw_prev):
            e.wait_ge(C.pe.h, pn_r)
            if dvp:
                e.wait_ge(C.dv.h, dvp)
            C.ac.inc(e.copy(out=lg[:], in_=PS[:, 4 * 512:4 * 512 + 32]))
        ac_lgev = C.ac.n
        an_lg = C.ac.n

        def step(e, instr):
            C.dv.inc(instr)
            e.wait_ge(C.dv.h, C.dv.n)

        @block.vector
        def _(e, t=t, an_lg=an_lg):
            e.wait_ge(C.ac.h, an_lg)
            step(e, e.tensor_copy(cur[:], lg[:]))
            for i in range(8):
                step(e, e.reduce_max(mi[:], cur[:], axis=AX.X))
                if i == 0:
                    step(e, e.tensor_scalar(out=m1n[:], in0=mi[:], scalar1=-1.0, scalar2=None,
                                            op0=ALU.mult))
                if i < 7:
                    step(e, e.tensor_scalar(out=tbig[:], in0=cur[:], scalar1=mi[:],
                                            scalar2=1e30, op0=ALU.is_ge, op1=ALU.mult))
                    step(e, e.tensor_tensor(out=cur[:], in0=cur[:], in1=tbig[:], op=ALU.subtract))
        dv_t8 = C.dv.n

        @block.scalar
        def _(e, dv_t8=dv_t8):
            e.wait_ge(C.dv.h, dv_t8)
            C.ac.inc(e.activation(out=pex[:], in_=lg[:], func=AF.Exp, bias=m1n[:], scale=1.0))
        an_p = C.ac.n

        @block.vector
        def _(e, t=t, an_p=an_p):
            e.wait_ge(C.ac.h, an_p)
            C.dv.inc(e.scalar_tensor_tensor(out=mw_all[:, t * 32:(t + 1) * 32], in0=lg[:],
                                            scalar=mi[:], in1=pex[:], op0=ALU.is_ge,
                                            op1=ALU.mult, accum_out=ssum[:]))
            e.wait_ge(C.dv.h, C.dv.n)
            C.dv.inc(e.reciprocal(srec[:], ssum[:]))
            e.wait_ge(C.dv.h, C.dv.n)
            C.dv.inc(e.tensor_scalar(out=mw_all[:, t * 32:(t + 1) * 32],
                                     in0=mw_all[:, t * 32:(t + 1) * 32],
                                     scalar1=srec[:], scalar2=None, op0=ALU.mult))
            e.wait_ge(C.dv.h, C.dv.n)
        dv_mw = C.dv.n

        dv_mw_prev = dv_mw
        @block.tensor
        def _(e, t=t, dv_mw=dv_mw, ac_mwT=ac_mwT):
            e.wait_ge(C.dv.h, dv_mw)
            if ac_mwT:
                e.wait_ge(C.ac.h, ac_mwT)
            C.pe.inc(e.transpose(PS[0:32, 5 * 512:5 * 512 + P], mw_all[:, t * 32:(t + 1) * 32],
                                 identf[:]))
        pn_mw = C.pe.n

        @block.scalar
        def _(e, t=t, pn_mw=pn_mw):
            e.wait_ge(C.pe.h, pn_mw)
            C.ac.inc(e.copy(out=mwT[:, t * P:(t + 1) * P], in_=PS[0:32, 5 * 512:5 * 512 + P]))
        ac_mwT = C.ac.n

    @block.sync
    def _(e, ac_mwT=ac_mwT):
        e.wait_ge(C.ac.h, ac_mwT)
        C.dma(e, "ds", out=mw_d, in_=mwT[:])
    ds_mw = C.ds.n

    if MOE_STAGE < 2:
        if "dbg" in C.ins:
            @block.gpsimd
            def _(e, am=ac_mwT, ds_mw=ds_mw):
                e.wait_ge(C.ac.h, am)
                e.wait_ge(C.ds.h, ds_mw)
                C.dma(e, "dg", out=C.ins["dbg"], in_=XNT1[:])
                C.dma(e, "dg", out=C.ins["dbgk"].rearrange("p n -> p n")[0:32, 0:NT * P], in_=mwT[:])
        C.barrier()
        C.pop_scope()
        return

    wl = {}
    @block.gpsimd
    def _(e, ds_mw=ds_mw):
        e.wait_ge(C.ds.h, ds_mw)
        wl[("wg", 0)] = C.dma(e, "dg", out=wg_sb[:], in_=ins["wg_t"][0])
        wl[("wu", 0)] = C.dma(e, "dg", out=wu_sb[:], in_=ins["wu_t"][0])
        wl[("wd", 0)] = C.dma(e, "dg", out=wd_sb[:], in_=ins["wd_t"][0])
    wb_ld = {}
    @block.scalar
    def _(e):
        e.wait_ge(C.ds.h, ds_mw)
        wb_ld[0] = C.dma(e, "da", out=wb[:], in_=mw_d[0:1, :].partition_broadcast(P).squeeze(1))

    bank_dv = {b: 0 for b in range(8)}
    bank_ac = {b: 0 for b in range(6)}
    gp_hT = 0
    pe_g = {}
    pe_u = {}
    pe_dn = 0
    n_experts = 32 if MOE_STAGE >= 3 else 1
    for ei in range(n_experts):
        for blk in range(2):
            for mc in range(6):
                acfirst = ac_mwT if (ei == 0 and blk == 0) else 0
                @block.tensor
                def _(e, ei=ei, blk=blk, mc=mc, need=wl[("wg", ei)], g1=bank_ac[mc],
                      gh=gp_hT, acfirst=acfirst):
                    e.wait_ge(C.dg.h, need)
                    if acfirst:
                        e.wait_ge(C.ac.h, acfirst)
                    if g1:
                        e.wait_ge(C.ac.h, g1)
                    for dc in range(NCH):
                        lhsT = wg_sb[:, dc * 768 + mc * P: dc * 768 + (mc + 1) * P]
                        rhs = XNT1[:, dc * TOK + blk * 512: dc * TOK + (blk + 1) * 512]
                        ins_ = e.matmul(PS[:, mc * 512:(mc + 1) * 512], lhsT, rhs,
                                        start=(dc == 0), stop=(dc == NCH - 1))
                    C.pe.inc(ins_)
                pe_g[mc] = C.pe.n
                @block.scalar
                def _(e, mc=mc, pn=pe_g[mc], gh=gp_hT):
                    e.wait_ge(C.pe.h, pn)
                    if gh:
                        e.wait_ge(C.gp.h, gh)
                    C.ac.inc(e.activation(out=sg[:, mc * 512:(mc + 1) * 512],
                                          in_=PS[:, mc * 512:(mc + 1) * 512], func=AF.Silu))
                bank_ac[mc] = C.ac.n
            for mc in range(6):
                @block.tensor
                def _(e, ei=ei, blk=blk, mc=mc, need=wl[("wu", ei)], ga=bank_ac[mc]):
                    e.wait_ge(C.dg.h, need)
                    e.wait_ge(C.ac.h, ga)
                    for dc in range(NCH):
                        lhsT = wu_sb[:, dc * 768 + mc * P: dc * 768 + (mc + 1) * P]
                        rhs = XNT1[:, dc * TOK + blk * 512: dc * TOK + (blk + 1) * 512]
                        ins_ = e.matmul(PS[:, mc * 512:(mc + 1) * 512], lhsT, rhs,
                                        start=(dc == 0), stop=(dc == NCH - 1))
                    C.pe.inc(ins_)
                pe_u[mc] = C.pe.n
                @block.scalar
                def _(e, mc=mc, pn=pe_u[mc], gh=gp_hT):
                    e.wait_ge(C.pe.h, pn)
                    if gh:
                        e.wait_ge(C.gp.h, gh)
                    C.ac.inc(e.copy(out=uev[:, mc * 512:(mc + 1) * 512],
                                    in_=PS[:, mc * 512:(mc + 1) * 512]))
                bank_ac[mc] = C.ac.n
                bank_dv[mc] = 0
            pn_u = C.pe.n
            ac_uev = C.ac.n
            @block.vector
            def _(e, ac_uev=ac_uev):
                e.wait_ge(C.ac.h, ac_uev)
                C.dv.inc(e.tensor_tensor(out=sg[:], in0=sg[:], in1=uev[:], op=ALU.mult))
            dv_h1 = C.dv.n

            wbn = wb_ld[ei]
            @block.gpsimd
            def _(e, dv_h1=dv_h1, blk=blk, wbn=wbn, pdn=pe_dn):
                e.wait_ge(C.dv.h, dv_h1)
                e.wait_ge(C.da.h, wbn)
                if pdn:
                    e.wait_ge(C.pe.h, pdn)
                C.gp.inc(e.tensor_tensor(
                    out=hT[:].rearrange("p (c n) -> p c n", c=6),
                    in0=sg[:].rearrange("p (c n) -> p c n", c=6),
                    in1=wb[:, blk * 512:(blk + 1) * 512].unsqueeze(1).broadcast_to((P, 6, 512)),
                    op=ALU.mult))
            gp_hT = C.gp.n

            if blk == 1 and ei + 1 < n_experts:
                @block.gpsimd
                def _(e, ei=ei, pg=pe_u[5], pu=pe_u[5]):
                    e.wait_ge(C.pe.h, pu)
                    wl[("wg", ei + 1)] = C.dma(e, "dg", out=wg_sb[:], in_=ins["wg_t"][ei + 1])
                    wl[("wu", ei + 1)] = C.dma(e, "dg", out=wu_sb[:], in_=ins["wu_t"][ei + 1])
                @block.scalar
                def _(e, ei=ei, gh=gp_hT):
                    e.wait_ge(C.gp.h, gh)
                    wb_ld[ei + 1] = C.dma(e, "da", out=wb[:],
                                          in_=mw_d[ei + 1:ei + 2, :].partition_broadcast(P).squeeze(1))

            for ti, (s, n_) in enumerate([(s, n_) for s in range(4) for n_ in range(4)]):
                bk = 6 + ti % 2
                @block.tensor
                def _(e, ei=ei, blk=blk, s=s, n_=n_, bk=bk, gh=gp_hT, need=wl[("wd", ei)],
                      gd=bank_dv[bk]):
                    e.wait_ge(C.gp.h, gh)
                    e.wait_ge(C.dg.h, need)
                    if gd:
                        e.wait_ge(C.dv.h, gd)
                    for mc in range(6):
                        lhsT = hT[:, mc * 512 + s * P: mc * 512 + (s + 1) * P]
                        rhs = wd_sb[:, mc * D + n_ * 512: mc * D + (n_ + 1) * 512]
                        ins_ = e.matmul(PS[:, bk * 512:(bk + 1) * 512], lhsT, rhs,
                                        start=(mc == 0), stop=(mc == 5))
                    C.pe.inc(ins_)
                pn_t = C.pe.n

                @block.vector
                def _(e, blk=blk, s=s, n_=n_, bk=bk, pn_t=pn_t):
                    e.wait_ge(C.pe.h, pn_t)
                    xs = X[:, (blk * 4 + s) * D + n_ * 512: (blk * 4 + s) * D + (n_ + 1) * 512]
                    C.dv.inc(e.tensor_tensor(out=xs, in0=xs, in1=PS[:, bk * 512:(bk + 1) * 512],
                                             op=ALU.add))
                bank_dv[bk] = C.dv.n
            pe_dn = C.pe.n
            if blk == 1 and ei + 1 < n_experts:
                @block.gpsimd
                def _(e, ei=ei, pdn=pe_dn):
                    e.wait_ge(C.pe.h, pdn)
                    wl[("wd", ei + 1)] = C.dma(e, "dg", out=wd_sb[:], in_=ins["wd_t"][ei + 1])

    C.barrier()
    C.pop_scope()




def emit_l1attn(C, X, PS):
    nc, block = C.nc, C.block
    ins = C.ins
    C.push_scope()

    XNT2 = C.sbuf("a_xnT", [P, NCH * TOK], BF16)
    identb = C.sbuf("a_ident", [P, P], BF16)
    onesc = C.sbuf("a_onesc", [P, 1], BF16)
    onesr = C.sbuf("a_onesr", [1, P], BF16)
    onesrf = C.sbuf("a_onesrf", [1, P], F32)
    xn2 = dbuf(C, "a_xn2", D, BF16)
    wst = dbuf(C, "a_wst", NCH * P, BF16)
    pstage = dbuf(C, "a_pstage", TOK, BF16)
    slope_sb = C.sbuf("a_slope", [1, 16], F32)
    sbc = C.sbuf("a_sbc", [P, 16], F32)
    arq = C.sbuf("a_arq", [1, 256], F32)
    ark = C.sbuf("a_ark", [P, 2], F32)
    arkp = C.sbuf("a_arkp", [P, 2], F32)
    tcn = C.sbuf("a_tcn", [1, 8], F32)
    kdec = C.sbuf("a_kdec", [P, 32], F32)
    expk = C.sbuf("a_expk", [P, 32], F32)
    causal = C.sbuf("a_causal", [P, 512], BF16)
    bdh = C.sbuf("a_bdh", [P, 16], F32)
    tmp256 = C.sbuf("a_t256", [1, 256], F32)
    qdec_bc = C.sbuf("a_qdecbc", [P, 256], BF16)
    coefs = C.sbuf("a_coefs", [1, P], F32)
    coef_bc = C.sbuf("a_coefbc", [P, P], F32)
    kv = C.sbuf("a_kv", [P, 16 * P], F32)
    S_out = C.sbuf("a_Sout", [P, 16 * P], F32)
    S_in = C.sbuf("a_Sin", [P, 16 * P], F32)
    qblk = C.sbuf("a_qblk", [P, 16 * 256], BF16)
    kblk = C.sbuf("a_kblk", [P, 16 * 256], BF16)
    vblk = C.sbuf("a_vblk", [P, 16 * 256], BF16)
    gblk = C.sbuf("a_gblk", [P, 16 * 256], BF16)
    S_r = qblk[:].bitcast(F32)
    outb = C.sbuf("a_outb", [P, 16 * 256], BF16)
    prodT = outb
    ktok2p = dbuf(C, "a_ktok", 256, BF16)
    vtok2p = dbuf(C, "a_vtok", 256, BF16)
    kdki2p = dbuf(C, "a_kdki", 256, BF16)
    kvbf2p = dbuf(C, "a_kvbf", P, BF16)
    qkd2p = dbuf(C, "a_qkd", 512, BF16)
    ktok, vtok2, kdki, kvbf, qkd = (ktok2p[0], vtok2p[0], kdki2p[0], kvbf2p[0], qkd2p[0])
    sq_sb = C.sbuf("a_sq", [P, 256], BF16)
    sq_sb2 = C.sbuf("a_sq2", [P, 256], BF16)
    sqp = [sq_sb, sq_sb2]
    rr = C.sbuf("a_rr", [1, 256], F32)
    rstd_bc = C.sbuf("a_rstdbc", [P, 256], F32)
    normw = C.sbuf("a_normw", [P, 16], F32)
    wrow = C.sbuf("a_wrow", [P, D], BF16)
    wrow2 = C.sbuf("a_wrow2", [P, D], BF16)
    scr2 = mk_ln_scr(C, "a_ln", need_xtmp=False)

    qd_d, kd_d, vd_d, gd_d = (C.scr["l1_q"], C.scr["l1_k"], C.scr["l1_v"], C.scr["l1_g"])
    wot_bf = C.scr["l1_wotbf"]
    ccin, ccag = C.scr["l1_ccin"], C.scr["l1_ccag"]

    @block.gpsimd
    def _(e):
        C.dma(e, "dg", out=identb[:], in_=ins["ident"])
        C.dma(e, "dg", out=onesc[:], in_=ins["ones_col"])
        C.dma(e, "dg", out=onesr[:], in_=ins["ones_row"])
        C.dma(e, "dg", out=causal[:], in_=ins["causal01"])
    @block.sync
    def _(e):
        C.dma(e, "ds", out=slope_sb[:], in_=ins["slope"])
        C.dma(e, "ds", out=onesrf[:], in_=ins["ones_row"])
        C.dma(e, "ds", out=arq[:], in_=ins["arqneg"])
        C.dma(e, "ds", out=ark[:], in_=ins["arkneg"])
        C.dma(e, "ds", out=arkp[:], in_=ins["arkpos"])
        C.dma(e, "ds", out=tcn[:], in_=ins["tcneg"])
        C.dma(e, "ds", out=normw[:], in_=ins["normw_col"])
    dsn0 = C.ds.n
    dgn0 = C.dg.n

    @block.tensor
    def _(e):
        e.wait_ge(C.ds.h, dsn0)
        e.wait_ge(C.dg.h, dgn0)
        C.pe.inc(e.matmul(PS[:, 0:16], onesrf[:], slope_sb[:], start=True, stop=True))
    pn0 = C.pe.n
    @block.scalar
    def _(e, pn0=pn0):
        e.wait_ge(C.pe.h, pn0)
        C.ac.inc(e.copy(out=sbc[:], in_=PS[:, 0:16]))
    ac_sbc = C.ac.n

    for h in range(16):
        @block.vector
        def _(e, h=h, ac_sbc=ac_sbc):
            if h == 0:
                e.wait_ge(C.ac.h, ac_sbc)
            C.dv.inc(e.tensor_scalar(out=kdec[:, h * 2:(h + 1) * 2], in0=ark[:],
                                     scalar1=sbc[:, h:h + 1], scalar2=None, op0=ALU.mult))
            e.wait_ge(C.dv.h, C.dv.n)
            C.dv.inc(e.tensor_scalar(out=expk[:, h * 2:(h + 1) * 2], in0=arkp[:],
                                     scalar1=sbc[:, h:h + 1], scalar2=None, op0=ALU.mult))
            e.wait_ge(C.dv.h, C.dv.n)
            C.dv.inc(e.tensor_scalar(out=bdh[:, h:h + 1], in0=sbc[:, h:h + 1], scalar1=-256.0,
                                     scalar2=None, op0=ALU.mult))
            e.wait_ge(C.dv.h, C.dv.n)
        dvh = C.dv.n
        @block.scalar
        def _(e, h=h, dvh=dvh):
            e.wait_ge(C.dv.h, dvh)
            e.activation(out=kdec[:, h * 2:(h + 1) * 2], in_=kdec[:, h * 2:(h + 1) * 2], func=AF.Exp)
            e.activation(out=expk[:, h * 2:(h + 1) * 2], in_=expk[:, h * 2:(h + 1) * 2], func=AF.Exp)
            C.ac.inc(e.activation(out=bdh[:, h:h + 1], in_=bdh[:, h:h + 1], func=AF.Exp))
    ac_tab = C.ac.n

    @block.vector
    def _(e, ac_tab=ac_tab):
        e.wait_ge(C.ac.h, ac_tab)
        for h in range(16):
            ins_ = e.tensor_scalar(out=coefs[0:1, h * 8:(h + 1) * 8], in0=tcn[:],
                                   scalar1=slope_sb[0:1, h:h + 1], scalar2=None, op0=ALU.mult)
        C.dv.inc(ins_)
    dv_cf = C.dv.n
    @block.scalar
    def _(e, dv_cf=dv_cf):
        e.wait_ge(C.dv.h, dv_cf)
        C.ac.inc(e.activation(out=coefs[:], in_=coefs[:], func=AF.Exp))
    ac_cf = C.ac.n
    @block.tensor
    def _(e, ac_cf=ac_cf):
        e.wait_ge(C.ac.h, ac_cf)
        C.pe.inc(e.matmul(PS[:, 512:640], onesrf[:], coefs[:], start=True, stop=True))
    pn_cf = C.pe.n
    @block.scalar
    def _(e, pn_cf=pn_cf):
        e.wait_ge(C.pe.h, pn_cf)
        C.ac.inc(e.copy(out=coef_bc[:], in_=PS[:, 512:640]))
    ac_cfb = C.ac.n

    @block.vector
    def _(e):
        e.memset(S_out[:], 0.0)
        C.dv.inc(e.memset(S_in[:], 0.0))
    dv_z = C.dv.n

    wot_ld0 = {}
    wot_sp0 = {}

    def emit_precast(oc):
        @block.gpsimd
        def _(e, oc=oc):
            if oc >= 2:
                e.wait_ge(C.ds.h, wot_sp0[oc - 2])
            wot_ld0[oc] = C.dma(e, "dg", out=wrow[:] if oc % 2 == 0 else wrow2[:],
                                in_=ins["wot_r"][oc])
        @block.sync
        def _(e, oc=oc):
            e.wait_ge(C.dg.h, wot_ld0[oc])
            C.dma(e, "ds", out=wot_bf[oc], in_=wrow[:] if oc % 2 == 0 else wrow2[:])
            wot_sp0[oc] = C.ds.n

    rdy_hist = []
    pe_tr = []
    ac_ev = []
    for t in range(NT):
        pre = []
        if t >= 2:
            k2, n2 = rdy_hist[t - 2]
            pre.append((C.dv, n2))
            pv = pe_tr[t - 2]
            @block.vector
            def _(e, pv=pv):
                e.wait_ge(C.pe.h, pv)
        kind, n = emit_ln_tile(C, X[:, t * D:(t + 1) * D], xn2[t % 2], scr2[t % 2],
                               pre_wait=pre)
        rdy_hist.append((kind, n))
        prev_ev = ac_ev[t - 1] if t >= 1 else 0
        @block.tensor
        def _(e, t=t, n=n, prev_ev=prev_ev):
            e.wait_ge(C.dv.h, n)
            e.wait_ge(C.ac.h, prev_ev)
            for dc in range(NCH):
                bb = 2 + dc // 8
                view = PS[:, bb * 512:(bb + 1) * 512].bitcast(BF16)[:, (dc % 8) * P:(dc % 8 + 1) * P]
                ins_ = e.transpose(view, xn2[t % 2][:, dc * P:(dc + 1) * P], identb[:])
            C.pe.inc(ins_)
        pe_tr.append(C.pe.n)
        pn = C.pe.n
        @block.scalar
        def _(e, t=t, pn=pn):
            e.wait_ge(C.pe.h, pn)
            for half in range(2):
                out = XNT2[:].rearrange("p (c n) -> p c n", c=NCH)[:, half * 8:(half + 1) * 8,
                                                                   t * P:(t + 1) * P]
                i2 = PS[:, (2 + half) * 512:(3 + half) * 512].bitcast(BF16).rearrange(
                    "p (c n) -> p c n", c=8)
                ins_ = e.copy(out=out, in_=i2)
            C.ac.inc(ins_)
        ac_ev.append(C.ac.n)
    ac_xnT = C.ac.n

    wjobs = [("wqkv_s", oc, qd_d if oc < 16 else (kd_d if oc < 32 else vd_d),
              oc % 16, AF.Silu) for oc in range(48)]
    wjobs += [("wgate_s", oc, gd_d, oc, AF.Sigmoid) for oc in range(16)]
    gp_pq = {}
    prev_pnq = [0]
    wld = []
    @block.gpsimd
    def _(e):
        for i in range(2):
            wname, oc, _, _, _ = wjobs[i]
            wld.append(C.dma(e, "dg", out=wst[i % 2], in_=ins[wname][oc]))
    ac_pst = {}
    ds_spill = {}

    def emit_wjob(ji):
        wname, oc, dst, slot, func = wjobs[ji]
        pb = 4 + ji % 2
        guard = ac_pst.get(ji - 2, 0)
        @block.tensor
        def _(e, ji=ji, oc=oc, pb=pb, need=wld[ji], guard=guard):
            e.wait_ge(C.dg.h, need)
            if ji == 0:
                e.wait_ge(C.ac.h, ac_xnT)
            if guard:
                e.wait_ge(C.ac.h, guard)
            for blk in range(4):
                for dc in range(NCH):
                    lhsT = wst[ji % 2][:, dc * P:(dc + 1) * P]
                    rhs = XNT2[:, dc * TOK + blk * 256: dc * TOK + (blk + 1) * 256]
                    ins_ = e.matmul(PS[:, pb * 512 + blk * 256: pb * 512 + blk * 256 + 256]
                                    if blk < 2 else
                                    PS[:, (pb + 2) * 512 + (blk - 2) * 256:
                                       (pb + 2) * 512 + (blk - 2) * 256 + 256],
                                    lhsT, rhs, start=(dc == 0), stop=(dc == NCH - 1))
            C.pe.inc(ins_)
        pn = C.pe.n
        if ji + 2 < len(wjobs):
            @block.gpsimd
            def _(e, ji=ji, pn=pn):
                e.wait_ge(C.pe.h, pn)
                wname2, oc2, _, _, _ = wjobs[ji + 2]
                wld.append(C.dma(e, "dg", out=wst[ji % 2], in_=ins[wname2][oc2]))
        is_q = (wname == "wqkv_s" and oc < 16)
        if is_q:
            @block.vector
            def _(e, oc=oc, pq=prev_pnq[0]):
                if pq:
                    e.wait_ge(C.pe.h, pq)
                C.dv.inc(e.tensor_scalar(out=tmp256[:], in0=arq[:],
                                         scalar1=slope_sb[0:1, oc:oc + 1], scalar2=None,
                                         op0=ALU.mult))
                e.wait_ge(C.dv.h, C.dv.n)
            dvq = C.dv.n
            @block.scalar
            def _(e, dvq=dvq):
                e.wait_ge(C.dv.h, dvq)
                C.ac.inc(e.activation(out=tmp256[:], in_=tmp256[:], func=AF.Exp))
            acq = C.ac.n
            @block.tensor
            def _(e, acq=acq, gq=gp_pq.get(ji - 1, 0)):
                e.wait_ge(C.ac.h, acq)
                if gq:
                    e.wait_ge(C.gp.h, gq)
                C.pe.inc(e.matmul(PS[:, 0:256], onesrf[:], tmp256[:], start=True, stop=True))
            pnq = C.pe.n
            prev_pnq[0] = pnq
            @block.scalar
            def _(e, pnq=pnq):
                e.wait_ge(C.pe.h, pnq)
                C.ac.inc(e.copy(out=qdec_bc[:], in_=PS[:, 0:256]))
            ac_qd = C.ac.n

        @block.scalar
        def _(e, ji=ji, pb=pb, pn=pn, func=func, sp=ds_spill.get(ji - 2, 0)):
            e.wait_ge(C.pe.h, pn)
            if sp:
                e.wait_ge(C.ds.h, sp)
            e.activation(out=pstage[ji % 2][:, 0:512], in_=PS[:, pb * 512:(pb + 1) * 512], func=func)
            C.ac.inc(e.activation(out=pstage[ji % 2][:, 512:1024],
                                  in_=PS[:, (pb + 2) * 512:(pb + 3) * 512], func=func))
        ac_pst[ji] = C.ac.n
        an = C.ac.n
        if is_q:
            @block.gpsimd
            def _(e, ji=ji, an=an, ac_qd=ac_qd):
                e.wait_ge(C.ac.h, an)
                e.wait_ge(C.ac.h, ac_qd)
                C.gp.inc(e.tensor_tensor(
                    out=pstage[ji % 2][:].rearrange("p (b n) -> p b n", b=4),
                    in0=pstage[ji % 2][:].rearrange("p (b n) -> p b n", b=4),
                    in1=qdec_bc[:].unsqueeze(1).broadcast_to((P, 4, 256)), op=ALU.mult))
            gp_pq[ji] = C.gp.n
            an = ("gp", C.gp.n)
        else:
            an = ("ac", an)
        @block.sync
        def _(e, ji=ji, dst=dst, slot=slot, an=an):
            e.wait_ge({"ac": C.ac, "gp": C.gp}[an[0]].h, an[1])
            C.dma(e, "ds", out=dst[slot], in_=pstage[ji % 2])
        ds_spill[ji] = C.ds.n

    for ji in range(48):
        emit_wjob(ji)
        if ji % 3 == 0 and ji // 3 < NCH:
            emit_precast(ji // 3)
    ds_wotbf = C.ds.n

    ld_blk = {}
    @block.scalar
    def _(e):
        e.wait_ge(C.ds.h, ds_spill[47])
        ld_blk[("k", 0)] = C.dma(e, "da", out=kblk[:].rearrange("p (c n) -> p c n", c=16),
                                 in_=kd_d.rearrange("c p n -> p c n")[:, :, 0:256])
        ld_blk[("v", 0)] = C.dma(e, "da", out=vblk[:].rearrange("p (c n) -> p c n", c=16),
                                 in_=vd_d.rearrange("c p n -> p c n")[:, :, 0:256])
    p1_kt_ac = [0, 0]
    p1_kd_dv = [0, 0]
    p1_stt_dv = [0, 0]
    p1_pend = {}
    p1_dv = 0

    def p1_A(blk, h):
        par = h % 2
        ktok, vtok2, kdki = ktok2p[par], vtok2p[par], kdki2p[par]
        TB = par * 512
        @block.tensor
        def _(e, blk=blk, h=h, par=par, need=ld_blk[("v", blk)], dz=dv_z, TB=TB,
              ga=p1_kt_ac[par], cfb=ac_cfb):
            e.wait_ge(C.da.h, need)
            if ga:
                e.wait_ge(C.ac.h, ga)
            if blk == 0 and h == 0:
                e.wait_ge(C.dv.h, dz)
            if blk == 0 and h < 2:
                e.wait_ge(C.ac.h, cfb)
            for sub in range(2):
                view = PS[:, TB:TB + 512].bitcast(BF16)[:, sub * P:(sub + 1) * P]
                ins_ = e.transpose(view, kblk[:, h * 256 + sub * P: h * 256 + (sub + 1) * P],
                                   identb[:])
            for sub in range(2):
                view = PS[:, TB:TB + 512].bitcast(BF16)[:, (2 + sub) * P:(3 + sub) * P]
                ins_ = e.transpose(view, vblk[:, h * 256 + sub * P: h * 256 + (sub + 1) * P],
                                   identb[:])
            C.pe.inc(ins_)
        pn = C.pe.n
        @block.scalar
        def _(e, pn=pn, gd=p1_kd_dv[par], ktok=ktok, vtok2=vtok2, TB=TB):
            e.wait_ge(C.pe.h, pn)
            if gd:
                e.wait_ge(C.dv.h, gd)
            e.copy(out=ktok[:], in_=PS[:, TB:TB + 512].bitcast(BF16)[:, 0:256])
            C.ac.inc(e.copy(out=vtok2[:], in_=PS[:, TB:TB + 512].bitcast(BF16)[:, 256:512]))
        p1_kt_ac[par] = C.ac.n
        an = C.ac.n
        @block.vector
        def _(e, h=h, an=an, ktok=ktok, kdki=kdki):
            e.wait_ge(C.ac.h, an)
            for sub in range(2):
                ins_ = e.tensor_scalar(out=kdki[:, sub * P:(sub + 1) * P],
                                       in0=ktok[:, sub * P:(sub + 1) * P],
                                       scalar1=kdec[:, h * 2 + sub:h * 2 + sub + 1],
                                       scalar2=None, op0=ALU.mult)
            C.dv.inc(ins_)
        p1_kd_dv[par] = C.dv.n
        p1_pend[h] = C.dv.n

    def p1_B(blk, h):
        par = h % 2
        vtok2, kdki = vtok2p[par], kdki2p[par]
        CB = (2 + par) * 512
        @block.tensor
        def _(e, h=h, dvk=p1_pend[h], gd=p1_stt_dv[par], vtok2=vtok2, kdki=kdki, CB=CB):
            e.wait_ge(C.dv.h, dvk)
            if gd:
                e.wait_ge(C.dv.h, gd)
            for sub in range(2):
                ins_ = e.matmul(PS[:, CB:CB + P], kdki[:, sub * P:(sub + 1) * P],
                                vtok2[:, sub * P:(sub + 1) * P], start=(sub == 0), stop=(sub == 1))
            C.pe.inc(ins_)
        pnc = C.pe.n
        @block.vector
        def _(e, h=h, pnc=pnc, CB=CB):
            e.wait_ge(C.pe.h, pnc)
            C.dv.inc(e.scalar_tensor_tensor(out=S_out[:, h * P:(h + 1) * P],
                                            in0=S_out[:, h * P:(h + 1) * P],
                                            scalar=bdh[:, h:h + 1], in1=PS[:, CB:CB + P],
                                            op0=ALU.mult, op1=ALU.add))
        p1_stt_dv[par] = C.dv.n

    for blk in range(4):
        for h in range(17):
            if h < 16:
                p1_A(blk, h)
            if h >= 1:
                p1_B(blk, h - 1)
        p1_dv = C.dv.n
        if blk + 1 < 4:
            @block.scalar
            def _(e, blk=blk, pa=C.pe.n):
                e.wait_ge(C.pe.h, pa)
                ld_blk[("k", blk + 1)] = C.dma(e, "da", out=kblk[:].rearrange("p (c n) -> p c n", c=16),
                                               in_=kd_d.rearrange("c p n -> p c n")[:, :, (blk + 1) * 256:(blk + 2) * 256])
                ld_blk[("v", blk + 1)] = C.dma(e, "da", out=vblk[:].rearrange("p (c n) -> p c n", c=16),
                                               in_=vd_d.rearrange("c p n -> p c n")[:, :, (blk + 1) * 256:(blk + 2) * 256])

    @block.sync
    def _(e, pd=p1_dv):
        e.wait_ge(C.dv.h, pd)
        C.dma(e, "ds", out=ccin.rearrange("(h p) d -> p h d", p=P),
              in_=S_out[:].rearrange("p (h d) -> p h d", h=16))
    ds_cc = C.ds.n
    cc_box = [0]
    sin_box = [0]

    def emit_cc():
        @block.gpsimd
        def _(e, ds_cc=ds_cc):
            e.wait_ge(C.ds.h, ds_cc)
            C.cc.inc(e.collective_compute("AllGather", ALU.bypass,
                                          replica_groups=[list(range(NCORES))],
                                          ins=[ccin.opt()], outs=[ccag.opt()]))
        cc_box[0] = C.cc.n

    def emit_sin_round(r):
        @block.gpsimd
        def _(e, r=r, cc_n=cc_box[0], pd=sin_box[0]):
            if r == 0:
                e.wait_ge(C.cc.h, cc_n)
            if pd:
                e.wait_ge(C.dv.h, pd)
            C.dma(e, "dg", out=S_r[:].rearrange("p (h d) -> p h d", h=16),
                  in_=ccag[r * 2048:(r + 1) * 2048, :].rearrange("(h p) d -> p h d", p=P))
        dg_r = C.dg.n
        @block.vector
        def _(e, r=r, dg_r=dg_r, ac_cfb=ac_cfb):
            e.wait_ge(C.dg.h, dg_r)
            if r == 0:
                e.wait_ge(C.ac.h, ac_cfb)
            for h in range(16):
                C.dv.inc(e.scalar_tensor_tensor(out=S_in[:, h * P:(h + 1) * P],
                                                in0=S_r[:, h * P:(h + 1) * P],
                                                scalar=coef_bc[:, h * 8 + r:h * 8 + r + 1],
                                                in1=S_in[:, h * P:(h + 1) * P],
                                                op0=ALU.mult, op1=ALU.add))
        sin_box[0] = C.dv.n

    for ji in range(48, 64):
        emit_wjob(ji)
        if ji == 49:
            emit_cc()
        if ji >= 56:
            emit_sin_round(ji - 56)
    dv_sin = sin_box[0]

    @block.vector
    def _(e, dv_sin=dv_sin):
        C.dv.inc(e.tensor_copy(kv[:], S_in[:]))
        e.wait_ge(C.dv.h, C.dv.n)
    dv_kv0 = C.dv.n

    ld2 = {}
    @block.scalar
    def _(e, dv_kv0=dv_kv0, sp63=ds_spill[63]):
        e.wait_ge(C.dv.h, dv_kv0)
        e.wait_ge(C.ds.h, sp63)
        for nm, dd in (("q", qd_d), ("k", kd_d), ("v", vd_d), ("g", gd_d)):
            ld2[(nm, 0)] = C.dma(e, "da", out={"q": qblk, "k": kblk, "v": vblk, "g": gblk}[nm][:]
                                 .rearrange("p (c n) -> p c n", c=16),
                                 in_=dd.rearrange("c p n -> p c n")[:, :, 0:256])
    wo_ld = {}
    @block.scalar
    def _(e, ds_wotbf=ds_wotbf):
        e.wait_ge(C.ds.h, ds_wotbf)
        wo_ld[0] = (C.da, C.dma(e, "da", out=wrow[:], in_=wot_bf[0]))
    @block.gpsimd
    def _(e, ds_wotbf=ds_wotbf):
        e.wait_ge(C.ds.h, ds_wotbf)
        wo_ld[1] = (C.dg, C.dma(e, "dg", out=wrow2[:], in_=wot_bf[1]))

    g_out_ac = [0, 0]
    g_qkd_dv = [0, 0]
    g_kt_ac = [0, 0]
    kv_stt_n = [dv_kv0] * 16
    stt_par = [0, 0]
    pe_h_hist = [0, 0]
    dv_add_prev = 0
    an_rb = 0
    p2_pend = {}

    def p2_A(blk, h, an_rb_cur, dv_add):
        par = h % 2
        ktok, vtok2, kdki, kvbf, qkd = (ktok2p[par], vtok2p[par], kdki2p[par],
                                        kvbf2p[par], qkd2p[par])
        QB = (1 if par == 0 else 6) * 512
        TB = (4 if par == 0 else 3) * 512

        @block.scalar
        def _(e, h=h, kvn=kv_stt_n[h], ph=pe_h_hist[par], kvbf=kvbf):
            if kvn:
                e.wait_ge(C.dv.h, kvn)
            if ph:
                e.wait_ge(C.pe.h, ph)
            C.ac.inc(e.copy(out=kvbf[:], in_=kv[:, h * P:(h + 1) * P]))
        ac_kvbf = C.ac.n

        @block.tensor
        def _(e, blk=blk, h=h, gq=g_qkd_dv[par], needk=ld2[("k", blk)],
              needq=ld2[("q", blk)], gadd=dv_add, QB=QB):
            e.wait_ge(C.da.h, needk)
            e.wait_ge(C.da.h, needq)
            if h < 2 and gadd:
                e.wait_ge(C.dv.h, gadd)
            if gq:
                e.wait_ge(C.dv.h, gq)
            for sub in range(2):
                lhsT = kblk[:, h * 256 + sub * P: h * 256 + (sub + 1) * P]
                ins_ = e.matmul(PS[:, QB + sub * 256: QB + (sub + 1) * 256], lhsT,
                                qblk[:, h * 256:(h + 1) * 256], start=True, stop=True)
            C.pe.inc(ins_)
        pn_qk = C.pe.n

        @block.vector
        def _(e, h=h, pn_qk=pn_qk, qkd=qkd, QB=QB):
            e.wait_ge(C.pe.h, pn_qk)
            for sub in range(2):
                ins_ = e.scalar_tensor_tensor(
                    out=qkd[:, sub * 256:(sub + 1) * 256],
                    in0=PS[:, QB + sub * 256: QB + (sub + 1) * 256],
                    scalar=expk[:, h * 2 + sub:h * 2 + sub + 1],
                    in1=causal[:, sub * 256:(sub + 1) * 256],
                    op0=ALU.mult, op1=ALU.mult)
            C.dv.inc(ins_)
        g_qkd_dv[par] = C.dv.n
        dv_qkd = C.dv.n

        rbg = an_rb_cur if (par == 1 and h == 1) else 0
        @block.tensor
        def _(e, blk=blk, h=h, needv=ld2[("v", blk)], ga=g_kt_ac[par], TB=TB, rbg=rbg,
              gadd=dv_add):
            e.wait_ge(C.da.h, needv)
            if ga:
                e.wait_ge(C.ac.h, ga)
            if h < 2 and gadd:
                e.wait_ge(C.dv.h, gadd)
            if rbg:
                e.wait_ge(C.ac.h, rbg)
            for sub in range(2):
                view = PS[:, TB:TB + 512].bitcast(BF16)[:, sub * P:(sub + 1) * P]
                e.transpose(view, kblk[:, h * 256 + sub * P: h * 256 + (sub + 1) * P], identb[:])
            for sub in range(2):
                view = PS[:, TB:TB + 512].bitcast(BF16)[:, (2 + sub) * P:(3 + sub) * P]
                ins_ = e.transpose(view, vblk[:, h * 256 + sub * P: h * 256 + (sub + 1) * P],
                                   identb[:])
            C.pe.inc(ins_)
        pn_t = C.pe.n
        @block.scalar
        def _(e, pn_t=pn_t, ktok=ktok, vtok2=vtok2, TB=TB, gd=p2_kd_dv[par]):
            e.wait_ge(C.pe.h, pn_t)
            if gd:
                e.wait_ge(C.dv.h, gd)
            e.copy(out=ktok[:], in_=PS[:, TB:TB + 512].bitcast(BF16)[:, 0:256])
            C.ac.inc(e.copy(out=vtok2[:], in_=PS[:, TB:TB + 512].bitcast(BF16)[:, 256:512]))
        g_kt_ac[par] = C.ac.n
        an_kt = C.ac.n

        @block.vector
        def _(e, h=h, an_kt=an_kt, ktok=ktok, kdki=kdki):
            e.wait_ge(C.ac.h, an_kt)
            for sub in range(2):
                ins_ = e.tensor_scalar(out=kdki[:, sub * P:(sub + 1) * P],
                                       in0=ktok[:, sub * P:(sub + 1) * P],
                                       scalar1=kdec[:, h * 2 + sub:h * 2 + sub + 1],
                                       scalar2=None, op0=ALU.mult)
            C.dv.inc(ins_)
        p2_kd_dv[par] = C.dv.n
        p2_pend[h] = (ac_kvbf, dv_qkd, C.dv.n)

    def p2_B(blk, h):
        par = h % 2
        vtok2, kdki, kvbf, qkd = vtok2p[par], kdki2p[par], kvbf2p[par], qkd2p[par]
        OB = (0 if par == 0 else 5) * 512
        CB = (2 if par == 0 else 7) * 512
        ac_kvbf, dv_qkd, dv_kdki = p2_pend[h]

        @block.tensor
        def _(e, h=h, dv_qkd=dv_qkd, dv_kdki=dv_kdki, gkv=stt_par[par], ac_kvbf=ac_kvbf,
              go=g_out_ac[par], kvbf=kvbf, vtok2=vtok2, kdki=kdki, qkd=qkd, OB=OB, CB=CB):
            e.wait_ge(C.dv.h, dv_qkd)
            e.wait_ge(C.dv.h, dv_kdki)
            e.wait_ge(C.ac.h, ac_kvbf)
            if go:
                e.wait_ge(C.ac.h, go)
            e.matmul(PS[:, OB:OB + 256], kvbf[:], qblk[:, h * 256:(h + 1) * 256],
                     start=True, stop=False)
            for sub in range(2):
                ins_ = e.matmul(PS[:, OB:OB + 256], vtok2[:, sub * P:(sub + 1) * P],
                                qkd[:, sub * 256:(sub + 1) * 256],
                                start=False, stop=(sub == 1))
            if gkv:
                e.wait_ge(C.dv.h, gkv)
            for sub in range(2):
                ins2 = e.matmul(PS[:, CB:CB + P], kdki[:, sub * P:(sub + 1) * P],
                                vtok2[:, sub * P:(sub + 1) * P], start=(sub == 0), stop=(sub == 1))
            C.pe.inc(ins_)
            C.pe.inc(ins2)
        pn_av = C.pe.n
        pe_h_hist[par] = C.pe.n

        @block.scalar
        def _(e, blk=blk, h=h, pn_av=pn_av, OB=OB):
            e.wait_ge(C.pe.h, pn_av)
            C.ac.inc(e.copy(out=outb[:, h * 256:(h + 1) * 256], in_=PS[:, OB:OB + 256]))
        g_out_ac[par] = C.ac.n

        @block.vector
        def _(e, h=h, pn_av=pn_av, CB=CB):
            e.wait_ge(C.pe.h, pn_av)
            C.dv.inc(e.scalar_tensor_tensor(out=kv[:, h * P:(h + 1) * P],
                                            in0=kv[:, h * P:(h + 1) * P],
                                            scalar=bdh[:, h:h + 1],
                                            in1=PS[:, CB:CB + P],
                                            op0=ALU.mult, op1=ALU.add))
        kv_stt_n[h] = C.dv.n
        stt_par[par] = C.dv.n

    p2_kd_dv = [0, 0]
    for blk in range(4):
        for h in range(17):
            if h < 16:
                p2_A(blk, h, an_rb, dv_add_prev)
            if h >= 1:
                p2_B(blk, h - 1)

        if blk + 1 < 4:
            @block.scalar
            def _(e, blk=blk, ph=max(pe_h_hist)):
                e.wait_ge(C.pe.h, ph)
                for nm, dd in (("q", qd_d), ("k", kd_d), ("v", vd_d)):
                    ld2[(nm, blk + 1)] = C.dma(e, "da",
                        out={"q": qblk, "k": kblk, "v": vblk}[nm][:]
                        .rearrange("p (c n) -> p c n", c=16),
                        in_=dd.rearrange("c p n -> p c n")[:, :, (blk + 1) * 256:(blk + 2) * 256])

        if L1_STAGE < 2:
            break
        pe_sq = [0, 0]
        for h in range(16):
            @block.scalar
            def _(e, h=h, g=pe_sq[h % 2]):
                if g:
                    e.wait_ge(C.pe.h, g)
                C.ac.inc(e.activation(out=sqp[h % 2][:], in_=outb[:, h * 256:(h + 1) * 256],
                                      func=AF.Square))
            an_sq = C.ac.n
            @block.tensor
            def _(e, h=h, an_sq=an_sq):
                e.wait_ge(C.ac.h, an_sq)
                C.pe.inc(e.matmul(PS[0:1, 3 * 512:3 * 512 + 256], onesc[:], sqp[h % 2][:],
                                  start=(h == 0), stop=(h == 15)))
            pe_sq[h % 2] = C.pe.n
        pn_sq = C.pe.n
        @block.vector
        def _(e, pn_sq=pn_sq):
            e.wait_ge(C.pe.h, pn_sq)
            C.dv.inc(e.tensor_scalar(out=rr[:], in0=PS[0:1, 3 * 512:3 * 512 + 256],
                                     scalar1=1.0 / D, scalar2=1e-6, op0=ALU.mult, op1=ALU.add))
            e.wait_ge(C.dv.h, C.dv.n)
        dv_rr = C.dv.n
        @block.scalar
        def _(e, dv_rr=dv_rr):
            e.wait_ge(C.dv.h, dv_rr)
            C.ac.inc(e.activation(out=rr[:], in_=rr[:], func=AF.Sqrt))
        an_rr = C.ac.n
        @block.vector
        def _(e, an_rr=an_rr):
            e.wait_ge(C.ac.h, an_rr)
            C.dv.inc(e.reciprocal(rr[:], rr[:]))
            e.wait_ge(C.dv.h, C.dv.n)
        dv_rc = C.dv.n
        @block.tensor
        def _(e, dv_rc=dv_rc):
            e.wait_ge(C.dv.h, dv_rc)
            C.pe.inc(e.matmul(PS[:, 3 * 512:3 * 512 + 256], onesrf[:], rr[:], start=True, stop=True))
        pn_rb = C.pe.n
        @block.scalar
        def _(e, pn_rb=pn_rb):
            e.wait_ge(C.pe.h, pn_rb)
            C.ac.inc(e.copy(out=rstd_bc[:], in_=PS[:, 3 * 512:3 * 512 + 256]))
        an_rb = C.ac.n

        dv_pr = 0
        gp_pr = 0
        gp_n2 = [0, 0]
        for h in range(16):
            @block.vector
            def _(e, h=h, an_rb=an_rb, needg=ld2[("g", blk)], g=gp_n2[h % 2]):
                if h == 0:
                    e.wait_ge(C.ac.h, an_rb)
                    e.wait_ge(C.da.h, needg)
                if g:
                    e.wait_ge(C.gp.h, g)
                C.dv.inc(e.scalar_tensor_tensor(out=sqp[h % 2][:], in0=outb[:, h * 256:(h + 1) * 256],
                                                scalar=normw[:, h:h + 1], in1=rstd_bc[:],
                                                op0=ALU.mult, op1=ALU.mult))
            dv_pr = C.dv.n
            @block.gpsimd
            def _(e, h=h, dv_pr=dv_pr):
                e.wait_ge(C.dv.h, dv_pr)
                C.gp.inc(e.tensor_tensor(out=prodT[:, h * 256:(h + 1) * 256], in0=sqp[h % 2][:],
                                         in1=gblk[:, h * 256:(h + 1) * 256], op=ALU.mult))
            gp_n2[h % 2] = C.gp.n
            gp_pr = C.gp.n

        if blk + 1 < 4:
            @block.scalar
            def _(e, blk=blk, gp_pr=gp_pr):
                e.wait_ge(C.gp.h, gp_pr)
                ld2[("g", blk + 1)] = C.dma(e, "da",
                    out=gblk[:].rearrange("p (c n) -> p c n", c=16),
                    in_=gd_d.rearrange("c p n -> p c n")[:, :, (blk + 1) * 256:(blk + 2) * 256])

        for dc in range(NCH):
            wbuf = wrow if dc % 2 == 0 else wrow2
            need = wo_ld[blk * NCH + dc]
            @block.tensor
            def _(e, dc=dc, need=need, gp_pr=gp_pr, dv_add=dv_add_prev,
                  wbuf=wbuf, anr=an_rb):
                e.wait_ge(need[0].h, need[1])
                if dc == 0:
                    e.wait_ge(C.gp.h, gp_pr)
                    e.wait_ge(C.ac.h, anr)
                    if dv_add:
                        e.wait_ge(C.dv.h, dv_add)
                for sub in range(2):
                    lhsT = prodT[:, dc * 256 + sub * P: dc * 256 + (sub + 1) * P]
                    for nn_ in range(4):
                        ins_ = e.matmul(PS[:, (sub * 4 + nn_) * 512:(sub * 4 + nn_ + 1) * 512],
                                        lhsT, wbuf[:, nn_ * 512:(nn_ + 1) * 512],
                                        start=(dc == 0), stop=(dc == NCH - 1))
                C.pe.inc(ins_)
            nxt = blk * NCH + dc + 2
            if nxt < 4 * NCH and nxt not in wo_ld:
                pn2 = C.pe.n
                if nxt % 2 == 0:
                    @block.scalar
                    def _(e, nxt=nxt, pn2=pn2):
                        e.wait_ge(C.pe.h, pn2)
                        wo_ld[nxt] = (C.da, C.dma(e, "da", out=wrow[:], in_=wot_bf[nxt % NCH]))
                else:
                    @block.gpsimd
                    def _(e, nxt=nxt, pn2=pn2):
                        e.wait_ge(C.pe.h, pn2)
                        wo_ld[nxt] = (C.dg, C.dma(e, "dg", out=wrow2[:], in_=wot_bf[nxt % NCH]))
        pn_op = C.pe.n
        @block.vector
        def _(e, blk=blk, pn_op=pn_op):
            e.wait_ge(C.pe.h, pn_op)
            for sub in range(2):
                xs = X[:, (blk * 2 + sub) * D:(blk * 2 + sub + 1) * D]
                ins_ = e.tensor_tensor(out=xs, in0=xs,
                                       in1=PS[:, sub * 4 * 512:(sub * 4 + 4) * 512], op=ALU.add)
            C.dv.inc(ins_)
        dv_add_prev = C.dv.n
    C.barrier()
    C.pop_scope()




def emit_l1ffn(C, X, PS):
    nc, block = C.nc, C.block
    ins = C.ins
    C.push_scope()

    XNT3 = C.sbuf("f_xnT", [P, NCH * TOK], BF16)
    hT = C.sbuf("f_hT", [P, 64 * 512], BF16)
    identb = C.sbuf("f_ident", [P, P], BF16)
    onescf = C.sbuf("f_onescf", [P, 1], F32)
    onesrf = C.sbuf("f_onesrf", [1, P], F32)
    xn2 = dbuf(C, "f_xn2", D, BF16)
    wfp = dbuf(C, "f_wfp", D, F32)
    wq = dbuf(C, "f_wq", D, BF16)
    qa = C.sbuf("f_qa", [P, D], BF16)
    b1c = C.sbuf("f_b1c", [P, 64], F32)
    b2r = C.sbuf("f_b2r", [1, D], BF16)
    b2bc = C.sbuf("f_b2bc", [P, D], BF16)
    acc = C.sbuf("f_acc", [P, 2], F32)
    accv = C.sbuf("f_accv", [P, 1], F32)
    sc2 = C.sbuf("f_sc2", [1, 2], F32)
    tbc = C.sbuf("f_tbc", [P, 2], F32)
    ntbc = C.sbuf("f_ntbc", [P, 2], F32)
    scr2 = mk_ln_scr(C, "f_ln", dump_ap=xn2[0], need_xtmp=False)
    w1q_d, w2q_d = C.scr["f_w1q"], C.scr["f_w2q"]
    ccin, ccag = C.scr["f_ccin"], C.scr["f_ccag"]

    @block.gpsimd
    def _(e):
        C.dma(e, "dg", out=identb[:], in_=ins["ident"])
        C.dma(e, "dg", out=b2r[:], in_=ins["b2row"])
    @block.sync
    def _(e):
        C.dma(e, "ds", out=onescf[:], in_=ins["ones_col"])
        C.dma(e, "ds", out=onesrf[:], in_=ins["ones_row"])
        C.dma(e, "ds", out=b1c[:], in_=ins["b1col"])
    dsn0 = C.ds.n
    dgn0 = C.dg.n

    onesrb = C.sbuf("f_onesrb", [1, P], BF16)
    @block.gpsimd
    def _(e):
        C.dma(e, "dg", out=onesrb[:], in_=ins["ones_row"])
    dgn1 = C.dg.n
    @block.tensor
    def _(e):
        e.wait_ge(C.dg.h, dgn1)
        e.wait_ge(C.dg.h, dgn0)
        for i in range(4):
            ins_ = e.matmul(PS[:, (2 + i) * 512:(3 + i) * 512], onesrb[:],
                            b2r[:, i * 512:(i + 1) * 512], start=True, stop=True)
        C.pe.inc(ins_)
    pn_b2 = C.pe.n
    @block.scalar
    def _(e, pn_b2=pn_b2):
        e.wait_ge(C.pe.h, pn_b2)
        C.ac.inc(e.copy(out=b2bc[:], in_=PS[:, 2 * 512:6 * 512]))
    ac_b2 = C.ac.n

    dv_acc = 0
    @block.vector
    def _(e):
        e.memset(acc[:], 0.0)
        C.dv.inc(e.memset(accv[:], 0.0))
    dv_z = C.dv.n
    part_ld = []
    @block.sync
    def _(e):
        part_ld.append(C.dma(e, "ds", out=wfp[0], in_=ins["wpart"][0]))
    ac_abs = 0
    for i in range(16):
        @block.scalar
        def _(e, i=i, need=part_ld[i], dz=dv_z):
            e.wait_ge(C.ds.h, need)
            C.ac.inc(e.activation(out=qa[:], in_=wfp[0], func=AF.Abs, accum_out=accv[:]))
        ac_abs = C.ac.n
        @block.vector
        def _(e, i=i, ac_abs=ac_abs, dz=dv_z):
            e.wait_ge(C.ac.h, ac_abs)
            if i == 0:
                e.wait_ge(C.dv.h, dz)
            col = 0 if i < 8 else 1
            C.dv.inc(e.tensor_tensor(out=acc[:, col:col + 1], in0=acc[:, col:col + 1],
                                     in1=accv[:], op=ALU.add))
            e.wait_ge(C.dv.h, C.dv.n)
        dv_acc = C.dv.n
        if i + 1 < 16:
            @block.sync
            def _(e, i=i, aa=ac_abs):
                e.wait_ge(C.ac.h, aa)
                part_ld.append(C.dma(e, "ds", out=wfp[0], in_=ins["wpart"][i + 1]))
    @block.tensor
    def _(e, da=dv_acc):
        e.wait_ge(C.dv.h, da)
        C.pe.inc(e.matmul(PS[0:1, 0:2], onescf[:], acc[:], start=True, stop=True))
    pn_acc = C.pe.n
    @block.scalar
    def _(e, pn_acc=pn_acc):
        e.wait_ge(C.pe.h, pn_acc)
        C.ac.inc(e.copy(out=sc2[:], in_=PS[0:1, 0:2]))
    ac_sc = C.ac.n
    @block.sync
    def _(e, ac_sc=ac_sc):
        e.wait_ge(C.ac.h, ac_sc)
        C.dma(e, "ds", out=ccin, in_=sc2[:])
    ds_cc = C.ds.n
    @block.gpsimd
    def _(e, ds_cc=ds_cc):
        e.wait_ge(C.ds.h, ds_cc)
        C.cc.inc(e.collective_compute("AllReduce", ALU.add,
                                      replica_groups=[list(range(NCORES))],
                                      ins=[ccin.opt()], outs=[ccag.opt()]))
    cc_n = C.cc.n
    @block.scalar
    def _(e, cc_n=cc_n):
        e.wait_ge(C.cc.h, cc_n)
        C.dma(e, "da", out=sc2[:], in_=ccag)
    da_sc = C.da.n
    @block.vector
    def _(e, da_sc=da_sc):
        e.wait_ge(C.da.h, da_sc)
        C.dv.inc(e.tensor_scalar(out=sc2[:], in0=sc2[:], scalar1=0.7 / (8192.0 * 2048.0),
                                 scalar2=None, op0=ALU.mult))
        e.wait_ge(C.dv.h, C.dv.n)
    dv_t = C.dv.n
    @block.tensor
    def _(e, dv_t=dv_t):
        e.wait_ge(C.dv.h, dv_t)
        C.pe.inc(e.matmul(PS[:, 0:2], onesrf[:], sc2[:], start=True, stop=True))
    pn_t = C.pe.n
    @block.scalar
    def _(e, pn_t=pn_t):
        e.wait_ge(C.pe.h, pn_t)
        C.ac.inc(e.copy(out=tbc[:], in_=PS[:, 0:2]))
    ac_t = C.ac.n
    @block.vector
    def _(e, ac_t=ac_t):
        e.wait_ge(C.ac.h, ac_t)
        C.dv.inc(e.tensor_scalar(out=ntbc[:], in0=tbc[:], scalar1=-1.0, scalar2=None,
                                 op0=ALU.mult))
        e.wait_ge(C.dv.h, C.dv.n)
    dv_nt = C.dv.n

    rdy_hist = []
    pe_tr = []
    ac_ev = []
    for t in range(NT):
        pre = []
        if t >= 1:
            pre.append((C.pe, pe_tr[t - 1]))
        if t >= 2:
            k2, n2 = rdy_hist[t - 2]
            pre.append((C.dv, n2))
            pv = pe_tr[t - 2]
            @block.vector
            def _(e, pv=pv):
                e.wait_ge(C.pe.h, pv)
        scr2[t % 2]["dump"] = xn2[(t + 1) % 2]
        kind, n = emit_ln_tile(C, X[:, t * D:(t + 1) * D], xn2[t % 2], scr2[t % 2],
                               pre_wait=pre)
        rdy_hist.append((kind, n))
        prev_ev = ac_ev[t - 1] if t >= 1 else 0
        @block.tensor
        def _(e, t=t, n=n, prev_ev=prev_ev):
            e.wait_ge(C.dv.h, n)
            e.wait_ge(C.ac.h, prev_ev)
            for dc in range(NCH):
                bb = 6 + dc // 8
                view = PS[:, bb * 512:(bb + 1) * 512].bitcast(BF16)[:, (dc % 8) * P:(dc % 8 + 1) * P]
                ins_ = e.transpose(view, xn2[t % 2][:, dc * P:(dc + 1) * P], identb[:])
            C.pe.inc(ins_)
        pe_tr.append(C.pe.n)
        pn = C.pe.n
        @block.scalar
        def _(e, t=t, pn=pn):
            e.wait_ge(C.pe.h, pn)
            for half in range(2):
                out = XNT3[:].rearrange("p (c n) -> p c n", c=NCH)[:, half * 8:(half + 1) * 8,
                                                                   t * P:(t + 1) * P]
                i2 = PS[:, (6 + half) * 512:(7 + half) * 512].bitcast(BF16).rearrange(
                    "p (c n) -> p c n", c=8)
                ins_ = e.copy(out=out, in_=i2)
            C.ac.inc(ins_)
        ac_ev.append(C.ac.n)
    ac_xnT = C.ac.n

    def emit_quant(src_ap, dst_ap, guards):
        col = guards["col"]
        box = [0]
        @block.vector
        def _(e, col=col, need=guards.get("need"), gpe=guards.get("pe", 0),
              gda=guards.get("da", 0)):
            if need:
                e.wait_ge(need[0].h, need[1])
            C.dv.inc(e.tensor_scalar(out=qa[:], in0=src_ap, scalar1=ntbc[:, col:col + 1],
                                     scalar2=None, op0=ALU.is_le))
            if gpe:
                e.wait_ge(C.pe.h, gpe)
            if gda:
                e.wait_ge(C.da.h, gda)
            C.dv.inc(e.tensor_scalar(out=dst_ap, in0=src_ap, scalar1=tbc[:, col:col + 1],
                                     scalar2=None, op0=ALU.is_ge))
            box[0] = C.dv.n
            C.dv.inc(e.tensor_tensor(out=dst_ap, in0=dst_ap, in1=qa[:], op=ALU.subtract))
        return box[0], C.dv.n

    bank_h = {0: 0, 1: 0}
    dv_down_prev = 0
    for half in range(2):
        hq_ld = []
        @block.sync
        def _(e, half=half, dnt=dv_nt, gpe=C.pe.n, gda=C.da.n):
            e.wait_ge(C.dv.h, dnt)
            if half == 0:
                hq_ld.append((C.ds, C.dma(e, "ds", out=wfp[0], in_=ins["w1_s"][0])))
            else:
                e.wait_ge(C.pe.h, gpe)
                e.wait_ge(C.da.h, gda)
                for i in range(2):
                    hq_ld.append((C.ds, C.dma(e, "ds", out=wq[i], in_=w1q_d[i])))
        da_w1sp = {}
        dv_abh = {}
        pe_prevq = {}
        for f in range(64):
            if half == 0:
                dv_ab, dv_sub = emit_quant(wfp[f % 2], wq[f % 2][:],
                    {"col": 0, "need": hq_ld[f], "pe": pe_prevq.get(f - 2, 0),
                     "da": da_w1sp.get(f - 2, 0)})
                dv_abh[f] = dv_ab
                if f + 1 < 64:
                    ga = dv_abh.get(f - 1, 0)
                    if (f + 1) % 2 == 0:
                        @block.sync
                        def _(e, f=f, ga=ga):
                            if ga:
                                e.wait_ge(C.dv.h, ga)
                            hq_ld.append((C.ds, C.dma(e, "ds", out=wfp[(f + 1) % 2],
                                                      in_=ins["w1_s"][f + 1])))
                    else:
                        @block.gpsimd
                        def _(e, f=f, ga=ga):
                            if ga:
                                e.wait_ge(C.dv.h, ga)
                            hq_ld.append((C.dg, C.dma(e, "dg", out=wfp[(f + 1) % 2],
                                                      in_=ins["w1_s"][f + 1])))
                @block.scalar
                def _(e, f=f, dv_sub=dv_sub):
                    e.wait_ge(C.dv.h, dv_sub)
                    C.dma(e, "da", out=w1q_d[f], in_=wq[f % 2][:])
                da_w1sp[f] = C.da.n
                ready = (C.dv, dv_sub)
            else:
                ready = hq_ld[f]
            bk = f % 2
            @block.tensor
            def _(e, f=f, half=half, bk=bk, ready=ready, ga=bank_h[bk], axn=ac_xnT,
                  gdp=dv_down_prev):
                e.wait_ge(ready[0].h, ready[1])
                if f == 0:
                    e.wait_ge(C.ac.h, axn)
                    if gdp:
                        e.wait_ge(C.dv.h, gdp)
                if ga:
                    e.wait_ge(C.ac.h, ga)
                for dc in range(NCH):
                    lhsT = wq[f % 2][:, dc * P:(dc + 1) * P]
                    rhs = XNT3[:, dc * TOK + half * 512: dc * TOK + (half + 1) * 512]
                    ins_ = e.matmul(PS[:, bk * 512:(bk + 1) * 512], lhsT, rhs,
                                    start=(dc == 0), stop=(dc == NCH - 1))
                C.pe.inc(ins_)
            pn = C.pe.n
            pe_prevq[f] = pn
            @block.scalar
            def _(e, f=f, bk=bk, pn=pn):
                e.wait_ge(C.pe.h, pn)
                C.ac.inc(e.activation(out=hT[:, f * 512:(f + 1) * 512],
                                      in_=PS[:, bk * 512:(bk + 1) * 512], func=AF.Gelu,
                                      bias=b1c[:, f:f + 1], scale=1.0))
            bank_h[bk] = C.ac.n
            if half == 1 and f + 2 < 64:
                if f % 2 == 0:
                    @block.sync
                    def _(e, f=f, pn=pn):
                        e.wait_ge(C.pe.h, pn)
                        hq_ld.append((C.ds, C.dma(e, "ds", out=wq[f % 2], in_=w1q_d[f + 2])))
                else:
                    @block.gpsimd
                    def _(e, f=f, pn=pn):
                        e.wait_ge(C.pe.h, pn)
                        hq_ld.append((C.dg, C.dma(e, "dg", out=wq[f % 2], in_=w1q_d[f + 2])))
        ac_h = C.ac.n

        for sg in range(2):
            dq_ld = []
            quantize_now = (half == 0 and sg == 0)
            @block.sync
            def _(e, sg=sg, quantize_now=quantize_now, gdv=C.dv.n, gpe=C.pe.n, gda=C.da.n):
                if quantize_now:
                    e.wait_ge(C.dv.h, gdv)
                    dq_ld.append((C.ds, C.dma(e, "ds", out=wfp[0], in_=ins["w2_s"][0])))
                else:
                    e.wait_ge(C.pe.h, gpe)
                    e.wait_ge(C.da.h, gda)
                    for i in range(2):
                        dq_ld.append((C.ds, C.dma(e, "ds", out=wq[i], in_=w2q_d[i])))
            da_w2sp = {}
            dv_abd = {}
            pe_prevq = {}
            for f in range(64):
                if quantize_now:
                    dv_ab, dv_sub = emit_quant(wfp[f % 2], wq[f % 2][:],
                        {"col": 1, "need": dq_ld[f], "pe": pe_prevq.get(f - 2, 0),
                         "da": da_w2sp.get(f - 2, 0)})
                    dv_abd[f] = dv_ab
                    if f + 1 < 64:
                        ga = dv_abd.get(f - 1, 0)
                        if (f + 1) % 2 == 0:
                            @block.sync
                            def _(e, f=f, ga=ga):
                                if ga:
                                    e.wait_ge(C.dv.h, ga)
                                dq_ld.append((C.ds, C.dma(e, "ds", out=wfp[(f + 1) % 2],
                                                          in_=ins["w2_s"][f + 1])))
                        else:
                            @block.gpsimd
                            def _(e, f=f, ga=ga):
                                if ga:
                                    e.wait_ge(C.dv.h, ga)
                                dq_ld.append((C.dg, C.dma(e, "dg", out=wfp[(f + 1) % 2],
                                                          in_=ins["w2_s"][f + 1])))
                    @block.scalar
                    def _(e, f=f, dv_sub=dv_sub):
                        e.wait_ge(C.dv.h, dv_sub)
                        C.dma(e, "da", out=w2q_d[f], in_=wq[f % 2][:])
                    da_w2sp[f] = C.da.n
                    ready = (C.dv, dv_sub)
                else:
                    ready = dq_ld[f]
                @block.tensor
                def _(e, f=f, half=half, sg=sg, ready=ready, ach=ac_h, gdp=dv_down_prev):
                    e.wait_ge(ready[0].h, ready[1])
                    if f == 0:
                        e.wait_ge(C.ac.h, ach)
                        if gdp:
                            e.wait_ge(C.dv.h, gdp)
                    for si in range(2):
                        s = sg * 2 + si
                        lhsT = hT[:, f * 512 + s * P: f * 512 + (s + 1) * P]
                        for n_ in range(4):
                            ins_ = e.matmul(PS[:, (si * 4 + n_) * 512:(si * 4 + n_ + 1) * 512],
                                            lhsT, wq[f % 2][:, n_ * 512:(n_ + 1) * 512],
                                            start=(f == 0), stop=(f == 63))
                    C.pe.inc(ins_)
                pn = C.pe.n
                pe_prevq[f] = pn
                if (not quantize_now) and f + 2 < 64:
                    if f % 2 == 0:
                        @block.sync
                        def _(e, f=f, pn=pn):
                            e.wait_ge(C.pe.h, pn)
                            dq_ld.append((C.ds, C.dma(e, "ds", out=wq[f % 2], in_=w2q_d[f + 2])))
                    else:
                        @block.gpsimd
                        def _(e, f=f, pn=pn):
                            e.wait_ge(C.pe.h, pn)
                            dq_ld.append((C.dg, C.dma(e, "dg", out=wq[f % 2], in_=w2q_d[f + 2])))
            pn_dn = C.pe.n
            @block.vector
            def _(e, half=half, sg=sg, pn_dn=pn_dn):
                e.wait_ge(C.pe.h, pn_dn)
                for si in range(2):
                    s = sg * 2 + si
                    tile = half * 4 + s
                    xs = X[:, tile * D:(tile + 1) * D]
                    ins_ = e.tensor_tensor(out=xs, in0=xs,
                                           in1=PS[:, si * 4 * 512:(si * 4 + 4) * 512], op=ALU.add)
                C.dv.inc(ins_)
            dv_down_prev = C.dv.n

    @block.vector
    def _(e, ac_b2=ac_b2):
        e.wait_ge(C.ac.h, ac_b2)
        for t in range(NT):
            xs = X[:, t * D:(t + 1) * D]
            ins_ = e.tensor_tensor(out=xs, in0=xs, in1=b2bc[:], op=ALU.add)
        C.dv.inc(ins_)

    C.barrier()
    C.pop_scope()


def build_program(blocks=("l0",)):
    nc = bass.Bass("TRN2", target_bir_lowering=False)
    es = ExitStack()
    ins = {}

    def din(name, shape, dtype=F32):
        ins[name] = nc.dram_tensor(name, list(shape), dtype, kind="ExternalInput").ap()

    din("x", (TOK + HALO, D) if "l0" in blocks else (TOK, D))
    xout = nc.dram_tensor("xout", [TOK, D], F32, kind="ExternalOutput").ap()
    if "l0" in blocks:
        din("wq_s", (NCH, P, D)); din("wk_s", (4, P, D)); din("wv_s", (4, P, D))
        din("wo_s", (NCH, P, D)); din("lnw_row", (1, D))
        din("cs_fm", (P, 1536)); din("sn_fm", (P, 1536)); din("mask0", (NT, P, KW))
    if "moe" in blocks:
        din("gwt", (P, NCH * 32)); din("wg_t", (32, P, NCH * 768))
        din("wu_t", (32, P, NCH * 768)); din("wd_t", (32, P, 6 * D))
    if "l1a" in blocks:
        din("wqkv_s", (48, P, D)); din("wgate_s", (NCH, P, D)); din("wot_r", (NCH, P, D))
        din("normw_col", (P, 16)); din("slope", (1, 16)); din("causal01", (P, 512))
        din("arqneg", (1, 256)); din("arkneg", (P, 2)); din("arkpos", (P, 2)); din("tcneg", (1, 8))
    if "l1f" in blocks:
        din("w1_s", (64, P, D)); din("w2_s", (64, P, D)); din("wpart", (16, P, D))
        din("b1col", (P, 64)); din("b2row", (1, D))
    din("ident", (P, P)); din("pswap", (P, P))
    din("ones_row", (1, P)); din("ones_col", (P, 1))
    if DBG_TAP:
        ins["dbg"] = nc.dram_tensor("dbg", [P, NCH * TOK], BF16, kind="ExternalOutput").ap()
        ins["dbgk"] = nc.dram_tensor("dbgk", [P, 4 * 1536], BF16, kind="ExternalOutput").ap()
        ins["dbgv"] = nc.dram_tensor("dbgv", [P, LT * 512], BF16, kind="ExternalOutput").ap()
        ins["dbgq"] = nc.dram_tensor("dbgq", [P, NCH * TOK], BF16, kind="ExternalOutput").ap()

    scr = {}
    if "l0" in blocks:
        scr["l0_qT"] = nc.dram_tensor("l0_qT", [NCH, P, TOK], BF16).ap()
    if "moe" in blocks:
        scr["moe_mw"] = nc.dram_tensor("moe_mw", [32, NT * P], BF16).ap()
    if "l1a" in blocks:
        for nm in ("l1_q", "l1_k", "l1_v", "l1_g"):
            scr[nm] = nc.dram_tensor(nm, [NCH, P, TOK], BF16).ap()
        scr["l1_wotbf"] = nc.dram_tensor("l1_wotbf", [NCH, P, D], BF16).ap()
        scr["l1_ccin"] = nc.dram_tensor("l1_ccin", [16 * P, P], F32).ap()
        scr["l1_ccag"] = nc.dram_tensor("l1_ccag", [NCORES * 16 * P, P], F32).ap()
    if "l1f" in blocks:
        scr["f_w1q"] = nc.dram_tensor("f_w1q", [64, P, D], BF16).ap()
        scr["f_w2q"] = nc.dram_tensor("f_w2q", [64, P, D], BF16).ap()
        scr["f_ccin"] = nc.dram_tensor("f_ccin", [1, 2], F32).ap()
        scr["f_ccag"] = nc.dram_tensor("f_ccag", [1, 2], F32).ap()

    block = es.enter_context(nc.Block())
    C = Ctx(nc, es, block)
    C.ins = ins
    C.scr = scr

    X = C.sbuf("X", [P, NT * D], F32)
    PS = es.enter_context(nc.psum_tensor("PS", [P, 4096], F32))[:]

    xl = []
    off = HALO if "l0" in blocks else 0
    @block.sync
    def _(e):
        for t in range(NT):
            xl.append(C.dma(e, "ds", out=X[:, t * D:(t + 1) * D],
                                           in_=ins["x"][off + t * P: off + (t + 1) * P, :]))
    for eng in ("scalar", "vector", "tensor", "gpsimd"):
        def mk(eng=eng):
            def f(e):
                e.wait_ge(C.ds.h, xl[-1])
            return f
        getattr(block, eng)(mk())

    if "l0" in blocks:
        emit_l0attn(C, X, PS)
    if "moe" in blocks:
        emit_moe(C, X, PS)
    if "l1a" in blocks:
        emit_l1attn(C, X, PS)
    if "l1f" in blocks:
        emit_l1ffn(C, X, PS)

    stn = []
    @block.sync
    def _(e):
        e.wait_ge(C.dv.h, C.dv.n)
        e.wait_ge(C.ac.h, C.ac.n)
        e.wait_ge(C.gp.h, C.gp.n)
        e.wait_ge(C.pe.h, C.pe.n)
        for t in range(NT):
            stn.append(C.dma(e, "ds", out=xout[t * P:(t + 1) * P, :],
                                            in_=X[:, t * D:(t + 1) * D]))
        e.wait_ge(C.ds.h, stn[-1])

    es.close()
    return nc


def kernel(**inputs):
    per_core, shared = make_host_inputs(inputs)
    nc = build_program(blocks=("l0", "moe", "l1a", "l1f"))
    in_maps = []
    for c in range(NCORES):
        m = dict(shared)
        m.update(per_core[c])
        in_maps.append(m)
    res = run_bass_kernel_spmd(nc, in_maps, core_ids=list(range(NCORES)))
    x = [res.results[c]["xout"] for c in range(NCORES)]
    return np.concatenate(x, axis=0).reshape(2, 4096, D)


# version 11
# speedup vs baseline: 1.1857x; 1.0073x over previous
import numpy as np
from contextlib import ExitStack
import concourse.bass as bass
import concourse.mybir as mybir
from concourse.bass_utils import run_bass_kernel_spmd

F32 = mybir.dt.float32
BF16 = mybir.dt.bfloat16
AF = mybir.ActivationFunctionType
ALU = mybir.AluOpType
AX = mybir.AxisListType

P = 128
D = 2048
NCORES = 8
TOK = 1024
NT = TOK // P
HALO = 512
LT = (TOK + HALO) // P
NCH = D // P
NEG = -3.0e38
W0 = 512
KW = W0 + P
INV_SQRT_HD = 0.08838834764831845
L0_STAGE = 3
MOE_STAGE = 3
L1_STAGE = 3
DBG_TAP = False


def _chunk_stream(w, oc_size=P):
    K, M = w.shape
    n_oc = M // oc_size
    return np.ascontiguousarray(
        w.reshape(K // P, P, n_oc, oc_size).transpose(2, 1, 0, 3).reshape(n_oc, P, -1))


def _rope_perm():
    pm = np.zeros(P, dtype=np.int64)
    pm[0:64] = np.arange(0, P, 2)
    pm[64:P] = np.arange(1, P, 2)
    return pm


def make_host_inputs(inputs):
    f32 = np.float32
    x = np.asarray(inputs["L0_attn_x"], f32).reshape(2 * 4096, D)
    per_core = [dict() for _ in range(NCORES)]
    shared = {}

    for c in range(NCORES):
        blkx = np.zeros((TOK + HALO, D), f32)
        if c % 4 == 0:
            blkx[HALO:] = x[c * TOK:(c + 1) * TOK]
        else:
            blkx[:] = x[c * TOK - HALO:(c + 1) * TOK]
        per_core[c]["x"] = blkx

    pm = _rope_perm()
    wq = np.asarray(inputs["L0_attn_wq"], f32)
    wk = np.asarray(inputs["L0_attn_wk"], f32)
    wv = np.asarray(inputs["L0_attn_wv"], f32)
    wo = np.asarray(inputs["L0_attn_wo"], f32)
    qperm = (np.arange(16)[:, None] * P + pm[None, :]).reshape(-1)
    kperm = (np.arange(4)[:, None] * P + pm[None, :]).reshape(-1)
    shared["wq_s"] = _chunk_stream(wq[:, qperm])
    shared["wk_s"] = _chunk_stream(wk[:, kperm])
    shared["wv_s"] = _chunk_stream(wv)
    shared["wo_s"] = _chunk_stream(wo)
    shared["lnw_row"] = np.asarray(inputs["L0_attn_ln_w"], f32).reshape(1, D)

    inv_freq = (1.0 / 10000.0 ** (np.arange(0, 128, 2) / 128.0)).astype(f32)
    for c in range(NCORES):
        pos0 = (c % 4) * TOK - HALO
        pos = pos0 + np.arange(TOK + HALO, dtype=f32)
        if c % 4 == 0:
            pos = np.maximum(pos, 0.0)
        ang = inv_freq[:, None] * pos[None, :]
        per_core[c]["cs_fm"] = np.concatenate([np.cos(ang), np.cos(ang)], 0).astype(f32)
        per_core[c]["sn_fm"] = np.concatenate([-np.sin(ang), np.sin(ang)], 0).astype(f32)

    r = np.arange(P)[:, None]
    j = np.arange(KW)[None, :]
    band = (j >= r) & (j <= r + W0)
    for c in range(NCORES):
        m = np.zeros((NT, P, KW), f32)
        for t in range(NT):
            ok = band.copy()
            if c % 4 == 0:
                ok = ok & ((t * P + j) >= HALO)
            m[t] = np.where(ok, 0.0, NEG)
        per_core[c]["mask0"] = m

    gw = np.asarray(inputs["L0_ffn_gate_w"], f32)
    shared["gwt"] = np.ascontiguousarray(gw.T.reshape(NCH, P, 32).transpose(1, 0, 2).reshape(P, NCH * 32))
    wg = np.asarray(inputs["L0_ffn_w_gate_proj"], f32)
    wu = np.asarray(inputs["L0_ffn_w_up_proj"], f32)
    wd = np.asarray(inputs["L0_ffn_w_down_proj"], f32)
    shared["wg_t"] = np.ascontiguousarray(wg.transpose(0, 2, 1).reshape(32, NCH, P, 768)
                                          .transpose(0, 2, 1, 3).reshape(32, P, NCH * 768))
    shared["wu_t"] = np.ascontiguousarray(wu.transpose(0, 2, 1).reshape(32, NCH, P, 768)
                                          .transpose(0, 2, 1, 3).reshape(32, P, NCH * 768))
    shared["wd_t"] = np.ascontiguousarray(wd.transpose(0, 2, 1).reshape(32, 6, P, D)
                                          .transpose(0, 2, 1, 3).reshape(32, P, 6 * D))

    wqkv = np.asarray(inputs["L1_attn_w_qkv"], f32)
    wgate1 = np.asarray(inputs["L1_attn_w_gate"], f32)
    wout1 = np.asarray(inputs["L1_attn_w_out"], f32)
    shared["wqkv_s"] = _chunk_stream(np.ascontiguousarray(wqkv.T))
    shared["wgate_s"] = _chunk_stream(np.ascontiguousarray(wgate1.T))
    shared["wot_r"] = np.ascontiguousarray(wout1.T.reshape(NCH, P, D))
    shared["normw_col"] = np.ascontiguousarray(
        np.asarray(inputs["L1_attn_norm_w"], f32).reshape(NCH, P).T)
    shared["slope"] = np.asarray(inputs["L1_attn_slope_rate"], f32).reshape(1, 16)
    cz = np.zeros((P, 512), f32)
    for sub in range(2):
        for khat in range(P):
            kg = sub * P + khat
            m = np.arange(256)
            cz[khat, sub * 256:(sub + 1) * 256] = (m >= kg).astype(f32)
    shared["causal01"] = cz
    shared["arqneg"] = (-(np.arange(256, dtype=f32) + 1.0)).reshape(1, 256)
    ak = np.zeros((P, 2), f32)
    akp = np.zeros((P, 2), f32)
    for sub in range(2):
        ak[:, sub] = -(255.0 - (sub * P + np.arange(P, dtype=f32)))
        akp[:, sub] = (sub * P + np.arange(P, dtype=f32)) + 1.0
    shared["arkneg"] = ak
    shared["arkpos"] = akp
    for c in range(NCORES):
        crel = c % 4
        tc = np.full((1, 8), -1e8, f32)
        for r in range(NCORES):
            if r // 4 == c // 4 and (r % 4) < crel:
                tc[0, r] = -1024.0 * (crel - 1 - (r % 4))
        per_core[c]["tcneg"] = tc

    w1 = np.asarray(inputs["L1_ffn_w1"], f32)
    w2 = np.asarray(inputs["L1_ffn_w2"], f32)
    w1_s = _chunk_stream(np.ascontiguousarray(w1.T))
    w2_s = np.ascontiguousarray(w2.T.reshape(64, P, D))
    shared["w1_s"] = w1_s
    shared["w2_s"] = w2_s
    shared["b1col"] = np.ascontiguousarray(
        np.asarray(inputs["L1_ffn_b1"], f32).reshape(64, P).T)
    shared["b2row"] = np.asarray(inputs["L1_ffn_b2"], f32).reshape(1, D)
    for c in range(NCORES):
        per_core[c]["wpart"] = np.ascontiguousarray(
            np.concatenate([w1_s[c * 8:(c + 1) * 8], w2_s[c * 8:(c + 1) * 8]], axis=0))

    shared["ident"] = np.eye(P, dtype=f32)
    psw = np.zeros((P, P), f32)
    for i in range(P):
        psw[(i + 64) % P, i] = 1.0
    shared["pswap"] = psw
    shared["ones_row"] = np.ones((1, P), f32)
    shared["ones_col"] = np.ones((P, 1), f32)
    return per_core, shared



class Sem:
    def __init__(self, nc, es, name):
        self.h = es.enter_context(nc.semaphore(name))
        self.n = 0

    def inc(self, instr, d=1):
        instr.then_inc(self.h, d)
        self.n += d
        return self.n


class Ctx:
    def __init__(self, nc, es, block):
        self.nc, self.es, self.block = nc, es, block
        self.pe = Sem(nc, es, "s_pe")
        self.dv = Sem(nc, es, "s_dv")
        self.ac = Sem(nc, es, "s_ac")
        self.gp = Sem(nc, es, "s_gp")
        self.ds = Sem(nc, es, "s_ds")
        self.da = Sem(nc, es, "s_da")
        self.dg = Sem(nc, es, "s_dg")
        self.cc = Sem(nc, es, "s_cc")
        self.ins = {}
        self.scr = {}
        self._scope = []

    def sbuf(self, name, shape, dtype):
        t = self._scope[-1].enter_context(self.nc.sbuf_tensor(name, shape, dtype)) \
            if self._scope else self.es.enter_context(self.nc.sbuf_tensor(name, shape, dtype))
        return t

    def dma(self, e, q, out, in_):
        s = {"ds": self.ds, "da": self.da, "dg": self.dg}[q]
        if s.n:
            e.wait_ge(s.h, s.n)
        return s.inc(e.dma_start(out=out, in_=in_), 16)

    def push_scope(self):
        self._scope.append(ExitStack())

    def pop_scope(self):
        self._scope.pop().close()

    def barrier(self):
        C = self
        targets = [(C.pe, "tensor"), (C.dv, "vector"), (C.ac, "scalar"),
                   (C.gp, "gpsimd"), (C.ds, "sync"), (C.da, None), (C.dg, None)]
        vals = [(sem, sem.n) for sem, eng in targets]
        for sem, eng in targets:
            if eng is None:
                continue
            def mk2(eng=eng, sem=sem):
                def f(e):
                    for s2, n2 in vals:
                        if s2 is not sem and n2:
                            e.wait_ge(s2.h, n2)
                return f
            getattr(C.block, eng)(mk2())



def dbuf(C, name, cols, dtype, n=2):
    t = C.sbuf(name, [P, n * cols], dtype)
    return [t[:, i * cols:(i + 1) * cols] for i in range(n)]


def emit_ln_tile(C, src, xn_out, scr, l0_rms=False, lnw_bc=None, pre_wait=None):
    block = C.block
    def w(e):
        if pre_wait:
            for s, n in pre_wait:
                e.wait_ge(s.h, n)

    def step(e, instr):
        C.dv.inc(instr)
        e.wait_ge(C.dv.h, C.dv.n)

    @block.scalar
    def _(e):
        w(e)
        e.activation(out=scr["dump"], in_=src, func=AF.Square, accum_out=scr["sumsq"][:])
        C.ac.inc(e.activation(out=scr["dump"], in_=src, func=AF.Identity,
                              accum_out=scr["sums"][:]))
    ac_stats = C.ac.n

    @block.vector
    def _(e):
        e.wait_ge(C.ac.h, ac_stats)
        step(e, e.tensor_scalar(out=scr["mean"][:], in0=scr["sums"][:], scalar1=1.0 / D,
                                scalar2=None, op0=ALU.mult))
        step(e, e.tensor_scalar(out=scr["t0"][:], in0=scr["mean"][:], scalar1=scr["mean"][:],
                                scalar2=None, op0=ALU.mult))
        step(e, e.tensor_scalar(out=scr["var"][:], in0=scr["sumsq"][:], scalar1=1.0 / D,
                                scalar2=scr["t0"][:], op0=ALU.mult, op1=ALU.subtract))
        C.dv.inc(e.tensor_scalar(out=scr["vare"][:], in0=scr["var"][:], scalar1=1e-5,
                                 scalar2=None, op0=ALU.add))
    dv_var = C.dv.n

    @block.scalar
    def _(e):
        e.wait_ge(C.dv.h, dv_var)
        C.ac.inc(e.activation(out=scr["std"][:], in_=scr["vare"][:], func=AF.Sqrt))
    ac_std = C.ac.n

    @block.vector
    def _(e):
        e.wait_ge(C.ac.h, ac_std)
        C.dv.inc(e.reciprocal(scr["rstd"][:], scr["std"][:]))
        e.wait_ge(C.dv.h, C.dv.n)

    if l0_rms:
        @block.vector
        def _(e):
            step(e, e.tensor_scalar(out=scr["t0"][:], in0=scr["var"][:], scalar1=scr["rstd"][:],
                                    scalar2=scr["rstd"][:], op0=ALU.mult, op1=ALU.mult))
            C.dv.inc(e.tensor_scalar(out=scr["t0"][:], in0=scr["t0"][:], scalar1=1e-6,
                                     scalar2=None, op0=ALU.add))
        dv_t = C.dv.n

        @block.scalar
        def _(e):
            e.wait_ge(C.dv.h, dv_t)
            C.ac.inc(e.activation(out=scr["std"][:], in_=scr["t0"][:], func=AF.Sqrt))
        ac2 = C.ac.n

        @block.vector
        def _(e):
            e.wait_ge(C.ac.h, ac2)
            step(e, e.reciprocal(scr["t0"][:], scr["std"][:]))
            step(e, e.tensor_scalar(out=scr["s_t"][:], in0=scr["rstd"][:],
                                    scalar1=scr["t0"][:], scalar2=None, op0=ALU.mult))
        scale_ap = scr["s_t"]
    else:
        scale_ap = scr["rstd"]

    if lnw_bc is None:
        @block.vector
        def _(e):
            w(e)
            C.dv.inc(e.tensor_scalar(out=xn_out, in0=src, scalar1=scr["mean"][:],
                                     scalar2=scale_ap[:], op0=ALU.subtract, op1=ALU.mult))
        return ("dv", C.dv.n)
    else:
        @block.vector
        def _(e):
            w(e)
            C.dv.inc(e.tensor_scalar(out=scr["xtmp"][:][:, 0:D] if False else scr["xtmp"][:], in0=src, scalar1=scr["mean"][:],
                                     scalar2=scale_ap[:], op0=ALU.subtract, op1=ALU.mult))
        dvn = C.dv.n

        @block.gpsimd
        def _(e):
            e.wait_ge(C.dv.h, dvn)
            C.gp.inc(e.tensor_tensor(out=xn_out, in0=scr["xtmp"][:], in1=lnw_bc, op=ALU.mult))
        return ("gp", C.gp.n)


def mk_ln_scr(C, tag, dump_ap=None, need_xtmp=True):
    dump = dump_ap if dump_ap is not None else C.sbuf(tag + "dump", [P, D], BF16)[:]
    xtmp = C.sbuf(tag + "xtmp", [P, D], BF16) if need_xtmp else None
    out = []
    for i in range(2):
        scr = {k: C.sbuf(tag + k + str(i), [P, 1], F32) for k in
               ["sums", "sumsq", "mean", "var", "vare", "t0", "std", "rstd", "s_t"]}
        scr["dump"] = dump
        scr["xtmp"] = xtmp
        out.append(scr)
    return out


def emit_bcast_row(C, PS, bank, onesr, row_ap, out_ap, n):
    block = C.block
    @block.tensor
    def _(e):
        nb = 0
        for off in range(0, n, 512):
            w_ = min(512, n - off)
            ins_ = e.matmul(PS[:, bank * 512 + off: bank * 512 + off + w_],
                            onesr[:], row_ap[:, off:off + w_], start=True, stop=True)
        C.pe.inc(ins_)
    pn = C.pe.n

    @block.scalar
    def _(e):
        e.wait_ge(C.pe.h, pn)
        C.ac.inc(e.copy(out=out_ap, in_=PS[:, bank * 512: bank * 512 + n]))
    return C.ac.n



def absmean_alloc(C):
    block = C.block
    b = {}
    b["acc"] = C.sbuf("am_acc", [P, 2], F32)
    b["accv"] = C.sbuf("am_accv", [P, 1], F32)
    b["sc2"] = C.sbuf("am_sc2", [1, 2], F32)
    b["tbc"] = C.sbuf("am_tbc", [P, 2], F32)
    b["ntbc"] = C.sbuf("am_ntbc", [P, 2], F32)
    b["onescf"] = C.sbuf("am_onesc", [P, 1], F32)
    @block.sync
    def _(e):
        C.dma(e, "ds", out=b["onescf"][:], in_=C.ins["ones_col"])
    b["ds0"] = C.ds.n
    @block.vector
    def _(e):
        e.memset(b["acc"][:], 0.0)
        C.dv.inc(e.memset(b["accv"][:], 0.0))
        e.wait_ge(C.dv.h, C.dv.n)
    b["dz"] = C.dv.n
    b["st"] = {"i": 0, "ld": 0, "ac2": [0, 0], "dv": 0}
    return b


def absmean_step(C, stage_ap, first_guard=None):
    am = getattr(C, "am", None)
    if am is None or am["st"]["i"] >= 16:
        return
    block = C.block
    s = am["st"]
    i = s["i"]
    s["i"] = i + 1
    col = 0 if i < 8 else 1
    @block.scalar
    def _(e, i=i, ga=s["ac2"][i % 2], fg=first_guard):
        if fg:
            e.wait_ge(fg[0].h, fg[1])
        if ga:
            e.wait_ge(C.ac.h, ga)
        s["ld"] = C.dma(e, "da", out=stage_ap, in_=C.ins["wpart"][i])
    @block.scalar
    def _(e, ld=s["ld"], dz=am["dz"], i=i, gdv=s["dv"]):
        e.wait_ge(C.da.h, ld)
        if i == 0:
            e.wait_ge(C.dv.h, dz)
        if gdv:
            e.wait_ge(C.dv.h, gdv)
        C.ac.inc(e.activation(out=stage_ap, in_=stage_ap, func=AF.Abs,
                              accum_out=am["accv"][:]))
    s["ac2"][i % 2] = C.ac.n
    @block.vector
    def _(e, an=s["ac2"][i % 2], col=col):
        e.wait_ge(C.ac.h, an)
        C.dv.inc(e.tensor_tensor(out=am["acc"][:, col:col + 1], in0=am["acc"][:, col:col + 1],
                                 in1=am["accv"][:], op=ALU.add))
        e.wait_ge(C.dv.h, C.dv.n)
    s["dv"] = C.dv.n


def absmean_finish_a(C):
    am = C.am
    block = C.block
    assert am["st"]["i"] == 16, am["st"]["i"]
    ccin, ccag = C.scr["f_ccin"], C.scr["f_ccag"]
    @block.tensor
    def _(e, da=am["st"]["dv"], ds0=am["ds0"]):
        e.wait_ge(C.dv.h, da)
        e.wait_ge(C.ds.h, ds0)
        C.pe.inc(e.matmul(C.PS[0:1, 0:2], am["onescf"][:], am["acc"][:],
                          start=True, stop=True))
    pn = C.pe.n
    @block.scalar
    def _(e, pn=pn):
        e.wait_ge(C.pe.h, pn)
        C.ac.inc(e.copy(out=am["sc2"][:], in_=C.PS[0:1, 0:2]))
    am["ac_sc"] = C.ac.n
    @block.sync
    def _(e, an=am["ac_sc"]):
        e.wait_ge(C.ac.h, an)
        C.dma(e, "ds", out=ccin, in_=am["sc2"][:])
    ds_cc = C.ds.n
    @block.gpsimd
    def _(e, ds_cc=ds_cc):
        e.wait_ge(C.ds.h, ds_cc)
        C.cc.inc(e.collective_compute("AllReduce", ALU.add,
                                      replica_groups=[list(range(NCORES))],
                                      ins=[ccin.opt()], outs=[ccag.opt()]))
    am["cc"] = C.cc.n


def absmean_finish_b(C, onesrf_ap, ds_need):
    am = C.am
    block = C.block
    ccag = C.scr["f_ccag"]
    @block.scalar
    def _(e, cc_n=am["cc"]):
        e.wait_ge(C.cc.h, cc_n)
        am["da_sc"] = C.dma(e, "da", out=am["sc2"][:], in_=ccag)
    @block.vector
    def _(e):
        e.wait_ge(C.da.h, am["da_sc"])
        C.dv.inc(e.tensor_scalar(out=am["sc2"][:], in0=am["sc2"][:],
                                 scalar1=0.7 / (8192.0 * 2048.0), scalar2=None, op0=ALU.mult))
        e.wait_ge(C.dv.h, C.dv.n)
    dv_t = C.dv.n
    @block.tensor
    def _(e, dv_t=dv_t, ds_need=ds_need):
        e.wait_ge(C.dv.h, dv_t)
        e.wait_ge(C.ds.h, ds_need)
        C.pe.inc(e.matmul(C.PS[:, 0:2], onesrf_ap, am["sc2"][:], start=True, stop=True))
    pn_t = C.pe.n
    @block.scalar
    def _(e, pn_t=pn_t):
        e.wait_ge(C.pe.h, pn_t)
        C.ac.inc(e.copy(out=am["tbc"][:], in_=C.PS[:, 0:2]))
    am["ac_t"] = C.ac.n
    @block.vector
    def _(e, ac_t=am["ac_t"]):
        e.wait_ge(C.ac.h, ac_t)
        C.dv.inc(e.tensor_scalar(out=am["ntbc"][:], in0=am["tbc"][:], scalar1=-1.0,
                                 scalar2=None, op0=ALU.mult))
        e.wait_ge(C.dv.h, C.dv.n)
    am["dv_nt"] = C.dv.n


def emit_l0attn(C, X, PS):
    nc, block = C.nc, C.block
    ins = C.ins
    C.push_scope()
    kT = C.sbuf("l0kT", [P, 4 * 1536], BF16)
    vtok = C.sbuf("l0vtok", [P, LT * 512], BF16)
    identb = C.sbuf("identb", [P, P], BF16)
    pswapb = C.sbuf("pswapb", [P, P], BF16)
    onesr = C.sbuf("onesr", [1, P], BF16)
    C.push_scope()

    big0 = C.sbuf("big0", [P, NCH * 1536], BF16)
    wst = dbuf(C, "l0wst", NCH * P, BF16)
    csf = C.sbuf("l0csf", [P, 1536], BF16)
    snf = C.sbuf("l0snf", [P, 1536], BF16)
    lnwb = C.sbuf("l0lnwb", [P, D], BF16)
    lnwrow = C.sbuf("lnwrow", [1, D], BF16)
    xhalo = dbuf(C, "l0xhalo", D, F32)
    xn2 = dbuf(C, "l0xn2", D, BF16)
    qsb = dbuf(C, "l0qsb", 512, BF16)
    rsw = dbuf(C, "l0rsw", 512, BF16)
    rcs = dbuf(C, "l0rcs", 512, BF16)
    qasm = dbuf(C, "l0qasm", TOK, BF16)
    scr2 = mk_ln_scr(C, "l0ln")

    qT_d = C.scr["l0_qT"]

    @block.gpsimd
    def _(e):
        C.dma(e, "dg", out=csf[:], in_=ins["cs_fm"])
        C.dma(e, "dg", out=snf[:], in_=ins["sn_fm"])
        C.dma(e, "dg", out=identb[:], in_=ins["ident"])
        C.dma(e, "dg", out=pswapb[:], in_=ins["pswap"])
        C.dma(e, "dg", out=onesr[:], in_=ins["ones_row"])
        C.dma(e, "dg", out=lnwrow[:], in_=ins["lnw_row"])
    dg_const = C.dg.n

    @block.tensor
    def _(e):
        e.wait_ge(C.dg.h, dg_const)
    @block.scalar
    def _(e):
        e.wait_ge(C.dg.h, dg_const)
    ac_lnw = emit_bcast_row(C, PS, 0, onesr, lnwrow, lnwb[:], D)

    halo_ld = []
    @block.sync
    def _(e):
        for i in range(2):
            halo_ld.append(C.dma(e, "ds", out=xhalo[i], in_=ins["x"][i * P:(i + 1) * P, :]))

    pe_tr = []
    ac_ev = []
    rdy_hist = []
    for lt in range(LT):
        if lt < 4:
            src = xhalo[lt % 2]
            pre = [(C.ds, halo_ld[lt])]
        else:
            src = X[:, (lt - 4) * D:(lt - 3) * D]
            pre = []
        if lt >= 2:
            k2, n2 = rdy_hist[lt - 2]
            pre.append(({"dv": C.dv, "gp": C.gp}[k2], n2))
            k1, n1 = rdy_hist[lt - 1]
            pre.append(({"dv": C.dv, "gp": C.gp}[k1], n1))
            pv = pe_tr[lt - 2]
            @block.vector
            def _(e, pv=pv):
                e.wait_ge(C.pe.h, pv)
            @block.gpsimd
            def _(e, pv=pv):
                e.wait_ge(C.pe.h, pv)
        kind, n = emit_ln_tile(C, src, xn2[lt % 2], scr2[lt % 2], l0_rms=True, lnw_bc=lnwb[:],
                               pre_wait=pre)
        rdy_hist.append((kind, n))
        sem = {"dv": C.dv, "gp": C.gp}[kind]
        if lt + 2 < 4:
            @block.sync
            def _(e, sem=sem, n=n, lt=lt):
                e.wait_ge(sem.h, n)
                halo_ld.append(C.dma(e, "ds", out=xhalo[lt % 2],
                                                    in_=ins["x"][(lt + 2) * P:(lt + 3) * P, :]))
        bank = (lt % 2) * 2
        prev_ev = ac_ev[lt - 2] if lt >= 2 else ac_lnw
        @block.tensor
        def _(e, sem=sem, n=n, bank=bank, prev_ev=prev_ev, lt=lt):
            e.wait_ge(sem.h, n)
            e.wait_ge(C.ac.h, prev_ev)
            for dc in range(NCH):
                bb = bank + dc // 8
                view = PS[:, bb * 512:(bb + 1) * 512].bitcast(BF16)[:, (dc % 8) * P:(dc % 8 + 1) * P]
                ins_ = e.transpose(view, xn2[lt % 2][:, dc * P:(dc + 1) * P], identb[:])
            C.pe.inc(ins_)
        pe_tr.append(C.pe.n)
        pn = C.pe.n

        @block.scalar
        def _(e, bank=bank, lt=lt, pn=pn):
            e.wait_ge(C.pe.h, pn)
            for half in range(2):
                bb = bank + half
                out = big0[:].rearrange("p (c n) -> p c n", c=NCH)[:, half * 8:(half + 1) * 8,
                                                                   lt * P:(lt + 1) * P]
                i2 = PS[:, bb * 512:(bb + 1) * 512].bitcast(BF16).rearrange("p (c n) -> p c n", c=8)
                ins_ = e.copy(out=out, in_=i2)
            C.ac.inc(ins_)
        ac_ev.append(C.ac.n)
        if lt >= 5 and C.am is not None:
            for _k in range(2):
                i = C.am["st"]["i"]
                fg = None
                if i < 2:
                    kk, nn = rdy_hist[2 + i]
                    fg = ({"dv": C.dv, "gp": C.gp}[kk], nn)
                absmean_step(C, xhalo[i % 2][:], first_guard=fg)
    ac_xnT = C.ac.n

    stream_ld = []
    bank_guard = {b: ("ac", 0) for b in range(8)}

    def guard_wait(e, b):
        kind, n = bank_guard[b]
        if n:
            e.wait_ge({"ac": C.ac, "dv": C.dv, "gp": C.gp, "ds": C.ds, "da": C.da}[kind].h, n)

    jobs = []
    for oc in range(4):
        jobs.append(("k", "wk_s", oc, 0, 1536))
    for oc in range(4):
        jobs.append(("v", "wv_s", oc, 0, 1536))
    for oc in range(NCH):
        jobs.append(("q", "wq_s", oc, HALO, TOK))

    @block.gpsimd
    def _(e):
        e.wait_ge(C.ac.h, ac_xnT)
        for i in range(2):
            tag, wname, oc, _, _ = jobs[i]
            stream_ld.append(C.dma(e, "dg", out=wst[i % 2], in_=ins[wname][oc]))

    vT_sb = dbuf(C, "l0vTsb", 1536, BF16)
    git = [0]
    rope_gp_hist = []
    qasm_spill = {0: 0, 1: 0}

    for ji, (tag, wname, oc, off, cols) in enumerate(jobs):
        need = stream_ld[ji]
        ngrp = cols // 512
        for g in range(ngrp):
            it = git[0]; git[0] += 1
            pb = 4 + (it % 2)
            @block.tensor
            def _(e, ji=ji, need=need, oc=oc, off=off, g=g, pb=pb, tag=tag):
                e.wait_ge(C.dg.h, need)
                if ji == 0 and g == 0:
                    e.wait_ge(C.ac.h, ac_xnT)
                guard_wait(e, pb)
                for dc in range(NCH):
                    lhsT = wst[ji % 2][:, dc * P:(dc + 1) * P]
                    rhs = big0[:, dc * 1536 + off + g * 512: dc * 1536 + off + (g + 1) * 512]
                    ins_ = e.matmul(PS[:, pb * 512:(pb + 1) * 512], lhsT, rhs,
                                    start=(dc == 0), stop=(dc == NCH - 1))
                C.pe.inc(ins_)
            pn = C.pe.n

            if tag == "v":
                @block.scalar
                def _(e, pn=pn, pb=pb, g=g, oc=oc):
                    e.wait_ge(C.pe.h, pn)
                    C.ac.inc(e.copy(out=vT_sb[oc % 2][:, g * 512:(g + 1) * 512],
                                    in_=PS[:, pb * 512:(pb + 1) * 512]))
                bank_guard[pb] = ("ac", C.ac.n)
            else:
                sb2 = 6 + (it % 2)
                ropehist = rope_gp_hist
                gguard = ropehist[-2] if len(ropehist) >= 2 else 0
                @block.scalar
                def _(e, pn=pn, pb=pb, it=it, gguard=gguard):
                    e.wait_ge(C.pe.h, pn)
                    if gguard:
                        e.wait_ge(C.gp.h, gguard)
                    C.ac.inc(e.copy(out=qsb[it % 2], in_=PS[:, pb * 512:(pb + 1) * 512]))
                an = C.ac.n
                bank_guard[pb] = ("ac", an)

                @block.tensor
                def _(e, an=an, sb2=sb2, it=it):
                    e.wait_ge(C.ac.h, an)
                    guard_wait(e, sb2)
                    C.pe.inc(e.matmul(PS[:, sb2 * 512:(sb2 + 1) * 512], pswapb[:],
                                      qsb[it % 2], start=True, stop=True))
                pn2 = C.pe.n

                cs_off = off + g * 512
                @block.vector
                def _(e, pn2=pn2, sb2=sb2, it=it, cs_off=cs_off, gguard=gguard):
                    e.wait_ge(C.pe.h, pn2)
                    if gguard:
                        e.wait_ge(C.gp.h, gguard)
                    C.dv.inc(e.tensor_tensor(out=rsw[it % 2],
                                             in0=PS[:, sb2 * 512:(sb2 + 1) * 512],
                                             in1=snf[:, cs_off:cs_off + 512], op=ALU.mult))
                dvn = C.dv.n
                bank_guard[sb2] = ("dv", dvn)

                if tag == "q":
                    out = qasm[oc % 2][:, g * 512:(g + 1) * 512]
                else:
                    out = kT[:, oc * 1536 + g * 512: oc * 1536 + (g + 1) * 512]
                qg = qasm_spill[oc % 2] if tag == "q" else 0
                @block.gpsimd
                def _(e, it=it, cs_off=cs_off, dvn=dvn, out=out, an=an, qg=qg):
                    e.wait_ge(C.ac.h, an)
                    if qg:
                        e.wait_ge(C.ds.h, qg)
                    e.tensor_tensor(out=rcs[it % 2], in0=qsb[it % 2],
                                    in1=csf[:, cs_off:cs_off + 512], op=ALU.mult)
                    e.wait_ge(C.dv.h, dvn)
                    C.gp.inc(e.tensor_tensor(out=out, in0=rcs[it % 2],
                                             in1=rsw[it % 2], op=ALU.add))
                rope_gp_hist.append(C.gp.n)

        if tag == "v":
            an_v = C.ac.n
            @block.tensor
            def _(e, an_v=an_v, oc=oc):
                e.wait_ge(C.ac.h, an_v)
                guard_wait(e, 2)
                guard_wait(e, 3)
                for lt in range(LT):
                    bb = 2 + lt // 8
                    view = PS[:, bb * 512:(bb + 1) * 512].bitcast(BF16)[:, (lt % 8) * P:(lt % 8 + 1) * P]
                    ins_ = e.transpose(view, vT_sb[oc % 2][:, lt * P:(lt + 1) * P], identb[:])
                C.pe.inc(ins_)
            pnv = C.pe.n

            @block.scalar
            def _(e, pnv=pnv, oc=oc):
                e.wait_ge(C.pe.h, pnv)
                out = vtok[:].rearrange("p (t n) -> p t n", t=LT)[:, 0:8, oc * P:(oc + 1) * P]
                C.ac.inc(e.copy(out=out, in_=PS[:, 2 * 512:3 * 512].bitcast(BF16).rearrange(
                    "p (t n) -> p t n", t=8)))
                out2 = vtok[:].rearrange("p (t n) -> p t n", t=LT)[:, 8:12, oc * P:(oc + 1) * P]
                C.ac.inc(e.copy(out=out2, in_=PS[:, 3 * 512:3 * 512 + 512].bitcast(BF16)[:, 0:4 * P]
                                .rearrange("p (t n) -> p t n", t=4)))
            bank_guard[2] = ("ac", C.ac.n)
            bank_guard[3] = ("ac", C.ac.n)
        elif tag == "q":
            gn = C.gp.n
            @block.sync
            def _(e, gn=gn, oc=oc):
                e.wait_ge(C.gp.h, gn)
                C.dma(e, "ds", out=qT_d[oc], in_=qasm[oc % 2])
            qasm_spill[oc % 2] = C.ds.n

        if ji + 2 < len(jobs):
            tag2, wname2, oc2, _, _ = jobs[ji + 2]
            pn_free = C.pe.n
            @block.gpsimd
            def _(e, pn_free=pn_free, wname2=wname2, oc2=oc2, ji=ji):
                e.wait_ge(C.pe.h, pn_free)
                stream_ld.append(C.dma(e, "dg", out=wst[ji % 2], in_=ins[wname2][oc2]))
        if C.am is not None:
            i = C.am["st"]["i"]
            absmean_step(C, xhalo[i % 2][:])

    if L0_STAGE < 1.5 and "dbgk" in C.ins:
        ac_all0 = C.ac.n
        @block.gpsimd
        def _(e):
            e.wait_ge(C.ac.h, ac_all0)
            C.dma(e, "dg", out=C.ins["dbgq"].rearrange("p (c n) -> p c n", c=NCH),
                  in_=big0[:].rearrange("p (c n) -> p c n", c=NCH)[:, :, 0:TOK])
            C.dma(e, "dg", out=C.ins["dbgk"], in_=big0[:, 0:4 * 1536])
    if L0_STAGE < 2 and "dbgk" in C.ins:
        gn_all = C.gp.n; ac_all = C.ac.n
        @block.gpsimd
        def _(e):
            e.wait_ge(C.gp.h, gn_all)
            e.wait_ge(C.ac.h, ac_all)
            C.dma(e, "dg", out=C.ins["dbgk"], in_=kT[:])
            C.dma(e, "dg", out=C.ins["dbgv"], in_=vtok[:])
            e.wait_ge(C.ds.h, C.ds.n)
            C.dma(e, "dg", out=C.ins["dbgq"].rearrange("p (c n) -> p c n", c=NCH), in_=C.scr["l0_qT"].rearrange("c p n -> p c n"))
    C.barrier()
    C.pop_scope()
    if L0_STAGE < 2:
        return

    C.push_scope()
    aoT = C.sbuf("l0aoT", [P, NCH * TOK], BF16)
    qt_sb = dbuf(C, "l0qt", NCH * P, BF16)
    masks = dbuf(C, "l0masks", KW, F32)
    ssc = dbuf(C, "l0ssc", KW, F32)
    sat = dbuf(C, "l0sat", KW, BF16)
    satT = dbuf(C, "l0satT", KW, BF16)
    oev = dbuf(C, "l0oev", 1024, BF16)
    wst2 = dbuf(C, "l0wst2", NCH * P, BF16)
    nmax = dbuf(C, "l0nmax", 1, F32)
    ssum = dbuf(C, "l0ssum", 1, F32)
    srec = dbuf(C, "l0srec", 1, F32)

    qt_ld = []
    mk_ld = []
    @block.scalar
    def _(e):
        for t in range(2):
            qt_ld.append(C.dma(e, "da", 
                out=qt_sb[t % 2].rearrange("p (c n) -> p c n", c=NCH),
                in_=C.scr["l0_qT"].rearrange("c p n -> p c n")[:, :, t * P:(t + 1) * P]))
    @block.sync
    def _(e):
        for t in range(2):
            mk_ld.append(C.dma(e, "ds", out=masks[t % 2],
                                              in_=ins["mask0"][t]))

    if C.am is not None:
        absmean_finish_a(C)
    am_sc = C.am["ac_sc"] if C.am is not None else 0

    bank_guard2 = {b: ("ac", 0) for b in range(8)}
    ac_exp_hist = []
    gp_nrm_hist = []
    pe_trans_hist = []
    def guard2(e, b):
        kind, n = bank_guard2[b]
        if n:
            e.wait_ge({"ac": C.ac, "dv": C.dv, "gp": C.gp}[kind].h, n)

    p0_pend = {}

    def p0_A(t, h):
        it = t * 16 + h
        sb = (it % 2) * 2
        @block.tensor
        def _(e, t=t, h=h, sb=sb, qn=qt_ld[t], mn=mk_ld[t], it=it):
            e.wait_ge(C.da.h, qn)
            if it == 0 and am_sc:
                e.wait_ge(C.ac.h, am_sc)
            guard2(e, sb)
            guard2(e, sb + 1)
            lhsT = qt_sb[t % 2][:, h * P:(h + 1) * P]
            kvh = h // 4
            base = kvh * 1536 + t * P
            e.matmul(PS[:, sb * 512:(sb + 1) * 512], lhsT, kT[:, base:base + 512],
                     start=True, stop=True)
            C.pe.inc(e.matmul(PS[:, (sb + 1) * 512:(sb + 1) * 512 + P], lhsT,
                              kT[:, base + 512:base + 640], start=True, stop=True))
        pn = C.pe.n

        acg = ac_exp_hist[it - 2] if it >= 2 else 0
        gng = gp_nrm_hist[it - 2] if it >= 2 else 0
        @block.vector
        def _(e, t=t, it=it, sb=sb, pn=pn, mn=mk_ld[t], acg=acg, gng=gng):
            e.wait_ge(C.pe.h, pn)
            e.wait_ge(C.ds.h, mn)
            if acg:
                e.wait_ge(C.ac.h, acg)
            if gng:
                e.wait_ge(C.dv.h, gng)
            C.dv.inc(e.scalar_tensor_tensor(out=ssc[it % 2], in0=PS[:, sb * 512: sb * 512 + KW],
                                   scalar=INV_SQRT_HD, in1=masks[t % 2],
                                   op0=ALU.mult, op1=ALU.add))
            e.wait_ge(C.dv.h, C.dv.n)
            C.dv.inc(e.reduce_max(nmax[it % 2], ssc[it % 2], axis=AX.X, negate=True))
        dvn = C.dv.n
        bank_guard2[sb] = ("dv", dvn)
        bank_guard2[sb + 1] = ("dv", dvn)

        peg = pe_trans_hist[it - 2] if it >= 2 else 0
        @block.scalar
        def _(e, it=it, dvn=dvn, peg=peg):
            e.wait_ge(C.dv.h, dvn)
            if peg:
                e.wait_ge(C.pe.h, peg)
            C.ac.inc(e.activation(out=sat[it % 2], in_=ssc[it % 2], func=AF.Exp,
                                  bias=nmax[it % 2], scale=1.0, accum_out=ssum[it % 2]))
        acn = C.ac.n
        ac_exp_hist.append(acn)

        @block.vector
        def _(e, acn=acn, it=it):
            e.wait_ge(C.ac.h, acn)
            C.dv.inc(e.reciprocal(srec[it % 2], ssum[it % 2]))

        @block.vector
        def _(e, it=it):
            C.dv.inc(e.tensor_scalar(out=sat[it % 2], in0=sat[it % 2],
                                     scalar1=srec[it % 2], scalar2=None, op0=ALU.mult))
            e.wait_ge(C.dv.h, C.dv.n)
        gn = C.dv.n
        gp_nrm_hist.append(gn)
        p0_pend[it] = (pn, dvn, gn)

    def p0_B(t, h):
        it = t * 16 + h
        atb = 4 + it % 2
        avb = 6 + (it // 4) % 2
        pn, dvn, gn = p0_pend[it]

        @block.tensor
        def _(e, it=it, atb=atb, gn=gn):
            e.wait_ge(C.dv.h, gn)
            guard2(e, atb)
            for kc in range(5):
                view = PS[:, atb * 512:(atb + 1) * 512].bitcast(BF16)[:, kc * P:(kc + 1) * P]
                ins_ = e.transpose(view, sat[it % 2][:, kc * P:(kc + 1) * P], identb[:])
            C.pe.inc(ins_)
        pn2 = C.pe.n
        pe_trans_hist.append(pn2)

        @block.vector
        def _(e, it=it, atb=atb, pn2=pn2):
            e.wait_ge(C.pe.h, pn2)
            C.dv.inc(e.tensor_copy(satT[it % 2],
                                   PS[:, atb * 512:(atb + 1) * 512].bitcast(BF16)[:, 0:KW]))
        bank_guard2[atb] = ("dv", C.dv.n)
        an = C.dv.n

        @block.tensor
        def _(e, t=t, h=h, it=it, avb=avb, an=an):
            e.wait_ge(C.dv.h, an)
            if h % 4 == 0:
                guard2(e, avb)
            kvh = h // 4
            for kc in range(5):
                lhsT = vtok[:, (t + kc) * 512 + kvh * P: (t + kc) * 512 + (kvh + 1) * P]
                ins_ = e.matmul(PS[:, avb * 512 + (h % 4) * P: avb * 512 + (h % 4 + 1) * P],
                                lhsT, satT[it % 2][:, kc * P:(kc + 1) * P],
                                start=(kc == 0), stop=(kc == 4))
            C.pe.inc(ins_)
        pn3 = C.pe.n

        if h % 4 == 3:
            @block.vector
            def _(e, t=t, h=h, avb=avb, pn3=pn3):
                e.wait_ge(C.pe.h, pn3)
                hb = h - 3
                out = aoT[:].rearrange("p (c n) -> p c n", c=NCH)[:, hb:hb + 4, t * P:(t + 1) * P]
                C.dv.inc(e.tensor_copy(out, PS[:, avb * 512:(avb + 1) * 512].rearrange(
                    "p (c n) -> p c n", c=4)))
            bank_guard2[avb] = ("dv", C.dv.n)
        if h == 15 and t + 2 <= NT - 1:
            @block.scalar
            def _(e, t=t, pn=pn, dvn=dvn):
                e.wait_ge(C.pe.h, pn)
                qt_ld.append(C.dma(e, "da",
                    out=qt_sb[t % 2].rearrange("p (c n) -> p c n", c=NCH),
                    in_=C.scr["l0_qT"].rearrange("c p n -> p c n")[:, :, (t + 2) * P:(t + 3) * P]))
            @block.sync
            def _(e, t=t, dvn=dvn):
                e.wait_ge(C.dv.h, dvn)
                mk_ld.append(C.dma(e, "ds", out=masks[t % 2], in_=ins["mask0"][t + 2]))

    its = [(t, h) for t in range(NT) for h in range(16)]
    L0_PIPE = 1
    for i in range(len(its) + L0_PIPE):
        if i < len(its):
            p0_A(*its[i])
        if i >= L0_PIPE:
            p0_B(*its[i - L0_PIPE])
    ac_attn = C.ac.n
    dv_attn = C.dv.n
    wo_ld = []
    @block.gpsimd
    def _(e):
        e.wait_ge(C.dv.h, dv_attn)
        for i in range(2):
            wo_ld.append(C.dma(e, "dg", out=wst2[i % 2], in_=ins["wo_s"][i]))

    bg = {b: ("ac", 0) for b in range(8)}
    def guard3(e, b):
        kind, n = bg[b]
        if n:
            e.wait_ge({"ac": C.ac, "dv": C.dv}[kind].h, n)

    for oc in range(NCH):
        need = wo_ld[oc]
        pb = (oc % 2) * 2
        tb = 4 + (oc % 2)
        @block.tensor
        def _(e, oc=oc, need=need, pb=pb):
            e.wait_ge(C.dg.h, need)
            if oc == 0:
                e.wait_ge(C.dv.h, dv_attn)
            guard3(e, pb)
            guard3(e, pb + 1)
            for dc in range(NCH):
                lhsT = wst2[oc % 2][:, dc * P:(dc + 1) * P]
                for g in range(2):
                    rhs = aoT[:, dc * TOK + g * 512: dc * TOK + (g + 1) * 512]
                    ins_ = e.matmul(PS[:, (pb + g) * 512:(pb + g + 1) * 512], lhsT, rhs,
                                    start=(dc == 0), stop=(dc == NCH - 1))
            C.pe.inc(ins_)
        pn = C.pe.n
        if oc + 2 < NCH:
            @block.gpsimd
            def _(e, oc=oc, pn=pn):
                e.wait_ge(C.pe.h, pn)
                wo_ld.append(C.dma(e, "dg", out=wst2[oc % 2], in_=ins["wo_s"][oc + 2]))

        @block.scalar
        def _(e, oc=oc, pb=pb, pn=pn):
            e.wait_ge(C.pe.h, pn)
            C.ac.inc(e.copy(out=oev[oc % 2], in_=PS[:, pb * 512: pb * 512 + 1024]))
        bg[pb] = ("ac", C.ac.n)
        bg[pb + 1] = ("ac", C.ac.n)
        an = C.ac.n

        if L0_STAGE < 2.8:
            bg[pb] = ("ac", C.ac.n)
            continue
        @block.tensor
        def _(e, oc=oc, tb=tb, an=an):
            e.wait_ge(C.ac.h, an)
            guard3(e, tb)
            for tt in range(NT):
                view = PS[:, tb * 512:(tb + 1) * 512].bitcast(BF16)[:, tt * P:(tt + 1) * P]
                ins_ = e.transpose(view, oev[oc % 2][:, tt * P:(tt + 1) * P], identb[:])
            C.pe.inc(ins_)
        pn2 = C.pe.n

        @block.vector
        def _(e, oc=oc, tb=tb, pn2=pn2):
            e.wait_ge(C.pe.h, pn2)
            out = X[:].rearrange("p (t n) -> p t n", t=NT)[:, :, oc * P:(oc + 1) * P]
            src = PS[:, tb * 512:(tb + 1) * 512].bitcast(BF16).rearrange("p (t n) -> p t n", t=NT)
            if L0_STAGE < 2.9:
                C.dv.inc(e.tensor_copy(scr["xtmp"][:, 0:TOK], src.rearrange("p t n -> p (t n)")))
            else:
                C.dv.inc(e.tensor_tensor(out=out, in0=out, in1=src, op=ALU.add))
        bg[tb] = ("dv", C.dv.n)

    C.barrier()
    C.pop_scope()
    C.pop_scope()



def emit_moe(C, X, PS):
    nc, block = C.nc, C.block
    ins = C.ins
    C.push_scope()

    XNT1 = C.sbuf("m_xnT", [P, NCH * TOK], BF16)
    xnf = C.sbuf("m_xnf", [P, D], F32)
    identb = C.sbuf("m_ident", [P, P], BF16)
    identf = C.sbuf("m_identf", [P, P], F32)
    gwt = C.sbuf("m_gwt", [P, NCH * 32], F32)
    gwh = C.sbuf("m_gwh", [P, NCH * 32], BF16)
    gwl = C.sbuf("m_gwl", [P, NCH * 32], BF16)
    lg = C.sbuf("m_lg", [P, 32], F32)
    cur = C.sbuf("m_cur", [P, 32], F32)
    pex = C.sbuf("m_pex", [P, 32], F32)
    tbig = C.sbuf("m_tbig", [P, 32], F32)
    mi = C.sbuf("m_mi", [P, 1], F32)
    m1n = C.sbuf("m_m1n", [P, 1], F32)
    ssum = C.sbuf("m_ssum", [P, 1], F32)
    srec = C.sbuf("m_srec", [P, 1], F32)
    mw_all = C.sbuf("m_mw", [P, NT * 32], F32)
    mwT = C.sbuf("m_mwT", [32, NT * P], BF16)
    wg_sb = C.sbuf("m_wg", [P, NCH * 768], BF16)
    wu_sb = C.sbuf("m_wu", [P, NCH * 768], BF16)
    wd_sb = C.sbuf("m_wd", [P, 6 * D], BF16)
    sg = C.sbuf("m_sg", [P, 6 * 512], BF16)
    uev = C.sbuf("m_uev", [P, 6 * 512], BF16)
    hT = C.sbuf("m_hT", [P, 6 * 512], BF16)
    wb = C.sbuf("m_wb", [P, TOK], BF16)
    xnb = [sg[:, 0:D], sg[:, 0:D]]
    xnlo = hT[:, 0:D]
    xnloT = wd_sb[:, D:2 * D]
    scr2 = mk_ln_scr(C, "m_ln", dump_ap=wd_sb[:, 0:D], need_xtmp=False)
    mw_d = C.scr["moe_mw"]

    @block.gpsimd
    def _(e):
        C.dma(e, "dg", out=identb[:], in_=ins["ident"])
        C.dma(e, "dg", out=gwh[:], in_=ins["gwt"])
    @block.sync
    def _(e):
        C.dma(e, "ds", out=gwt[:], in_=ins["gwt"])
        C.dma(e, "ds", out=identf[:], in_=ins["ident"])
    dsn0 = C.ds.n
    dgn0 = C.dg.n

    @block.vector
    def _(e):
        e.wait_ge(C.ds.h, dsn0)
        e.wait_ge(C.dg.h, dgn0)
        C.dv.inc(e.tensor_tensor(out=gwl[:], in0=gwt[:], in1=gwh[:], op=ALU.subtract))
    dv_gw = C.dv.n

    rdy_hist = []
    pe_tr = []
    ac_ev = []
    pe_lo = [0] * NT
    ac_lgev = 0
    ac_mwT = 0
    dv_mw_prev = 0
    for t in range(NT):
        pre = []
        if t >= 2:
            k2, n2 = rdy_hist[t - 2]
            pre.append((C.dv, n2))
            pv = pe_tr[t - 2]
            @block.vector
            def _(e, pv=pv):
                e.wait_ge(C.pe.h, pv)
        kind, n = emit_ln_tile(C, X[:, t * D:(t + 1) * D], xnf[:], scr2[t % 2],
                               pre_wait=pre)
        rdy_hist.append((kind, n))
        plo = pe_lo[t - 1] if t >= 1 else 0
        @block.vector
        def _(e, t=t, plo=plo):
            if plo:
                e.wait_ge(C.pe.h, plo)
            C.dv.inc(e.tensor_copy(xnb[t % 2], xnf[:]))
            e.wait_ge(C.dv.h, C.dv.n)
            C.dv.inc(e.tensor_tensor(out=xnlo, in0=xnf[:], in1=xnb[t % 2], op=ALU.subtract))
        dv_hilo = C.dv.n

        prev_ev = ac_ev[t - 1] if t >= 1 else 0
        @block.tensor
        def _(e, t=t, dv_hilo=dv_hilo, prev_ev=prev_ev):
            e.wait_ge(C.dv.h, dv_hilo)
            e.wait_ge(C.ac.h, prev_ev)
            for dc in range(NCH):
                bb = dc // 8
                view = PS[:, bb * 512:(bb + 1) * 512].bitcast(BF16)[:, (dc % 8) * P:(dc % 8 + 1) * P]
                ins_ = e.transpose(view, xnb[t % 2][:, dc * P:(dc + 1) * P], identb[:])
            C.pe.inc(ins_)
            for dc in range(NCH):
                bb = 2 + dc // 8
                view = PS[:, bb * 512:(bb + 1) * 512].bitcast(BF16)[:, (dc % 8) * P:(dc % 8 + 1) * P]
                ins_ = e.transpose(view, xnlo[:, dc * P:(dc + 1) * P], identb[:])
            C.pe.inc(ins_)
        pe_tr.append(C.pe.n - 1)
        pe_lo[t] = C.pe.n
        pn = C.pe.n

        @block.scalar
        def _(e, t=t, pn=pn):
            e.wait_ge(C.pe.h, pn)
            for half in range(2):
                out = XNT1[:].rearrange("p (c n) -> p c n", c=NCH)[:, half * 8:(half + 1) * 8,
                                                                   t * P:(t + 1) * P]
                i2 = PS[:, half * 512:(half + 1) * 512].bitcast(BF16).rearrange("p (c n) -> p c n", c=8)
                ins_ = e.copy(out=out, in_=i2)
            for half in range(2):
                out = xnloT.rearrange("p (c n) -> p c n", c=NCH)[:, half * 8:(half + 1) * 8, :]
                i2 = PS[:, (2 + half) * 512:(3 + half) * 512].bitcast(BF16).rearrange("p (c n) -> p c n", c=8)
                ins_ = e.copy(out=out, in_=i2)
            C.ac.inc(ins_)
        ac_ev.append(C.ac.n)
        an_t = C.ac.n

        @block.tensor
        def _(e, t=t, an_t=an_t, dv_gw=dv_gw, ac_lgev=ac_lgev):
            e.wait_ge(C.ac.h, an_t)
            if t == 0:
                e.wait_ge(C.dv.h, dv_gw)
            if ac_lgev:
                e.wait_ge(C.ac.h, ac_lgev)
            combos = [(0, gwh), (0, gwl), (1, gwh)]
            nmm = len(combos) * NCH
            k = 0
            for (which, gw) in combos:
                for dc in range(NCH):
                    if which == 0:
                        lhsT = XNT1[:, dc * TOK + t * P: dc * TOK + (t + 1) * P]
                    else:
                        lhsT = xnloT[:, dc * P:(dc + 1) * P]
                    ins_ = e.matmul(PS[:, 4 * 512:4 * 512 + 32], lhsT, gw[:, dc * 32:(dc + 1) * 32],
                                    start=(k == 0), stop=(k == nmm - 1))
                    k += 1
            C.pe.inc(ins_)
        pn_r = C.pe.n

        @block.scalar
        def _(e, pn_r=pn_r, dvp=dv_mw_prev):
            e.wait_ge(C.pe.h, pn_r)
            if dvp:
                e.wait_ge(C.dv.h, dvp)
            C.ac.inc(e.copy(out=lg[:], in_=PS[:, 4 * 512:4 * 512 + 32]))
        ac_lgev = C.ac.n
        an_lg = C.ac.n

        def step(e, instr):
            C.dv.inc(instr)
            e.wait_ge(C.dv.h, C.dv.n)

        @block.vector
        def _(e, t=t, an_lg=an_lg):
            e.wait_ge(C.ac.h, an_lg)
            step(e, e.tensor_copy(cur[:], lg[:]))
            for i in range(8):
                step(e, e.reduce_max(mi[:], cur[:], axis=AX.X))
                if i == 0:
                    step(e, e.tensor_scalar(out=m1n[:], in0=mi[:], scalar1=-1.0, scalar2=None,
                                            op0=ALU.mult))
                if i < 7:
                    step(e, e.tensor_scalar(out=tbig[:], in0=cur[:], scalar1=mi[:],
                                            scalar2=1e30, op0=ALU.is_ge, op1=ALU.mult))
                    step(e, e.tensor_tensor(out=cur[:], in0=cur[:], in1=tbig[:], op=ALU.subtract))
        dv_t8 = C.dv.n

        @block.scalar
        def _(e, dv_t8=dv_t8):
            e.wait_ge(C.dv.h, dv_t8)
            C.ac.inc(e.activation(out=pex[:], in_=lg[:], func=AF.Exp, bias=m1n[:], scale=1.0))
        an_p = C.ac.n

        @block.vector
        def _(e, t=t, an_p=an_p):
            e.wait_ge(C.ac.h, an_p)
            C.dv.inc(e.scalar_tensor_tensor(out=mw_all[:, t * 32:(t + 1) * 32], in0=lg[:],
                                            scalar=mi[:], in1=pex[:], op0=ALU.is_ge,
                                            op1=ALU.mult, accum_out=ssum[:]))
            e.wait_ge(C.dv.h, C.dv.n)
            C.dv.inc(e.reciprocal(srec[:], ssum[:]))
            e.wait_ge(C.dv.h, C.dv.n)
            C.dv.inc(e.tensor_scalar(out=mw_all[:, t * 32:(t + 1) * 32],
                                     in0=mw_all[:, t * 32:(t + 1) * 32],
                                     scalar1=srec[:], scalar2=None, op0=ALU.mult))
            e.wait_ge(C.dv.h, C.dv.n)
        dv_mw = C.dv.n

        dv_mw_prev = dv_mw
        @block.tensor
        def _(e, t=t, dv_mw=dv_mw, ac_mwT=ac_mwT):
            e.wait_ge(C.dv.h, dv_mw)
            if ac_mwT:
                e.wait_ge(C.ac.h, ac_mwT)
            C.pe.inc(e.transpose(PS[0:32, 5 * 512:5 * 512 + P], mw_all[:, t * 32:(t + 1) * 32],
                                 identf[:]))
        pn_mw = C.pe.n

        @block.scalar
        def _(e, t=t, pn_mw=pn_mw):
            e.wait_ge(C.pe.h, pn_mw)
            C.ac.inc(e.copy(out=mwT[:, t * P:(t + 1) * P], in_=PS[0:32, 5 * 512:5 * 512 + P]))
        ac_mwT = C.ac.n

    @block.sync
    def _(e, ac_mwT=ac_mwT):
        e.wait_ge(C.ac.h, ac_mwT)
        C.dma(e, "ds", out=mw_d, in_=mwT[:])
    ds_mw = C.ds.n

    if MOE_STAGE < 2:
        if "dbg" in C.ins:
            @block.gpsimd
            def _(e, am=ac_mwT, ds_mw=ds_mw):
                e.wait_ge(C.ac.h, am)
                e.wait_ge(C.ds.h, ds_mw)
                C.dma(e, "dg", out=C.ins["dbg"], in_=XNT1[:])
                C.dma(e, "dg", out=C.ins["dbgk"].rearrange("p n -> p n")[0:32, 0:NT * P], in_=mwT[:])
        C.barrier()
        C.pop_scope()
        return

    wl = {}
    @block.gpsimd
    def _(e, ds_mw=ds_mw):
        e.wait_ge(C.ds.h, ds_mw)
        wl[("wg", 0)] = C.dma(e, "dg", out=wg_sb[:], in_=ins["wg_t"][0])
        wl[("wu", 0)] = C.dma(e, "dg", out=wu_sb[:], in_=ins["wu_t"][0])
        wl[("wd", 0)] = C.dma(e, "dg", out=wd_sb[:], in_=ins["wd_t"][0])
    wb_ld = {}
    @block.scalar
    def _(e):
        e.wait_ge(C.ds.h, ds_mw)
        wb_ld[0] = C.dma(e, "da", out=wb[:], in_=mw_d[0:1, :].partition_broadcast(P).squeeze(1))

    bank_dv = {b: 0 for b in range(8)}
    bank_ac = {b: 0 for b in range(6)}
    gp_hT = 0
    pe_g = {}
    pe_u = {}
    pe_dn = 0
    n_experts = 32 if MOE_STAGE >= 3 else 1
    for ei in range(n_experts):
        for blk in range(2):
            for mc in range(6):
                acfirst = ac_mwT if (ei == 0 and blk == 0) else 0
                @block.tensor
                def _(e, ei=ei, blk=blk, mc=mc, need=wl[("wg", ei)], g1=bank_ac[mc],
                      gh=gp_hT, acfirst=acfirst):
                    e.wait_ge(C.dg.h, need)
                    if acfirst:
                        e.wait_ge(C.ac.h, acfirst)
                    if g1:
                        e.wait_ge(C.ac.h, g1)
                    for dc in range(NCH):
                        lhsT = wg_sb[:, dc * 768 + mc * P: dc * 768 + (mc + 1) * P]
                        rhs = XNT1[:, dc * TOK + blk * 512: dc * TOK + (blk + 1) * 512]
                        ins_ = e.matmul(PS[:, mc * 512:(mc + 1) * 512], lhsT, rhs,
                                        start=(dc == 0), stop=(dc == NCH - 1))
                    C.pe.inc(ins_)
                pe_g[mc] = C.pe.n
                @block.scalar
                def _(e, mc=mc, pn=pe_g[mc], gh=gp_hT):
                    e.wait_ge(C.pe.h, pn)
                    if gh:
                        e.wait_ge(C.gp.h, gh)
                    C.ac.inc(e.activation(out=sg[:, mc * 512:(mc + 1) * 512],
                                          in_=PS[:, mc * 512:(mc + 1) * 512], func=AF.Silu))
                bank_ac[mc] = C.ac.n
            for mc in range(6):
                @block.tensor
                def _(e, ei=ei, blk=blk, mc=mc, need=wl[("wu", ei)], ga=bank_ac[mc]):
                    e.wait_ge(C.dg.h, need)
                    e.wait_ge(C.ac.h, ga)
                    for dc in range(NCH):
                        lhsT = wu_sb[:, dc * 768 + mc * P: dc * 768 + (mc + 1) * P]
                        rhs = XNT1[:, dc * TOK + blk * 512: dc * TOK + (blk + 1) * 512]
                        ins_ = e.matmul(PS[:, mc * 512:(mc + 1) * 512], lhsT, rhs,
                                        start=(dc == 0), stop=(dc == NCH - 1))
                    C.pe.inc(ins_)
                pe_u[mc] = C.pe.n
                @block.scalar
                def _(e, mc=mc, pn=pe_u[mc], gh=gp_hT):
                    e.wait_ge(C.pe.h, pn)
                    if gh:
                        e.wait_ge(C.gp.h, gh)
                    C.ac.inc(e.copy(out=uev[:, mc * 512:(mc + 1) * 512],
                                    in_=PS[:, mc * 512:(mc + 1) * 512]))
                bank_ac[mc] = C.ac.n
                bank_dv[mc] = 0
            pn_u = C.pe.n
            ac_uev = C.ac.n
            @block.vector
            def _(e, ac_uev=ac_uev):
                e.wait_ge(C.ac.h, ac_uev)
                C.dv.inc(e.tensor_tensor(out=sg[:], in0=sg[:], in1=uev[:], op=ALU.mult))
            dv_h1 = C.dv.n

            wbn = wb_ld[ei]
            @block.gpsimd
            def _(e, dv_h1=dv_h1, blk=blk, wbn=wbn, pdn=pe_dn):
                e.wait_ge(C.dv.h, dv_h1)
                e.wait_ge(C.da.h, wbn)
                if pdn:
                    e.wait_ge(C.pe.h, pdn)
                C.gp.inc(e.tensor_tensor(
                    out=hT[:].rearrange("p (c n) -> p c n", c=6),
                    in0=sg[:].rearrange("p (c n) -> p c n", c=6),
                    in1=wb[:, blk * 512:(blk + 1) * 512].unsqueeze(1).broadcast_to((P, 6, 512)),
                    op=ALU.mult))
            gp_hT = C.gp.n

            if blk == 1 and ei + 1 < n_experts:
                @block.gpsimd
                def _(e, ei=ei, pg=pe_u[5], pu=pe_u[5]):
                    e.wait_ge(C.pe.h, pu)
                    wl[("wg", ei + 1)] = C.dma(e, "dg", out=wg_sb[:], in_=ins["wg_t"][ei + 1])
                    wl[("wu", ei + 1)] = C.dma(e, "dg", out=wu_sb[:], in_=ins["wu_t"][ei + 1])
                @block.scalar
                def _(e, ei=ei, gh=gp_hT):
                    e.wait_ge(C.gp.h, gh)
                    wb_ld[ei + 1] = C.dma(e, "da", out=wb[:],
                                          in_=mw_d[ei + 1:ei + 2, :].partition_broadcast(P).squeeze(1))

            for ti, (s, n_) in enumerate([(s, n_) for s in range(4) for n_ in range(4)]):
                bk = 6 + ti % 2
                @block.tensor
                def _(e, ei=ei, blk=blk, s=s, n_=n_, bk=bk, gh=gp_hT, need=wl[("wd", ei)],
                      gd=bank_dv[bk]):
                    e.wait_ge(C.gp.h, gh)
                    e.wait_ge(C.dg.h, need)
                    if gd:
                        e.wait_ge(C.dv.h, gd)
                    for mc in range(6):
                        lhsT = hT[:, mc * 512 + s * P: mc * 512 + (s + 1) * P]
                        rhs = wd_sb[:, mc * D + n_ * 512: mc * D + (n_ + 1) * 512]
                        ins_ = e.matmul(PS[:, bk * 512:(bk + 1) * 512], lhsT, rhs,
                                        start=(mc == 0), stop=(mc == 5))
                    C.pe.inc(ins_)
                pn_t = C.pe.n

                @block.vector
                def _(e, blk=blk, s=s, n_=n_, bk=bk, pn_t=pn_t):
                    e.wait_ge(C.pe.h, pn_t)
                    xs = X[:, (blk * 4 + s) * D + n_ * 512: (blk * 4 + s) * D + (n_ + 1) * 512]
                    C.dv.inc(e.tensor_tensor(out=xs, in0=xs, in1=PS[:, bk * 512:(bk + 1) * 512],
                                             op=ALU.add))
                bank_dv[bk] = C.dv.n
            pe_dn = C.pe.n
            if blk == 1 and ei + 1 < n_experts:
                @block.gpsimd
                def _(e, ei=ei, pdn=pe_dn):
                    e.wait_ge(C.pe.h, pdn)
                    wl[("wd", ei + 1)] = C.dma(e, "dg", out=wd_sb[:], in_=ins["wd_t"][ei + 1])

    C.barrier()
    C.pop_scope()




def emit_l1attn(C, X, PS):
    nc, block = C.nc, C.block
    ins = C.ins
    C.push_scope()

    XNT2 = C.sbuf("a_xnT", [P, NCH * TOK], BF16)
    identb = C.sbuf("a_ident", [P, P], BF16)
    onesc = C.sbuf("a_onesc", [P, 1], BF16)
    onesr = C.sbuf("a_onesr", [1, P], BF16)
    onesrf = C.sbuf("a_onesrf", [1, P], F32)
    xn2 = dbuf(C, "a_xn2", D, BF16)
    wst = dbuf(C, "a_wst", NCH * P, BF16)
    pstage = dbuf(C, "a_pstage", TOK, BF16)
    slope_sb = C.sbuf("a_slope", [1, 16], F32)
    sbc = C.sbuf("a_sbc", [P, 16], F32)
    arq = C.sbuf("a_arq", [1, 256], F32)
    ark = C.sbuf("a_ark", [P, 2], F32)
    arkp = C.sbuf("a_arkp", [P, 2], F32)
    tcn = C.sbuf("a_tcn", [1, 8], F32)
    kdec = C.sbuf("a_kdec", [P, 32], F32)
    expk = C.sbuf("a_expk", [P, 32], F32)
    causal = C.sbuf("a_causal", [P, 512], BF16)
    bdh = C.sbuf("a_bdh", [P, 16], F32)
    tmp256 = C.sbuf("a_t256", [1, 256], F32)
    qdec_bc = C.sbuf("a_qdecbc", [P, 256], BF16)
    coefs = C.sbuf("a_coefs", [1, P], F32)
    coef_bc = C.sbuf("a_coefbc", [P, P], F32)
    kv = C.sbuf("a_kv", [P, 16 * P], F32)
    S_out = C.sbuf("a_Sout", [P, 16 * P], F32)
    S_in = C.sbuf("a_Sin", [P, 16 * P], F32)
    qblk = C.sbuf("a_qblk", [P, 16 * 256], BF16)
    kblk = C.sbuf("a_kblk", [P, 16 * 256], BF16)
    vblk = C.sbuf("a_vblk", [P, 16 * 256], BF16)
    gblk = C.sbuf("a_gblk", [P, 16 * 256], BF16)
    S_r = qblk[:].bitcast(F32)
    outb = C.sbuf("a_outb", [P, 16 * 256], BF16)
    prodT = outb
    ktok2p = dbuf(C, "a_ktok", 256, BF16)
    vtok2p = dbuf(C, "a_vtok", 256, BF16)
    kdki2p = dbuf(C, "a_kdki", 256, BF16)
    kvbf2p = dbuf(C, "a_kvbf", P, BF16)
    qkd2p = dbuf(C, "a_qkd", 512, BF16)
    ktok, vtok2, kdki, kvbf, qkd = (ktok2p[0], vtok2p[0], kdki2p[0], kvbf2p[0], qkd2p[0])
    sq_sb = C.sbuf("a_sq", [P, 256], BF16)
    sq_sb2 = C.sbuf("a_sq2", [P, 256], BF16)
    sqp = [sq_sb, sq_sb2]
    rr = C.sbuf("a_rr", [1, 256], F32)
    rstd_bc = C.sbuf("a_rstdbc", [P, 256], F32)
    normw = C.sbuf("a_normw", [P, 16], F32)
    wrow = C.sbuf("a_wrow", [P, D], BF16)
    wrow2 = C.sbuf("a_wrow2", [P, D], BF16)
    scr2 = mk_ln_scr(C, "a_ln", need_xtmp=False)

    qd_d, kd_d, vd_d, gd_d = (C.scr["l1_q"], C.scr["l1_k"], C.scr["l1_v"], C.scr["l1_g"])
    wot_bf = C.scr["l1_wotbf"]
    ccin, ccag = C.scr["l1_ccin"], C.scr["l1_ccag"]

    @block.gpsimd
    def _(e):
        C.dma(e, "dg", out=identb[:], in_=ins["ident"])
        C.dma(e, "dg", out=onesc[:], in_=ins["ones_col"])
        C.dma(e, "dg", out=onesr[:], in_=ins["ones_row"])
        C.dma(e, "dg", out=causal[:], in_=ins["causal01"])
    @block.sync
    def _(e):
        C.dma(e, "ds", out=slope_sb[:], in_=ins["slope"])
        C.dma(e, "ds", out=onesrf[:], in_=ins["ones_row"])
        C.dma(e, "ds", out=arq[:], in_=ins["arqneg"])
        C.dma(e, "ds", out=ark[:], in_=ins["arkneg"])
        C.dma(e, "ds", out=arkp[:], in_=ins["arkpos"])
        C.dma(e, "ds", out=tcn[:], in_=ins["tcneg"])
        C.dma(e, "ds", out=normw[:], in_=ins["normw_col"])
    dsn0 = C.ds.n
    dgn0 = C.dg.n

    if C.am is not None:
        absmean_finish_b(C, onesrf[:], dsn0)

    @block.tensor
    def _(e):
        e.wait_ge(C.ds.h, dsn0)
        e.wait_ge(C.dg.h, dgn0)
        if C.am is not None:
            e.wait_ge(C.ac.h, C.am["ac_t"])
        C.pe.inc(e.matmul(PS[:, 0:16], onesrf[:], slope_sb[:], start=True, stop=True))
    pn0 = C.pe.n
    @block.scalar
    def _(e, pn0=pn0):
        e.wait_ge(C.pe.h, pn0)
        C.ac.inc(e.copy(out=sbc[:], in_=PS[:, 0:16]))
    ac_sbc = C.ac.n

    for h in range(16):
        @block.vector
        def _(e, h=h, ac_sbc=ac_sbc):
            if h == 0:
                e.wait_ge(C.ac.h, ac_sbc)
            C.dv.inc(e.tensor_scalar(out=kdec[:, h * 2:(h + 1) * 2], in0=ark[:],
                                     scalar1=sbc[:, h:h + 1], scalar2=None, op0=ALU.mult))
            e.wait_ge(C.dv.h, C.dv.n)
            C.dv.inc(e.tensor_scalar(out=expk[:, h * 2:(h + 1) * 2], in0=arkp[:],
                                     scalar1=sbc[:, h:h + 1], scalar2=None, op0=ALU.mult))
            e.wait_ge(C.dv.h, C.dv.n)
            C.dv.inc(e.tensor_scalar(out=bdh[:, h:h + 1], in0=sbc[:, h:h + 1], scalar1=-256.0,
                                     scalar2=None, op0=ALU.mult))
            e.wait_ge(C.dv.h, C.dv.n)
        dvh = C.dv.n
        @block.scalar
        def _(e, h=h, dvh=dvh):
            e.wait_ge(C.dv.h, dvh)
            e.activation(out=kdec[:, h * 2:(h + 1) * 2], in_=kdec[:, h * 2:(h + 1) * 2], func=AF.Exp)
            e.activation(out=expk[:, h * 2:(h + 1) * 2], in_=expk[:, h * 2:(h + 1) * 2], func=AF.Exp)
            C.ac.inc(e.activation(out=bdh[:, h:h + 1], in_=bdh[:, h:h + 1], func=AF.Exp))
    ac_tab = C.ac.n

    @block.vector
    def _(e, ac_tab=ac_tab):
        e.wait_ge(C.ac.h, ac_tab)
        for h in range(16):
            ins_ = e.tensor_scalar(out=coefs[0:1, h * 8:(h + 1) * 8], in0=tcn[:],
                                   scalar1=slope_sb[0:1, h:h + 1], scalar2=None, op0=ALU.mult)
        C.dv.inc(ins_)
    dv_cf = C.dv.n
    @block.scalar
    def _(e, dv_cf=dv_cf):
        e.wait_ge(C.dv.h, dv_cf)
        C.ac.inc(e.activation(out=coefs[:], in_=coefs[:], func=AF.Exp))
    ac_cf = C.ac.n
    @block.tensor
    def _(e, ac_cf=ac_cf):
        e.wait_ge(C.ac.h, ac_cf)
        C.pe.inc(e.matmul(PS[:, 512:640], onesrf[:], coefs[:], start=True, stop=True))
    pn_cf = C.pe.n
    @block.scalar
    def _(e, pn_cf=pn_cf):
        e.wait_ge(C.pe.h, pn_cf)
        C.ac.inc(e.copy(out=coef_bc[:], in_=PS[:, 512:640]))
    ac_cfb = C.ac.n

    @block.vector
    def _(e):
        e.memset(S_out[:], 0.0)
        C.dv.inc(e.memset(S_in[:], 0.0))
    dv_z = C.dv.n

    wot_ld0 = {}
    wot_sp0 = {}

    def emit_precast(oc):
        @block.gpsimd
        def _(e, oc=oc):
            if oc >= 2:
                e.wait_ge(C.ds.h, wot_sp0[oc - 2])
            wot_ld0[oc] = C.dma(e, "dg", out=wrow[:] if oc % 2 == 0 else wrow2[:],
                                in_=ins["wot_r"][oc])
        @block.sync
        def _(e, oc=oc):
            e.wait_ge(C.dg.h, wot_ld0[oc])
            C.dma(e, "ds", out=wot_bf[oc], in_=wrow[:] if oc % 2 == 0 else wrow2[:])
            wot_sp0[oc] = C.ds.n

    rdy_hist = []
    pe_tr = []
    ac_ev = []
    for t in range(NT):
        pre = []
        if t >= 2:
            k2, n2 = rdy_hist[t - 2]
            pre.append((C.dv, n2))
            pv = pe_tr[t - 2]
            @block.vector
            def _(e, pv=pv):
                e.wait_ge(C.pe.h, pv)
        kind, n = emit_ln_tile(C, X[:, t * D:(t + 1) * D], xn2[t % 2], scr2[t % 2],
                               pre_wait=pre)
        rdy_hist.append((kind, n))
        prev_ev = ac_ev[t - 1] if t >= 1 else 0
        @block.tensor
        def _(e, t=t, n=n, prev_ev=prev_ev):
            e.wait_ge(C.dv.h, n)
            e.wait_ge(C.ac.h, prev_ev)
            for dc in range(NCH):
                bb = 2 + dc // 8
                view = PS[:, bb * 512:(bb + 1) * 512].bitcast(BF16)[:, (dc % 8) * P:(dc % 8 + 1) * P]
                ins_ = e.transpose(view, xn2[t % 2][:, dc * P:(dc + 1) * P], identb[:])
            C.pe.inc(ins_)
        pe_tr.append(C.pe.n)
        pn = C.pe.n
        @block.scalar
        def _(e, t=t, pn=pn):
            e.wait_ge(C.pe.h, pn)
            for half in range(2):
                out = XNT2[:].rearrange("p (c n) -> p c n", c=NCH)[:, half * 8:(half + 1) * 8,
                                                                   t * P:(t + 1) * P]
                i2 = PS[:, (2 + half) * 512:(3 + half) * 512].bitcast(BF16).rearrange(
                    "p (c n) -> p c n", c=8)
                ins_ = e.copy(out=out, in_=i2)
            C.ac.inc(ins_)
        ac_ev.append(C.ac.n)
    ac_xnT = C.ac.n

    wjobs = [("wqkv_s", oc, qd_d if oc < 16 else (kd_d if oc < 32 else vd_d),
              oc % 16, AF.Silu) for oc in range(48)]
    wjobs += [("wgate_s", oc, gd_d, oc, AF.Sigmoid) for oc in range(16)]
    gp_pq = {}
    prev_pnq = [0]
    wld = []
    @block.gpsimd
    def _(e):
        for i in range(2):
            wname, oc, _, _, _ = wjobs[i]
            wld.append(C.dma(e, "dg", out=wst[i % 2], in_=ins[wname][oc]))
    ac_pst = {}
    ds_spill = {}

    def emit_wjob(ji):
        wname, oc, dst, slot, func = wjobs[ji]
        pb = 4 + ji % 2
        guard = ac_pst.get(ji - 2, 0)
        @block.tensor
        def _(e, ji=ji, oc=oc, pb=pb, need=wld[ji], guard=guard):
            e.wait_ge(C.dg.h, need)
            if ji == 0:
                e.wait_ge(C.ac.h, ac_xnT)
            if guard:
                e.wait_ge(C.ac.h, guard)
            for blk in range(4):
                for dc in range(NCH):
                    lhsT = wst[ji % 2][:, dc * P:(dc + 1) * P]
                    rhs = XNT2[:, dc * TOK + blk * 256: dc * TOK + (blk + 1) * 256]
                    ins_ = e.matmul(PS[:, pb * 512 + blk * 256: pb * 512 + blk * 256 + 256]
                                    if blk < 2 else
                                    PS[:, (pb + 2) * 512 + (blk - 2) * 256:
                                       (pb + 2) * 512 + (blk - 2) * 256 + 256],
                                    lhsT, rhs, start=(dc == 0), stop=(dc == NCH - 1))
            C.pe.inc(ins_)
        pn = C.pe.n
        if ji + 2 < len(wjobs):
            @block.gpsimd
            def _(e, ji=ji, pn=pn):
                e.wait_ge(C.pe.h, pn)
                wname2, oc2, _, _, _ = wjobs[ji + 2]
                wld.append(C.dma(e, "dg", out=wst[ji % 2], in_=ins[wname2][oc2]))
        is_q = (wname == "wqkv_s" and oc < 16)
        if is_q:
            @block.vector
            def _(e, oc=oc, pq=prev_pnq[0]):
                if pq:
                    e.wait_ge(C.pe.h, pq)
                C.dv.inc(e.tensor_scalar(out=tmp256[:], in0=arq[:],
                                         scalar1=slope_sb[0:1, oc:oc + 1], scalar2=None,
                                         op0=ALU.mult))
                e.wait_ge(C.dv.h, C.dv.n)
            dvq = C.dv.n
            @block.scalar
            def _(e, dvq=dvq):
                e.wait_ge(C.dv.h, dvq)
                C.ac.inc(e.activation(out=tmp256[:], in_=tmp256[:], func=AF.Exp))
            acq = C.ac.n
            @block.tensor
            def _(e, acq=acq, gq=gp_pq.get(ji - 1, 0)):
                e.wait_ge(C.ac.h, acq)
                if gq:
                    e.wait_ge(C.gp.h, gq)
                C.pe.inc(e.matmul(PS[:, 0:256], onesrf[:], tmp256[:], start=True, stop=True))
            pnq = C.pe.n
            prev_pnq[0] = pnq
            @block.scalar
            def _(e, pnq=pnq):
                e.wait_ge(C.pe.h, pnq)
                C.ac.inc(e.copy(out=qdec_bc[:], in_=PS[:, 0:256]))
            ac_qd = C.ac.n

        @block.scalar
        def _(e, ji=ji, pb=pb, pn=pn, func=func, sp=ds_spill.get(ji - 2, 0)):
            e.wait_ge(C.pe.h, pn)
            if sp:
                e.wait_ge(C.ds.h, sp)
            e.activation(out=pstage[ji % 2][:, 0:512], in_=PS[:, pb * 512:(pb + 1) * 512], func=func)
            C.ac.inc(e.activation(out=pstage[ji % 2][:, 512:1024],
                                  in_=PS[:, (pb + 2) * 512:(pb + 3) * 512], func=func))
        ac_pst[ji] = C.ac.n
        an = C.ac.n
        if is_q:
            @block.gpsimd
            def _(e, ji=ji, an=an, ac_qd=ac_qd):
                e.wait_ge(C.ac.h, an)
                e.wait_ge(C.ac.h, ac_qd)
                C.gp.inc(e.tensor_tensor(
                    out=pstage[ji % 2][:].rearrange("p (b n) -> p b n", b=4),
                    in0=pstage[ji % 2][:].rearrange("p (b n) -> p b n", b=4),
                    in1=qdec_bc[:].unsqueeze(1).broadcast_to((P, 4, 256)), op=ALU.mult))
            gp_pq[ji] = C.gp.n
            an = ("gp", C.gp.n)
        else:
            an = ("ac", an)
        @block.sync
        def _(e, ji=ji, dst=dst, slot=slot, an=an):
            e.wait_ge({"ac": C.ac, "gp": C.gp}[an[0]].h, an[1])
            C.dma(e, "ds", out=dst[slot], in_=pstage[ji % 2])
        ds_spill[ji] = C.ds.n

    for ji in range(48):
        emit_wjob(ji)
        if ji % 3 == 0 and ji // 3 < NCH:
            emit_precast(ji // 3)
    ds_wotbf = C.ds.n

    ld_blk = {}
    @block.scalar
    def _(e):
        e.wait_ge(C.ds.h, ds_spill[47])
        ld_blk[("k", 0)] = C.dma(e, "da", out=kblk[:].rearrange("p (c n) -> p c n", c=16),
                                 in_=kd_d.rearrange("c p n -> p c n")[:, :, 0:256])
        ld_blk[("v", 0)] = C.dma(e, "da", out=vblk[:].rearrange("p (c n) -> p c n", c=16),
                                 in_=vd_d.rearrange("c p n -> p c n")[:, :, 0:256])
    p1_kt_ac = [0, 0]
    p1_kd_dv = [0, 0]
    p1_stt_dv = [0, 0]
    p1_pend = {}
    p1_dv = 0

    def p1_A(blk, h):
        par = h % 2
        ktok, vtok2, kdki = ktok2p[par], vtok2p[par], kdki2p[par]
        TB = par * 512
        @block.tensor
        def _(e, blk=blk, h=h, par=par, need=ld_blk[("v", blk)], dz=dv_z, TB=TB,
              ga=p1_kt_ac[par], cfb=ac_cfb):
            e.wait_ge(C.da.h, need)
            if ga:
                e.wait_ge(C.ac.h, ga)
            if blk == 0 and h == 0:
                e.wait_ge(C.dv.h, dz)
            if blk == 0 and h < 2:
                e.wait_ge(C.ac.h, cfb)
            for sub in range(2):
                view = PS[:, TB:TB + 512].bitcast(BF16)[:, sub * P:(sub + 1) * P]
                ins_ = e.transpose(view, kblk[:, h * 256 + sub * P: h * 256 + (sub + 1) * P],
                                   identb[:])
            for sub in range(2):
                view = PS[:, TB:TB + 512].bitcast(BF16)[:, (2 + sub) * P:(3 + sub) * P]
                ins_ = e.transpose(view, vblk[:, h * 256 + sub * P: h * 256 + (sub + 1) * P],
                                   identb[:])
            C.pe.inc(ins_)
        pn = C.pe.n
        @block.scalar
        def _(e, pn=pn, gd=p1_kd_dv[par], ktok=ktok, vtok2=vtok2, TB=TB):
            e.wait_ge(C.pe.h, pn)
            if gd:
                e.wait_ge(C.dv.h, gd)
            e.copy(out=ktok[:], in_=PS[:, TB:TB + 512].bitcast(BF16)[:, 0:256])
            C.ac.inc(e.copy(out=vtok2[:], in_=PS[:, TB:TB + 512].bitcast(BF16)[:, 256:512]))
        p1_kt_ac[par] = C.ac.n
        an = C.ac.n
        @block.vector
        def _(e, h=h, an=an, ktok=ktok, kdki=kdki):
            e.wait_ge(C.ac.h, an)
            for sub in range(2):
                ins_ = e.tensor_scalar(out=kdki[:, sub * P:(sub + 1) * P],
                                       in0=ktok[:, sub * P:(sub + 1) * P],
                                       scalar1=kdec[:, h * 2 + sub:h * 2 + sub + 1],
                                       scalar2=None, op0=ALU.mult)
            C.dv.inc(ins_)
        p1_kd_dv[par] = C.dv.n
        p1_pend[h] = C.dv.n

    def p1_B(blk, h):
        par = h % 2
        vtok2, kdki = vtok2p[par], kdki2p[par]
        CB = (2 + par) * 512
        @block.tensor
        def _(e, h=h, dvk=p1_pend[h], gd=p1_stt_dv[par], vtok2=vtok2, kdki=kdki, CB=CB):
            e.wait_ge(C.dv.h, dvk)
            if gd:
                e.wait_ge(C.dv.h, gd)
            for sub in range(2):
                ins_ = e.matmul(PS[:, CB:CB + P], kdki[:, sub * P:(sub + 1) * P],
                                vtok2[:, sub * P:(sub + 1) * P], start=(sub == 0), stop=(sub == 1))
            C.pe.inc(ins_)
        pnc = C.pe.n
        @block.vector
        def _(e, h=h, pnc=pnc, CB=CB):
            e.wait_ge(C.pe.h, pnc)
            C.dv.inc(e.scalar_tensor_tensor(out=S_out[:, h * P:(h + 1) * P],
                                            in0=S_out[:, h * P:(h + 1) * P],
                                            scalar=bdh[:, h:h + 1], in1=PS[:, CB:CB + P],
                                            op0=ALU.mult, op1=ALU.add))
        p1_stt_dv[par] = C.dv.n

    for blk in range(4):
        for h in range(17):
            if h < 16:
                p1_A(blk, h)
            if h >= 1:
                p1_B(blk, h - 1)
        p1_dv = C.dv.n
        if blk + 1 < 4:
            @block.scalar
            def _(e, blk=blk, pa=C.pe.n):
                e.wait_ge(C.pe.h, pa)
                ld_blk[("k", blk + 1)] = C.dma(e, "da", out=kblk[:].rearrange("p (c n) -> p c n", c=16),
                                               in_=kd_d.rearrange("c p n -> p c n")[:, :, (blk + 1) * 256:(blk + 2) * 256])
                ld_blk[("v", blk + 1)] = C.dma(e, "da", out=vblk[:].rearrange("p (c n) -> p c n", c=16),
                                               in_=vd_d.rearrange("c p n -> p c n")[:, :, (blk + 1) * 256:(blk + 2) * 256])

    @block.sync
    def _(e, pd=p1_dv):
        e.wait_ge(C.dv.h, pd)
        C.dma(e, "ds", out=ccin.rearrange("(h p) d -> p h d", p=P),
              in_=S_out[:].rearrange("p (h d) -> p h d", h=16))
    ds_cc = C.ds.n
    cc_box = [0]
    sin_box = [0]

    def emit_cc():
        @block.gpsimd
        def _(e, ds_cc=ds_cc):
            e.wait_ge(C.ds.h, ds_cc)
            C.cc.inc(e.collective_compute("AllGather", ALU.bypass,
                                          replica_groups=[list(range(NCORES))],
                                          ins=[ccin.opt()], outs=[ccag.opt()]))
        cc_box[0] = C.cc.n

    def emit_sin_round(r):
        @block.gpsimd
        def _(e, r=r, cc_n=cc_box[0], pd=sin_box[0]):
            if r == 0:
                e.wait_ge(C.cc.h, cc_n)
            if pd:
                e.wait_ge(C.dv.h, pd)
            C.dma(e, "dg", out=S_r[:].rearrange("p (h d) -> p h d", h=16),
                  in_=ccag[r * 2048:(r + 1) * 2048, :].rearrange("(h p) d -> p h d", p=P))
        dg_r = C.dg.n
        @block.vector
        def _(e, r=r, dg_r=dg_r, ac_cfb=ac_cfb):
            e.wait_ge(C.dg.h, dg_r)
            if r == 0:
                e.wait_ge(C.ac.h, ac_cfb)
            for h in range(16):
                C.dv.inc(e.scalar_tensor_tensor(out=S_in[:, h * P:(h + 1) * P],
                                                in0=S_r[:, h * P:(h + 1) * P],
                                                scalar=coef_bc[:, h * 8 + r:h * 8 + r + 1],
                                                in1=S_in[:, h * P:(h + 1) * P],
                                                op0=ALU.mult, op1=ALU.add))
        sin_box[0] = C.dv.n

    for ji in range(48, 64):
        emit_wjob(ji)
        if ji == 49:
            emit_cc()
        if ji >= 56:
            emit_sin_round(ji - 56)
    dv_sin = sin_box[0]

    @block.vector
    def _(e, dv_sin=dv_sin):
        C.dv.inc(e.tensor_copy(kv[:], S_in[:]))
        e.wait_ge(C.dv.h, C.dv.n)
    dv_kv0 = C.dv.n

    ld2 = {}
    @block.scalar
    def _(e, dv_kv0=dv_kv0, sp63=ds_spill[63]):
        e.wait_ge(C.dv.h, dv_kv0)
        e.wait_ge(C.ds.h, sp63)
        for nm, dd in (("q", qd_d), ("k", kd_d), ("v", vd_d), ("g", gd_d)):
            ld2[(nm, 0)] = C.dma(e, "da", out={"q": qblk, "k": kblk, "v": vblk, "g": gblk}[nm][:]
                                 .rearrange("p (c n) -> p c n", c=16),
                                 in_=dd.rearrange("c p n -> p c n")[:, :, 0:256])
    wo_ld = {}
    @block.scalar
    def _(e, ds_wotbf=ds_wotbf):
        e.wait_ge(C.ds.h, ds_wotbf)
        wo_ld[0] = (C.da, C.dma(e, "da", out=wrow[:], in_=wot_bf[0]))
    @block.gpsimd
    def _(e, ds_wotbf=ds_wotbf):
        e.wait_ge(C.ds.h, ds_wotbf)
        wo_ld[1] = (C.dg, C.dma(e, "dg", out=wrow2[:], in_=wot_bf[1]))

    g_out_ac = [0, 0]
    g_qkd_dv = [0, 0]
    g_kt_ac = [0, 0]
    kv_stt_n = [dv_kv0] * 16
    stt_par = [0, 0]
    pe_h_hist = [0, 0]
    dv_add_prev = 0
    an_rb = 0
    p2_pend = {}

    def p2_A(blk, h, an_rb_cur, dv_add):
        par = h % 2
        ktok, vtok2, kdki, kvbf, qkd = (ktok2p[par], vtok2p[par], kdki2p[par],
                                        kvbf2p[par], qkd2p[par])
        QB = (1 if par == 0 else 6) * 512
        TB = (4 if par == 0 else 3) * 512

        @block.scalar
        def _(e, h=h, kvn=kv_stt_n[h], ph=pe_h_hist[par], kvbf=kvbf):
            if kvn:
                e.wait_ge(C.dv.h, kvn)
            if ph:
                e.wait_ge(C.pe.h, ph)
            C.ac.inc(e.copy(out=kvbf[:], in_=kv[:, h * P:(h + 1) * P]))
        ac_kvbf = C.ac.n

        @block.tensor
        def _(e, blk=blk, h=h, gq=g_qkd_dv[par], needk=ld2[("k", blk)],
              needq=ld2[("q", blk)], gadd=dv_add, QB=QB):
            e.wait_ge(C.da.h, needk)
            e.wait_ge(C.da.h, needq)
            if h < 2 and gadd:
                e.wait_ge(C.dv.h, gadd)
            if gq:
                e.wait_ge(C.dv.h, gq)
            for sub in range(2):
                lhsT = kblk[:, h * 256 + sub * P: h * 256 + (sub + 1) * P]
                ins_ = e.matmul(PS[:, QB + sub * 256: QB + (sub + 1) * 256], lhsT,
                                qblk[:, h * 256:(h + 1) * 256], start=True, stop=True)
            C.pe.inc(ins_)
        pn_qk = C.pe.n

        @block.vector
        def _(e, h=h, pn_qk=pn_qk, qkd=qkd, QB=QB):
            e.wait_ge(C.pe.h, pn_qk)
            for sub in range(2):
                ins_ = e.scalar_tensor_tensor(
                    out=qkd[:, sub * 256:(sub + 1) * 256],
                    in0=PS[:, QB + sub * 256: QB + (sub + 1) * 256],
                    scalar=expk[:, h * 2 + sub:h * 2 + sub + 1],
                    in1=causal[:, sub * 256:(sub + 1) * 256],
                    op0=ALU.mult, op1=ALU.mult)
            C.dv.inc(ins_)
        g_qkd_dv[par] = C.dv.n
        dv_qkd = C.dv.n

        rbg = an_rb_cur if (par == 1 and h == 1) else 0
        @block.tensor
        def _(e, blk=blk, h=h, needv=ld2[("v", blk)], ga=g_kt_ac[par], TB=TB, rbg=rbg,
              gadd=dv_add):
            e.wait_ge(C.da.h, needv)
            if ga:
                e.wait_ge(C.ac.h, ga)
            if h < 2 and gadd:
                e.wait_ge(C.dv.h, gadd)
            if rbg:
                e.wait_ge(C.ac.h, rbg)
            for sub in range(2):
                view = PS[:, TB:TB + 512].bitcast(BF16)[:, sub * P:(sub + 1) * P]
                e.transpose(view, kblk[:, h * 256 + sub * P: h * 256 + (sub + 1) * P], identb[:])
            for sub in range(2):
                view = PS[:, TB:TB + 512].bitcast(BF16)[:, (2 + sub) * P:(3 + sub) * P]
                ins_ = e.transpose(view, vblk[:, h * 256 + sub * P: h * 256 + (sub + 1) * P],
                                   identb[:])
            C.pe.inc(ins_)
        pn_t = C.pe.n
        @block.scalar
        def _(e, pn_t=pn_t, ktok=ktok, vtok2=vtok2, TB=TB, gd=p2_kd_dv[par]):
            e.wait_ge(C.pe.h, pn_t)
            if gd:
                e.wait_ge(C.dv.h, gd)
            e.copy(out=ktok[:], in_=PS[:, TB:TB + 512].bitcast(BF16)[:, 0:256])
            C.ac.inc(e.copy(out=vtok2[:], in_=PS[:, TB:TB + 512].bitcast(BF16)[:, 256:512]))
        g_kt_ac[par] = C.ac.n
        an_kt = C.ac.n

        @block.vector
        def _(e, h=h, an_kt=an_kt, ktok=ktok, kdki=kdki):
            e.wait_ge(C.ac.h, an_kt)
            for sub in range(2):
                ins_ = e.tensor_scalar(out=kdki[:, sub * P:(sub + 1) * P],
                                       in0=ktok[:, sub * P:(sub + 1) * P],
                                       scalar1=kdec[:, h * 2 + sub:h * 2 + sub + 1],
                                       scalar2=None, op0=ALU.mult)
            C.dv.inc(ins_)
        p2_kd_dv[par] = C.dv.n
        p2_pend[h] = (ac_kvbf, dv_qkd, C.dv.n)

    def p2_B(blk, h):
        par = h % 2
        vtok2, kdki, kvbf, qkd = vtok2p[par], kdki2p[par], kvbf2p[par], qkd2p[par]
        OB = (0 if par == 0 else 5) * 512
        CB = (2 if par == 0 else 7) * 512
        ac_kvbf, dv_qkd, dv_kdki = p2_pend[h]

        @block.tensor
        def _(e, h=h, dv_qkd=dv_qkd, dv_kdki=dv_kdki, gkv=stt_par[par], ac_kvbf=ac_kvbf,
              go=g_out_ac[par], kvbf=kvbf, vtok2=vtok2, kdki=kdki, qkd=qkd, OB=OB, CB=CB):
            e.wait_ge(C.dv.h, dv_qkd)
            e.wait_ge(C.dv.h, dv_kdki)
            e.wait_ge(C.ac.h, ac_kvbf)
            if go:
                e.wait_ge(C.ac.h, go)
            e.matmul(PS[:, OB:OB + 256], kvbf[:], qblk[:, h * 256:(h + 1) * 256],
                     start=True, stop=False)
            for sub in range(2):
                ins_ = e.matmul(PS[:, OB:OB + 256], vtok2[:, sub * P:(sub + 1) * P],
                                qkd[:, sub * 256:(sub + 1) * 256],
                                start=False, stop=(sub == 1))
            if gkv:
                e.wait_ge(C.dv.h, gkv)
            for sub in range(2):
                ins2 = e.matmul(PS[:, CB:CB + P], kdki[:, sub * P:(sub + 1) * P],
                                vtok2[:, sub * P:(sub + 1) * P], start=(sub == 0), stop=(sub == 1))
            C.pe.inc(ins_)
            C.pe.inc(ins2)
        pn_av = C.pe.n
        pe_h_hist[par] = C.pe.n

        @block.scalar
        def _(e, blk=blk, h=h, pn_av=pn_av, OB=OB):
            e.wait_ge(C.pe.h, pn_av)
            C.ac.inc(e.copy(out=outb[:, h * 256:(h + 1) * 256], in_=PS[:, OB:OB + 256]))
        g_out_ac[par] = C.ac.n

        @block.vector
        def _(e, h=h, pn_av=pn_av, CB=CB):
            e.wait_ge(C.pe.h, pn_av)
            C.dv.inc(e.scalar_tensor_tensor(out=kv[:, h * P:(h + 1) * P],
                                            in0=kv[:, h * P:(h + 1) * P],
                                            scalar=bdh[:, h:h + 1],
                                            in1=PS[:, CB:CB + P],
                                            op0=ALU.mult, op1=ALU.add))
        kv_stt_n[h] = C.dv.n
        stt_par[par] = C.dv.n

    p2_kd_dv = [0, 0]
    for blk in range(4):
        for h in range(17):
            if h < 16:
                p2_A(blk, h, an_rb, dv_add_prev)
            if h >= 1:
                p2_B(blk, h - 1)

        if blk + 1 < 4:
            @block.scalar
            def _(e, blk=blk, ph=max(pe_h_hist)):
                e.wait_ge(C.pe.h, ph)
                for nm, dd in (("q", qd_d), ("k", kd_d), ("v", vd_d)):
                    ld2[(nm, blk + 1)] = C.dma(e, "da",
                        out={"q": qblk, "k": kblk, "v": vblk}[nm][:]
                        .rearrange("p (c n) -> p c n", c=16),
                        in_=dd.rearrange("c p n -> p c n")[:, :, (blk + 1) * 256:(blk + 2) * 256])

        if L1_STAGE < 2:
            break
        pe_sq = [0, 0]
        for h in range(16):
            @block.scalar
            def _(e, h=h, g=pe_sq[h % 2]):
                if g:
                    e.wait_ge(C.pe.h, g)
                C.ac.inc(e.activation(out=sqp[h % 2][:], in_=outb[:, h * 256:(h + 1) * 256],
                                      func=AF.Square))
            an_sq = C.ac.n
            @block.tensor
            def _(e, h=h, an_sq=an_sq):
                e.wait_ge(C.ac.h, an_sq)
                C.pe.inc(e.matmul(PS[0:1, 3 * 512:3 * 512 + 256], onesc[:], sqp[h % 2][:],
                                  start=(h == 0), stop=(h == 15)))
            pe_sq[h % 2] = C.pe.n
        pn_sq = C.pe.n
        @block.vector
        def _(e, pn_sq=pn_sq):
            e.wait_ge(C.pe.h, pn_sq)
            C.dv.inc(e.tensor_scalar(out=rr[:], in0=PS[0:1, 3 * 512:3 * 512 + 256],
                                     scalar1=1.0 / D, scalar2=1e-6, op0=ALU.mult, op1=ALU.add))
            e.wait_ge(C.dv.h, C.dv.n)
        dv_rr = C.dv.n
        @block.scalar
        def _(e, dv_rr=dv_rr):
            e.wait_ge(C.dv.h, dv_rr)
            C.ac.inc(e.activation(out=rr[:], in_=rr[:], func=AF.Sqrt))
        an_rr = C.ac.n
        @block.vector
        def _(e, an_rr=an_rr):
            e.wait_ge(C.ac.h, an_rr)
            C.dv.inc(e.reciprocal(rr[:], rr[:]))
            e.wait_ge(C.dv.h, C.dv.n)
        dv_rc = C.dv.n
        @block.tensor
        def _(e, dv_rc=dv_rc):
            e.wait_ge(C.dv.h, dv_rc)
            C.pe.inc(e.matmul(PS[:, 3 * 512:3 * 512 + 256], onesrf[:], rr[:], start=True, stop=True))
        pn_rb = C.pe.n
        @block.scalar
        def _(e, pn_rb=pn_rb):
            e.wait_ge(C.pe.h, pn_rb)
            C.ac.inc(e.copy(out=rstd_bc[:], in_=PS[:, 3 * 512:3 * 512 + 256]))
        an_rb = C.ac.n

        dv_pr = 0
        gp_pr = 0
        gp_n2 = [0, 0]
        for h in range(16):
            @block.vector
            def _(e, h=h, an_rb=an_rb, needg=ld2[("g", blk)], g=gp_n2[h % 2]):
                if h == 0:
                    e.wait_ge(C.ac.h, an_rb)
                    e.wait_ge(C.da.h, needg)
                if g:
                    e.wait_ge(C.gp.h, g)
                C.dv.inc(e.scalar_tensor_tensor(out=sqp[h % 2][:], in0=outb[:, h * 256:(h + 1) * 256],
                                                scalar=normw[:, h:h + 1], in1=rstd_bc[:],
                                                op0=ALU.mult, op1=ALU.mult))
            dv_pr = C.dv.n
            @block.gpsimd
            def _(e, h=h, dv_pr=dv_pr):
                e.wait_ge(C.dv.h, dv_pr)
                C.gp.inc(e.tensor_tensor(out=prodT[:, h * 256:(h + 1) * 256], in0=sqp[h % 2][:],
                                         in1=gblk[:, h * 256:(h + 1) * 256], op=ALU.mult))
            gp_n2[h % 2] = C.gp.n
            gp_pr = C.gp.n

        if blk + 1 < 4:
            @block.scalar
            def _(e, blk=blk, gp_pr=gp_pr):
                e.wait_ge(C.gp.h, gp_pr)
                ld2[("g", blk + 1)] = C.dma(e, "da",
                    out=gblk[:].rearrange("p (c n) -> p c n", c=16),
                    in_=gd_d.rearrange("c p n -> p c n")[:, :, (blk + 1) * 256:(blk + 2) * 256])

        for dc in range(NCH):
            wbuf = wrow if dc % 2 == 0 else wrow2
            need = wo_ld[blk * NCH + dc]
            @block.tensor
            def _(e, dc=dc, need=need, gp_pr=gp_pr, dv_add=dv_add_prev,
                  wbuf=wbuf, anr=an_rb):
                e.wait_ge(need[0].h, need[1])
                if dc == 0:
                    e.wait_ge(C.gp.h, gp_pr)
                    e.wait_ge(C.ac.h, anr)
                    if dv_add:
                        e.wait_ge(C.dv.h, dv_add)
                for sub in range(2):
                    lhsT = prodT[:, dc * 256 + sub * P: dc * 256 + (sub + 1) * P]
                    for nn_ in range(4):
                        ins_ = e.matmul(PS[:, (sub * 4 + nn_) * 512:(sub * 4 + nn_ + 1) * 512],
                                        lhsT, wbuf[:, nn_ * 512:(nn_ + 1) * 512],
                                        start=(dc == 0), stop=(dc == NCH - 1))
                C.pe.inc(ins_)
            nxt = blk * NCH + dc + 2
            if nxt < 4 * NCH and nxt not in wo_ld:
                pn2 = C.pe.n
                if nxt % 2 == 0:
                    @block.scalar
                    def _(e, nxt=nxt, pn2=pn2):
                        e.wait_ge(C.pe.h, pn2)
                        wo_ld[nxt] = (C.da, C.dma(e, "da", out=wrow[:], in_=wot_bf[nxt % NCH]))
                else:
                    @block.gpsimd
                    def _(e, nxt=nxt, pn2=pn2):
                        e.wait_ge(C.pe.h, pn2)
                        wo_ld[nxt] = (C.dg, C.dma(e, "dg", out=wrow2[:], in_=wot_bf[nxt % NCH]))
        pn_op = C.pe.n
        @block.vector
        def _(e, blk=blk, pn_op=pn_op):
            e.wait_ge(C.pe.h, pn_op)
            for sub in range(2):
                xs = X[:, (blk * 2 + sub) * D:(blk * 2 + sub + 1) * D]
                ins_ = e.tensor_tensor(out=xs, in0=xs,
                                       in1=PS[:, sub * 4 * 512:(sub * 4 + 4) * 512], op=ALU.add)
            C.dv.inc(ins_)
        dv_add_prev = C.dv.n
    C.barrier()
    C.pop_scope()




def emit_l1ffn(C, X, PS):
    nc, block = C.nc, C.block
    ins = C.ins
    C.push_scope()

    XNT3 = C.sbuf("f_xnT", [P, NCH * TOK], BF16)
    hT = C.sbuf("f_hT", [P, 64 * 512], BF16)
    identb = C.sbuf("f_ident", [P, P], BF16)
    onescf = C.sbuf("f_onescf", [P, 1], F32)
    onesrf = C.sbuf("f_onesrf", [1, P], F32)
    xn2 = dbuf(C, "f_xn2", D, BF16)
    wfp = dbuf(C, "f_wfp", D, F32)
    wq = dbuf(C, "f_wq", D, BF16)
    b1c = C.sbuf("f_b1c", [P, 64], F32)
    b2r = C.sbuf("f_b2r", [1, D], BF16)
    b2bc = C.sbuf("f_b2bc", [P, D], BF16)
    qa = C.sbuf("f_qa", [P, D], BF16)
    qa_ap = qa[:]
    if C.am is None:
        acc = C.sbuf("f_acc", [P, 2], F32)
        accv = C.sbuf("f_accv", [P, 1], F32)
        sc2 = C.sbuf("f_sc2", [1, 2], F32)
        tbc = C.sbuf("f_tbc", [P, 2], F32)
        ntbc = C.sbuf("f_ntbc", [P, 2], F32)
    else:
        tbc, ntbc = C.am["tbc"], C.am["ntbc"]
    scr2 = mk_ln_scr(C, "f_ln", dump_ap=xn2[0], need_xtmp=False)
    w1q_d, w2q_d = C.scr["f_w1q"], C.scr["f_w2q"]
    ccin, ccag = C.scr["f_ccin"], C.scr["f_ccag"]

    @block.gpsimd
    def _(e):
        C.dma(e, "dg", out=identb[:], in_=ins["ident"])
        C.dma(e, "dg", out=b2r[:], in_=ins["b2row"])
    @block.sync
    def _(e):
        C.dma(e, "ds", out=onescf[:], in_=ins["ones_col"])
        C.dma(e, "ds", out=onesrf[:], in_=ins["ones_row"])
        C.dma(e, "ds", out=b1c[:], in_=ins["b1col"])
    dsn0 = C.ds.n
    dgn0 = C.dg.n

    onesrb = C.sbuf("f_onesrb", [1, P], BF16)
    @block.gpsimd
    def _(e):
        C.dma(e, "dg", out=onesrb[:], in_=ins["ones_row"])
    dgn1 = C.dg.n
    @block.tensor
    def _(e):
        e.wait_ge(C.dg.h, dgn1)
        e.wait_ge(C.dg.h, dgn0)
        for i in range(4):
            ins_ = e.matmul(PS[:, (2 + i) * 512:(3 + i) * 512], onesrb[:],
                            b2r[:, i * 512:(i + 1) * 512], start=True, stop=True)
        C.pe.inc(ins_)
    pn_b2 = C.pe.n
    @block.scalar
    def _(e, pn_b2=pn_b2):
        e.wait_ge(C.pe.h, pn_b2)
        C.ac.inc(e.copy(out=b2bc[:], in_=PS[:, 2 * 512:6 * 512]))
    ac_b2 = C.ac.n

    if C.am is None:
        dv_acc = 0
        @block.vector
        def _(e):
            e.memset(acc[:], 0.0)
            C.dv.inc(e.memset(accv[:], 0.0))
        dv_z = C.dv.n
        part_ld = []
        @block.sync
        def _(e):
            part_ld.append(C.dma(e, "ds", out=wfp[0], in_=ins["wpart"][0]))
        ac_abs = 0
        for i in range(16):
            @block.scalar
            def _(e, i=i, need=part_ld[i], dz=dv_z):
                e.wait_ge(C.ds.h, need)
                C.ac.inc(e.activation(out=qa[:], in_=wfp[0], func=AF.Abs, accum_out=accv[:]))
            ac_abs = C.ac.n
            @block.vector
            def _(e, i=i, ac_abs=ac_abs, dz=dv_z):
                e.wait_ge(C.ac.h, ac_abs)
                if i == 0:
                    e.wait_ge(C.dv.h, dz)
                col = 0 if i < 8 else 1
                C.dv.inc(e.tensor_tensor(out=acc[:, col:col + 1], in0=acc[:, col:col + 1],
                                         in1=accv[:], op=ALU.add))
                e.wait_ge(C.dv.h, C.dv.n)
            dv_acc = C.dv.n
            if i + 1 < 16:
                @block.sync
                def _(e, i=i, aa=ac_abs):
                    e.wait_ge(C.ac.h, aa)
                    part_ld.append(C.dma(e, "ds", out=wfp[0], in_=ins["wpart"][i + 1]))
        @block.tensor
        def _(e, da=dv_acc):
            e.wait_ge(C.dv.h, da)
            C.pe.inc(e.matmul(PS[0:1, 0:2], onescf[:], acc[:], start=True, stop=True))
        pn_acc = C.pe.n
        @block.scalar
        def _(e, pn_acc=pn_acc):
            e.wait_ge(C.pe.h, pn_acc)
            C.ac.inc(e.copy(out=sc2[:], in_=PS[0:1, 0:2]))
        ac_sc = C.ac.n
        @block.sync
        def _(e, ac_sc=ac_sc):
            e.wait_ge(C.ac.h, ac_sc)
            C.dma(e, "ds", out=ccin, in_=sc2[:])
        ds_cc = C.ds.n
        @block.gpsimd
        def _(e, ds_cc=ds_cc):
            e.wait_ge(C.ds.h, ds_cc)
            C.cc.inc(e.collective_compute("AllReduce", ALU.add,
                                          replica_groups=[list(range(NCORES))],
                                          ins=[ccin.opt()], outs=[ccag.opt()]))
        cc_n = C.cc.n
        @block.scalar
        def _(e, cc_n=cc_n):
            e.wait_ge(C.cc.h, cc_n)
            C.dma(e, "da", out=sc2[:], in_=ccag)
        da_sc = C.da.n
        @block.vector
        def _(e, da_sc=da_sc):
            e.wait_ge(C.da.h, da_sc)
            C.dv.inc(e.tensor_scalar(out=sc2[:], in0=sc2[:], scalar1=0.7 / (8192.0 * 2048.0),
                                     scalar2=None, op0=ALU.mult))
            e.wait_ge(C.dv.h, C.dv.n)
        dv_t = C.dv.n
        @block.tensor
        def _(e, dv_t=dv_t):
            e.wait_ge(C.dv.h, dv_t)
            C.pe.inc(e.matmul(PS[:, 0:2], onesrf[:], sc2[:], start=True, stop=True))
        pn_t = C.pe.n
        @block.scalar
        def _(e, pn_t=pn_t):
            e.wait_ge(C.pe.h, pn_t)
            C.ac.inc(e.copy(out=tbc[:], in_=PS[:, 0:2]))
        ac_t = C.ac.n
        @block.vector
        def _(e, ac_t=ac_t):
            e.wait_ge(C.ac.h, ac_t)
            C.dv.inc(e.tensor_scalar(out=ntbc[:], in0=tbc[:], scalar1=-1.0, scalar2=None,
                                     op0=ALU.mult))
            e.wait_ge(C.dv.h, C.dv.n)
        dv_nt = C.dv.n
    else:
        dv_nt = C.am["dv_nt"]

    rdy_hist = []
    pe_tr = []
    ac_ev = []
    for t in range(NT):
        pre = []
        if t >= 1:
            pre.append((C.pe, pe_tr[t - 1]))
        if t >= 2:
            k2, n2 = rdy_hist[t - 2]
            pre.append((C.dv, n2))
            pv = pe_tr[t - 2]
            @block.vector
            def _(e, pv=pv):
                e.wait_ge(C.pe.h, pv)
        scr2[t % 2]["dump"] = xn2[(t + 1) % 2]
        kind, n = emit_ln_tile(C, X[:, t * D:(t + 1) * D], xn2[t % 2], scr2[t % 2],
                               pre_wait=pre)
        rdy_hist.append((kind, n))
        prev_ev = ac_ev[t - 1] if t >= 1 else 0
        @block.tensor
        def _(e, t=t, n=n, prev_ev=prev_ev):
            e.wait_ge(C.dv.h, n)
            e.wait_ge(C.ac.h, prev_ev)
            for dc in range(NCH):
                bb = 6 + dc // 8
                view = PS[:, bb * 512:(bb + 1) * 512].bitcast(BF16)[:, (dc % 8) * P:(dc % 8 + 1) * P]
                ins_ = e.transpose(view, xn2[t % 2][:, dc * P:(dc + 1) * P], identb[:])
            C.pe.inc(ins_)
        pe_tr.append(C.pe.n)
        pn = C.pe.n
        @block.scalar
        def _(e, t=t, pn=pn):
            e.wait_ge(C.pe.h, pn)
            for half in range(2):
                out = XNT3[:].rearrange("p (c n) -> p c n", c=NCH)[:, half * 8:(half + 1) * 8,
                                                                   t * P:(t + 1) * P]
                i2 = PS[:, (6 + half) * 512:(7 + half) * 512].bitcast(BF16).rearrange(
                    "p (c n) -> p c n", c=8)
                ins_ = e.copy(out=out, in_=i2)
            C.ac.inc(ins_)
        ac_ev.append(C.ac.n)
    ac_xnT = C.ac.n

    def emit_quant(src_ap, dst_ap, guards):
        col = guards["col"]
        box = [0]
        @block.vector
        def _(e, col=col, need=guards.get("need"), gpe=guards.get("pe", 0),
              gda=guards.get("da", 0)):
            if need:
                e.wait_ge(need[0].h, need[1])
            C.dv.inc(e.tensor_scalar(out=qa_ap, in0=src_ap, scalar1=ntbc[:, col:col + 1],
                                     scalar2=None, op0=ALU.is_le))
            if gpe:
                e.wait_ge(C.pe.h, gpe)
            if gda:
                e.wait_ge(C.da.h, gda)
            C.dv.inc(e.tensor_scalar(out=dst_ap, in0=src_ap, scalar1=tbc[:, col:col + 1],
                                     scalar2=None, op0=ALU.is_ge))
            box[0] = C.dv.n
            C.dv.inc(e.tensor_tensor(out=dst_ap, in0=dst_ap, in1=qa_ap, op=ALU.subtract))
        return box[0], C.dv.n

    bank_h = {0: 0, 1: 0}
    dv_down_prev = 0
    for half in range(2):
        hq_ld = []
        @block.sync
        def _(e, half=half, dnt=dv_nt, gpe=C.pe.n, gda=C.da.n):
            e.wait_ge(C.dv.h, dnt)
            if half == 0:
                hq_ld.append((C.ds, C.dma(e, "ds", out=wfp[0], in_=ins["w1_s"][0])))
            else:
                e.wait_ge(C.pe.h, gpe)
                e.wait_ge(C.da.h, gda)
                for i in range(2):
                    hq_ld.append((C.ds, C.dma(e, "ds", out=wq[i], in_=w1q_d[i])))
        da_w1sp = {}
        dv_abh = {}
        pe_prevq = {}
        for f in range(64):
            if half == 0:
                dv_ab, dv_sub = emit_quant(wfp[f % 2], wq[f % 2][:],
                    {"col": 0, "need": hq_ld[f], "pe": pe_prevq.get(f - 2, 0),
                     "da": da_w1sp.get(f - 2, 0)})
                dv_abh[f] = dv_ab
                if f + 1 < 64:
                    ga = dv_abh.get(f - 1, 0)
                    if (f + 1) % 2 == 0:
                        @block.sync
                        def _(e, f=f, ga=ga):
                            if ga:
                                e.wait_ge(C.dv.h, ga)
                            hq_ld.append((C.ds, C.dma(e, "ds", out=wfp[(f + 1) % 2],
                                                      in_=ins["w1_s"][f + 1])))
                    else:
                        @block.gpsimd
                        def _(e, f=f, ga=ga):
                            if ga:
                                e.wait_ge(C.dv.h, ga)
                            hq_ld.append((C.dg, C.dma(e, "dg", out=wfp[(f + 1) % 2],
                                                      in_=ins["w1_s"][f + 1])))
                @block.scalar
                def _(e, f=f, dv_sub=dv_sub):
                    e.wait_ge(C.dv.h, dv_sub)
                    C.dma(e, "da", out=w1q_d[f], in_=wq[f % 2][:])
                da_w1sp[f] = C.da.n
                ready = (C.dv, dv_sub)
            else:
                ready = hq_ld[f]
            bk = f % 2
            @block.tensor
            def _(e, f=f, half=half, bk=bk, ready=ready, ga=bank_h[bk], axn=ac_xnT,
                  gdp=dv_down_prev):
                e.wait_ge(ready[0].h, ready[1])
                if f == 0:
                    e.wait_ge(C.ac.h, axn)
                    if gdp:
                        e.wait_ge(C.dv.h, gdp)
                if ga:
                    e.wait_ge(C.ac.h, ga)
                for dc in range(NCH):
                    lhsT = wq[f % 2][:, dc * P:(dc + 1) * P]
                    rhs = XNT3[:, dc * TOK + half * 512: dc * TOK + (half + 1) * 512]
                    ins_ = e.matmul(PS[:, bk * 512:(bk + 1) * 512], lhsT, rhs,
                                    start=(dc == 0), stop=(dc == NCH - 1))
                C.pe.inc(ins_)
            pn = C.pe.n
            pe_prevq[f] = pn
            @block.scalar
            def _(e, f=f, bk=bk, pn=pn):
                e.wait_ge(C.pe.h, pn)
                C.ac.inc(e.activation(out=hT[:, f * 512:(f + 1) * 512],
                                      in_=PS[:, bk * 512:(bk + 1) * 512], func=AF.Gelu,
                                      bias=b1c[:, f:f + 1], scale=1.0))
            bank_h[bk] = C.ac.n
            if half == 1 and f + 2 < 64:
                if f % 2 == 0:
                    @block.sync
                    def _(e, f=f, pn=pn):
                        e.wait_ge(C.pe.h, pn)
                        hq_ld.append((C.ds, C.dma(e, "ds", out=wq[f % 2], in_=w1q_d[f + 2])))
                else:
                    @block.gpsimd
                    def _(e, f=f, pn=pn):
                        e.wait_ge(C.pe.h, pn)
                        hq_ld.append((C.dg, C.dma(e, "dg", out=wq[f % 2], in_=w1q_d[f + 2])))
        ac_h = C.ac.n

        for sg in range(2):
            dq_ld = []
            quantize_now = (half == 0 and sg == 0)
            @block.sync
            def _(e, sg=sg, quantize_now=quantize_now, gdv=C.dv.n, gpe=C.pe.n, gda=C.da.n):
                if quantize_now:
                    e.wait_ge(C.dv.h, gdv)
                    dq_ld.append((C.ds, C.dma(e, "ds", out=wfp[0], in_=ins["w2_s"][0])))
                else:
                    e.wait_ge(C.pe.h, gpe)
                    e.wait_ge(C.da.h, gda)
                    for i in range(2):
                        dq_ld.append((C.ds, C.dma(e, "ds", out=wq[i], in_=w2q_d[i])))
            da_w2sp = {}
            dv_abd = {}
            pe_prevq = {}
            for f in range(64):
                if quantize_now:
                    dv_ab, dv_sub = emit_quant(wfp[f % 2], wq[f % 2][:],
                        {"col": 1, "need": dq_ld[f], "pe": pe_prevq.get(f - 2, 0),
                         "da": da_w2sp.get(f - 2, 0)})
                    dv_abd[f] = dv_ab
                    if f + 1 < 64:
                        ga = dv_abd.get(f - 1, 0)
                        if (f + 1) % 2 == 0:
                            @block.sync
                            def _(e, f=f, ga=ga):
                                if ga:
                                    e.wait_ge(C.dv.h, ga)
                                dq_ld.append((C.ds, C.dma(e, "ds", out=wfp[(f + 1) % 2],
                                                          in_=ins["w2_s"][f + 1])))
                        else:
                            @block.gpsimd
                            def _(e, f=f, ga=ga):
                                if ga:
                                    e.wait_ge(C.dv.h, ga)
                                dq_ld.append((C.dg, C.dma(e, "dg", out=wfp[(f + 1) % 2],
                                                          in_=ins["w2_s"][f + 1])))
                    @block.scalar
                    def _(e, f=f, dv_sub=dv_sub):
                        e.wait_ge(C.dv.h, dv_sub)
                        C.dma(e, "da", out=w2q_d[f], in_=wq[f % 2][:])
                    da_w2sp[f] = C.da.n
                    ready = (C.dv, dv_sub)
                else:
                    ready = dq_ld[f]
                @block.tensor
                def _(e, f=f, half=half, sg=sg, ready=ready, ach=ac_h, gdp=dv_down_prev):
                    e.wait_ge(ready[0].h, ready[1])
                    if f == 0:
                        e.wait_ge(C.ac.h, ach)
                        if gdp:
                            e.wait_ge(C.dv.h, gdp)
                    for si in range(2):
                        s = sg * 2 + si
                        lhsT = hT[:, f * 512 + s * P: f * 512 + (s + 1) * P]
                        for n_ in range(4):
                            ins_ = e.matmul(PS[:, (si * 4 + n_) * 512:(si * 4 + n_ + 1) * 512],
                                            lhsT, wq[f % 2][:, n_ * 512:(n_ + 1) * 512],
                                            start=(f == 0), stop=(f == 63))
                    C.pe.inc(ins_)
                pn = C.pe.n
                pe_prevq[f] = pn
                if (not quantize_now) and f + 2 < 64:
                    if f % 2 == 0:
                        @block.sync
                        def _(e, f=f, pn=pn):
                            e.wait_ge(C.pe.h, pn)
                            dq_ld.append((C.ds, C.dma(e, "ds", out=wq[f % 2], in_=w2q_d[f + 2])))
                    else:
                        @block.gpsimd
                        def _(e, f=f, pn=pn):
                            e.wait_ge(C.pe.h, pn)
                            dq_ld.append((C.dg, C.dma(e, "dg", out=wq[f % 2], in_=w2q_d[f + 2])))
            pn_dn = C.pe.n
            @block.vector
            def _(e, half=half, sg=sg, pn_dn=pn_dn):
                e.wait_ge(C.pe.h, pn_dn)
                for si in range(2):
                    s = sg * 2 + si
                    tile = half * 4 + s
                    xs = X[:, tile * D:(tile + 1) * D]
                    ins_ = e.tensor_tensor(out=xs, in0=xs,
                                           in1=PS[:, si * 4 * 512:(si * 4 + 4) * 512], op=ALU.add)
                C.dv.inc(ins_)
            dv_down_prev = C.dv.n

    @block.vector
    def _(e, ac_b2=ac_b2):
        e.wait_ge(C.ac.h, ac_b2)
        for t in range(NT):
            xs = X[:, t * D:(t + 1) * D]
            ins_ = e.tensor_tensor(out=xs, in0=xs, in1=b2bc[:], op=ALU.add)
        C.dv.inc(ins_)

    C.barrier()
    C.pop_scope()


def build_program(blocks=("l0",)):
    nc = bass.Bass("TRN2", target_bir_lowering=False)
    es = ExitStack()
    ins = {}

    def din(name, shape, dtype=F32):
        ins[name] = nc.dram_tensor(name, list(shape), dtype, kind="ExternalInput").ap()

    din("x", (TOK + HALO, D) if "l0" in blocks else (TOK, D))
    xout = nc.dram_tensor("xout", [TOK, D], F32, kind="ExternalOutput").ap()
    if "l0" in blocks:
        din("wq_s", (NCH, P, D)); din("wk_s", (4, P, D)); din("wv_s", (4, P, D))
        din("wo_s", (NCH, P, D)); din("lnw_row", (1, D))
        din("cs_fm", (P, 1536)); din("sn_fm", (P, 1536)); din("mask0", (NT, P, KW))
    if "moe" in blocks:
        din("gwt", (P, NCH * 32)); din("wg_t", (32, P, NCH * 768))
        din("wu_t", (32, P, NCH * 768)); din("wd_t", (32, P, 6 * D))
    if "l1a" in blocks:
        din("wqkv_s", (48, P, D)); din("wgate_s", (NCH, P, D)); din("wot_r", (NCH, P, D))
        din("normw_col", (P, 16)); din("slope", (1, 16)); din("causal01", (P, 512))
        din("arqneg", (1, 256)); din("arkneg", (P, 2)); din("arkpos", (P, 2)); din("tcneg", (1, 8))
    if "l1f" in blocks:
        din("w1_s", (64, P, D)); din("w2_s", (64, P, D)); din("wpart", (16, P, D))
        din("b1col", (P, 64)); din("b2row", (1, D))
    din("ident", (P, P)); din("pswap", (P, P))
    din("ones_row", (1, P)); din("ones_col", (P, 1))
    if DBG_TAP:
        ins["dbg"] = nc.dram_tensor("dbg", [P, NCH * TOK], BF16, kind="ExternalOutput").ap()
        ins["dbgk"] = nc.dram_tensor("dbgk", [P, 4 * 1536], BF16, kind="ExternalOutput").ap()
        ins["dbgv"] = nc.dram_tensor("dbgv", [P, LT * 512], BF16, kind="ExternalOutput").ap()
        ins["dbgq"] = nc.dram_tensor("dbgq", [P, NCH * TOK], BF16, kind="ExternalOutput").ap()

    scr = {}
    if "l0" in blocks:
        scr["l0_qT"] = nc.dram_tensor("l0_qT", [NCH, P, TOK], BF16).ap()
    if "moe" in blocks:
        scr["moe_mw"] = nc.dram_tensor("moe_mw", [32, NT * P], BF16).ap()
    if "l1a" in blocks:
        for nm in ("l1_q", "l1_k", "l1_v", "l1_g"):
            scr[nm] = nc.dram_tensor(nm, [NCH, P, TOK], BF16).ap()
        scr["l1_wotbf"] = nc.dram_tensor("l1_wotbf", [NCH, P, D], BF16).ap()
        scr["l1_ccin"] = nc.dram_tensor("l1_ccin", [16 * P, P], F32).ap()
        scr["l1_ccag"] = nc.dram_tensor("l1_ccag", [NCORES * 16 * P, P], F32).ap()
    if "l1f" in blocks:
        scr["f_w1q"] = nc.dram_tensor("f_w1q", [64, P, D], BF16).ap()
        scr["f_w2q"] = nc.dram_tensor("f_w2q", [64, P, D], BF16).ap()
        scr["f_ccin"] = nc.dram_tensor("f_ccin", [1, 2], F32).ap()
        scr["f_ccag"] = nc.dram_tensor("f_ccag", [1, 2], F32).ap()

    block = es.enter_context(nc.Block())
    C = Ctx(nc, es, block)
    C.ins = ins
    C.scr = scr

    X = C.sbuf("X", [P, NT * D], F32)
    PS = es.enter_context(nc.psum_tensor("PS", [P, 4096], F32))[:]
    C.PS = PS
    C.am = absmean_alloc(C) if ("l0" in blocks and "l1f" in blocks) else None

    xl = []
    off = HALO if "l0" in blocks else 0
    @block.sync
    def _(e):
        for t in range(NT):
            xl.append(C.dma(e, "ds", out=X[:, t * D:(t + 1) * D],
                                           in_=ins["x"][off + t * P: off + (t + 1) * P, :]))
    for eng in ("scalar", "vector", "tensor", "gpsimd"):
        def mk(eng=eng):
            def f(e):
                e.wait_ge(C.ds.h, xl[-1])
            return f
        getattr(block, eng)(mk())

    if "l0" in blocks:
        emit_l0attn(C, X, PS)
    if "moe" in blocks:
        emit_moe(C, X, PS)
    if "l1a" in blocks:
        emit_l1attn(C, X, PS)
    if "l1f" in blocks:
        emit_l1ffn(C, X, PS)

    stn = []
    @block.sync
    def _(e):
        e.wait_ge(C.dv.h, C.dv.n)
        e.wait_ge(C.ac.h, C.ac.n)
        e.wait_ge(C.gp.h, C.gp.n)
        e.wait_ge(C.pe.h, C.pe.n)
        for t in range(NT):
            stn.append(C.dma(e, "ds", out=xout[t * P:(t + 1) * P, :],
                                            in_=X[:, t * D:(t + 1) * D]))
        e.wait_ge(C.ds.h, stn[-1])

    es.close()
    return nc


def kernel(**inputs):
    per_core, shared = make_host_inputs(inputs)
    nc = build_program(blocks=("l0", "moe", "l1a", "l1f"))
    in_maps = []
    for c in range(NCORES):
        m = dict(shared)
        m.update(per_core[c])
        in_maps.append(m)
    res = run_bass_kernel_spmd(nc, in_maps, core_ids=list(range(NCORES)))
    x = [res.results[c]["xout"] for c in range(NCORES)]
    return np.concatenate(x, axis=0).reshape(2, 4096, D)


# version 12 (final)
# speedup vs baseline: 1.1861x; 1.0004x over previous
import numpy as np
from contextlib import ExitStack
import concourse.bass as bass
import concourse.mybir as mybir
from concourse.bass_utils import run_bass_kernel_spmd

F32 = mybir.dt.float32
BF16 = mybir.dt.bfloat16
AF = mybir.ActivationFunctionType
ALU = mybir.AluOpType
AX = mybir.AxisListType

P = 128
D = 2048
NCORES = 8
TOK = 1024
NT = TOK // P
HALO = 512
LT = (TOK + HALO) // P
NCH = D // P
NEG = -3.0e38
W0 = 512
KW = W0 + P
INV_SQRT_HD = 0.08838834764831845
L0_STAGE = 3
MOE_STAGE = 3
L1_STAGE = 3
DBG_TAP = False


def _chunk_stream(w, oc_size=P):
    K, M = w.shape
    n_oc = M // oc_size
    return np.ascontiguousarray(
        w.reshape(K // P, P, n_oc, oc_size).transpose(2, 1, 0, 3).reshape(n_oc, P, -1))


def _rope_perm():
    pm = np.zeros(P, dtype=np.int64)
    pm[0:64] = np.arange(0, P, 2)
    pm[64:P] = np.arange(1, P, 2)
    return pm


def make_host_inputs(inputs):
    f32 = np.float32
    x = np.asarray(inputs["L0_attn_x"], f32).reshape(2 * 4096, D)
    per_core = [dict() for _ in range(NCORES)]
    shared = {}

    for c in range(NCORES):
        blkx = np.zeros((TOK + HALO, D), f32)
        if c % 4 == 0:
            blkx[HALO:] = x[c * TOK:(c + 1) * TOK]
        else:
            blkx[:] = x[c * TOK - HALO:(c + 1) * TOK]
        per_core[c]["x"] = blkx

    pm = _rope_perm()
    wq = np.asarray(inputs["L0_attn_wq"], f32)
    wk = np.asarray(inputs["L0_attn_wk"], f32)
    wv = np.asarray(inputs["L0_attn_wv"], f32)
    wo = np.asarray(inputs["L0_attn_wo"], f32)
    qperm = (np.arange(16)[:, None] * P + pm[None, :]).reshape(-1)
    kperm = (np.arange(4)[:, None] * P + pm[None, :]).reshape(-1)
    shared["wq_s"] = _chunk_stream(wq[:, qperm])
    shared["wk_s"] = _chunk_stream(wk[:, kperm])
    shared["wv_s"] = _chunk_stream(wv)
    shared["wo_s"] = _chunk_stream(wo)
    shared["lnw_row"] = np.asarray(inputs["L0_attn_ln_w"], f32).reshape(1, D)

    inv_freq = (1.0 / 10000.0 ** (np.arange(0, 128, 2) / 128.0)).astype(f32)
    for c in range(NCORES):
        pos0 = (c % 4) * TOK - HALO
        pos = pos0 + np.arange(TOK + HALO, dtype=f32)
        if c % 4 == 0:
            pos = np.maximum(pos, 0.0)
        ang = inv_freq[:, None] * pos[None, :]
        per_core[c]["cs_fm"] = np.concatenate([np.cos(ang), np.cos(ang)], 0).astype(f32)
        per_core[c]["sn_fm"] = np.concatenate([-np.sin(ang), np.sin(ang)], 0).astype(f32)

    r = np.arange(P)[:, None]
    j = np.arange(KW)[None, :]
    band = (j >= r) & (j <= r + W0)
    for c in range(NCORES):
        m = np.zeros((NT, P, KW), f32)
        for t in range(NT):
            ok = band.copy()
            if c % 4 == 0:
                ok = ok & ((t * P + j) >= HALO)
            m[t] = np.where(ok, 0.0, NEG)
        per_core[c]["mask0"] = m

    gw = np.asarray(inputs["L0_ffn_gate_w"], f32)
    shared["gwt"] = np.ascontiguousarray(gw.T.reshape(NCH, P, 32).transpose(1, 0, 2).reshape(P, NCH * 32))
    wg = np.asarray(inputs["L0_ffn_w_gate_proj"], f32)
    wu = np.asarray(inputs["L0_ffn_w_up_proj"], f32)
    wd = np.asarray(inputs["L0_ffn_w_down_proj"], f32)
    shared["wg_t"] = np.ascontiguousarray(wg.transpose(0, 2, 1).reshape(32, NCH, P, 768)
                                          .transpose(0, 2, 1, 3).reshape(32, P, NCH * 768))
    shared["wu_t"] = np.ascontiguousarray(wu.transpose(0, 2, 1).reshape(32, NCH, P, 768)
                                          .transpose(0, 2, 1, 3).reshape(32, P, NCH * 768))
    shared["wd_t"] = np.ascontiguousarray(wd.transpose(0, 2, 1).reshape(32, 6, P, D)
                                          .transpose(0, 2, 1, 3).reshape(32, P, 6 * D))

    wqkv = np.asarray(inputs["L1_attn_w_qkv"], f32)
    wgate1 = np.asarray(inputs["L1_attn_w_gate"], f32)
    wout1 = np.asarray(inputs["L1_attn_w_out"], f32)
    shared["wqkv_s"] = _chunk_stream(np.ascontiguousarray(wqkv.T))
    shared["wgate_s"] = _chunk_stream(np.ascontiguousarray(wgate1.T))
    shared["wot_r"] = np.ascontiguousarray(wout1.T.reshape(NCH, P, D))
    shared["normw_col"] = np.ascontiguousarray(
        np.asarray(inputs["L1_attn_norm_w"], f32).reshape(NCH, P).T)
    shared["slope"] = np.asarray(inputs["L1_attn_slope_rate"], f32).reshape(1, 16)
    cz = np.zeros((P, 512), f32)
    for sub in range(2):
        for khat in range(P):
            kg = sub * P + khat
            m = np.arange(256)
            cz[khat, sub * 256:(sub + 1) * 256] = (m >= kg).astype(f32)
    shared["causal01"] = cz
    shared["arqneg"] = (-(np.arange(256, dtype=f32) + 1.0)).reshape(1, 256)
    ak = np.zeros((P, 2), f32)
    akp = np.zeros((P, 2), f32)
    for sub in range(2):
        ak[:, sub] = -(255.0 - (sub * P + np.arange(P, dtype=f32)))
        akp[:, sub] = (sub * P + np.arange(P, dtype=f32)) + 1.0
    shared["arkneg"] = ak
    shared["arkpos"] = akp
    for c in range(NCORES):
        crel = c % 4
        tc = np.full((1, 8), -1e8, f32)
        for r in range(NCORES):
            if r // 4 == c // 4 and (r % 4) < crel:
                tc[0, r] = -1024.0 * (crel - 1 - (r % 4))
        per_core[c]["tcneg"] = tc

    w1 = np.asarray(inputs["L1_ffn_w1"], f32)
    w2 = np.asarray(inputs["L1_ffn_w2"], f32)
    w1_s = _chunk_stream(np.ascontiguousarray(w1.T))
    w2_s = np.ascontiguousarray(w2.T.reshape(64, P, D))
    shared["w1_s"] = w1_s
    shared["w2_s"] = w2_s
    shared["b1col"] = np.ascontiguousarray(
        np.asarray(inputs["L1_ffn_b1"], f32).reshape(64, P).T)
    shared["b2row"] = np.asarray(inputs["L1_ffn_b2"], f32).reshape(1, D)
    for c in range(NCORES):
        per_core[c]["wpart"] = np.ascontiguousarray(
            np.concatenate([w1_s[c * 8:(c + 1) * 8], w2_s[c * 8:(c + 1) * 8]], axis=0))

    shared["ident"] = np.eye(P, dtype=f32)
    psw = np.zeros((P, P), f32)
    for i in range(P):
        psw[(i + 64) % P, i] = 1.0
    shared["pswap"] = psw
    shared["ones_row"] = np.ones((1, P), f32)
    shared["ones_col"] = np.ones((P, 1), f32)
    return per_core, shared



class Sem:
    def __init__(self, nc, es, name):
        self.h = es.enter_context(nc.semaphore(name))
        self.n = 0

    def inc(self, instr, d=1):
        instr.then_inc(self.h, d)
        self.n += d
        return self.n


class Ctx:
    def __init__(self, nc, es, block):
        self.nc, self.es, self.block = nc, es, block
        self.pe = Sem(nc, es, "s_pe")
        self.dv = Sem(nc, es, "s_dv")
        self.ac = Sem(nc, es, "s_ac")
        self.gp = Sem(nc, es, "s_gp")
        self.ds = Sem(nc, es, "s_ds")
        self.da = Sem(nc, es, "s_da")
        self.dg = Sem(nc, es, "s_dg")
        self.cc = Sem(nc, es, "s_cc")
        self.ins = {}
        self.scr = {}
        self._scope = []

    def sbuf(self, name, shape, dtype):
        t = self._scope[-1].enter_context(self.nc.sbuf_tensor(name, shape, dtype)) \
            if self._scope else self.es.enter_context(self.nc.sbuf_tensor(name, shape, dtype))
        return t

    def dma(self, e, q, out, in_):
        s = {"ds": self.ds, "da": self.da, "dg": self.dg}[q]
        if s.n:
            e.wait_ge(s.h, s.n)
        return s.inc(e.dma_start(out=out, in_=in_), 16)

    def push_scope(self):
        self._scope.append(ExitStack())

    def pop_scope(self):
        self._scope.pop().close()

    def barrier(self):
        C = self
        targets = [(C.pe, "tensor"), (C.dv, "vector"), (C.ac, "scalar"),
                   (C.gp, "gpsimd"), (C.ds, "sync"), (C.da, None), (C.dg, None)]
        vals = [(sem, sem.n) for sem, eng in targets]
        for sem, eng in targets:
            if eng is None:
                continue
            def mk2(eng=eng, sem=sem):
                def f(e):
                    for s2, n2 in vals:
                        if s2 is not sem and n2:
                            e.wait_ge(s2.h, n2)
                return f
            getattr(C.block, eng)(mk2())



def dbuf(C, name, cols, dtype, n=2):
    t = C.sbuf(name, [P, n * cols], dtype)
    return [t[:, i * cols:(i + 1) * cols] for i in range(n)]


def emit_ln_tile(C, src, xn_out, scr, l0_rms=False, lnw_bc=None, pre_wait=None):
    block = C.block
    def w(e):
        if pre_wait:
            for s, n in pre_wait:
                e.wait_ge(s.h, n)

    def step(e, instr):
        C.dv.inc(instr)
        e.wait_ge(C.dv.h, C.dv.n)

    @block.scalar
    def _(e):
        w(e)
        e.activation(out=scr["dump"], in_=src, func=AF.Square, accum_out=scr["sumsq"][:])
        C.ac.inc(e.activation(out=scr["dump"], in_=src, func=AF.Identity,
                              accum_out=scr["sums"][:]))
    ac_stats = C.ac.n

    @block.vector
    def _(e):
        e.wait_ge(C.ac.h, ac_stats)
        step(e, e.tensor_scalar(out=scr["mean"][:], in0=scr["sums"][:], scalar1=1.0 / D,
                                scalar2=None, op0=ALU.mult))
        step(e, e.tensor_scalar(out=scr["t0"][:], in0=scr["mean"][:], scalar1=scr["mean"][:],
                                scalar2=None, op0=ALU.mult))
        step(e, e.tensor_scalar(out=scr["var"][:], in0=scr["sumsq"][:], scalar1=1.0 / D,
                                scalar2=scr["t0"][:], op0=ALU.mult, op1=ALU.subtract))
        C.dv.inc(e.tensor_scalar(out=scr["vare"][:], in0=scr["var"][:], scalar1=1e-5,
                                 scalar2=None, op0=ALU.add))
    dv_var = C.dv.n

    @block.scalar
    def _(e):
        e.wait_ge(C.dv.h, dv_var)
        C.ac.inc(e.activation(out=scr["std"][:], in_=scr["vare"][:], func=AF.Sqrt))
    ac_std = C.ac.n

    @block.vector
    def _(e):
        e.wait_ge(C.ac.h, ac_std)
        C.dv.inc(e.reciprocal(scr["rstd"][:], scr["std"][:]))
        e.wait_ge(C.dv.h, C.dv.n)

    if l0_rms:
        @block.vector
        def _(e):
            step(e, e.tensor_scalar(out=scr["t0"][:], in0=scr["var"][:], scalar1=scr["rstd"][:],
                                    scalar2=scr["rstd"][:], op0=ALU.mult, op1=ALU.mult))
            C.dv.inc(e.tensor_scalar(out=scr["t0"][:], in0=scr["t0"][:], scalar1=1e-6,
                                     scalar2=None, op0=ALU.add))
        dv_t = C.dv.n

        @block.scalar
        def _(e):
            e.wait_ge(C.dv.h, dv_t)
            C.ac.inc(e.activation(out=scr["std"][:], in_=scr["t0"][:], func=AF.Sqrt))
        ac2 = C.ac.n

        @block.vector
        def _(e):
            e.wait_ge(C.ac.h, ac2)
            step(e, e.reciprocal(scr["t0"][:], scr["std"][:]))
            step(e, e.tensor_scalar(out=scr["s_t"][:], in0=scr["rstd"][:],
                                    scalar1=scr["t0"][:], scalar2=None, op0=ALU.mult))
        scale_ap = scr["s_t"]
    else:
        scale_ap = scr["rstd"]

    if lnw_bc is None:
        @block.vector
        def _(e):
            w(e)
            C.dv.inc(e.tensor_scalar(out=xn_out, in0=src, scalar1=scr["mean"][:],
                                     scalar2=scale_ap[:], op0=ALU.subtract, op1=ALU.mult))
        return ("dv", C.dv.n)
    else:
        @block.vector
        def _(e):
            w(e)
            C.dv.inc(e.tensor_scalar(out=scr["xtmp"][:][:, 0:D] if False else scr["xtmp"][:], in0=src, scalar1=scr["mean"][:],
                                     scalar2=scale_ap[:], op0=ALU.subtract, op1=ALU.mult))
        dvn = C.dv.n

        @block.gpsimd
        def _(e):
            e.wait_ge(C.dv.h, dvn)
            C.gp.inc(e.tensor_tensor(out=xn_out, in0=scr["xtmp"][:], in1=lnw_bc, op=ALU.mult))
        return ("gp", C.gp.n)


def mk_ln_scr(C, tag, dump_ap=None, need_xtmp=True):
    dump = dump_ap if dump_ap is not None else C.sbuf(tag + "dump", [P, D], BF16)[:]
    xtmp = C.sbuf(tag + "xtmp", [P, D], BF16) if need_xtmp else None
    out = []
    for i in range(2):
        scr = {k: C.sbuf(tag + k + str(i), [P, 1], F32) for k in
               ["sums", "sumsq", "mean", "var", "vare", "t0", "std", "rstd", "s_t"]}
        scr["dump"] = dump
        scr["xtmp"] = xtmp
        out.append(scr)
    return out


def emit_bcast_row(C, PS, bank, onesr, row_ap, out_ap, n):
    block = C.block
    @block.tensor
    def _(e):
        nb = 0
        for off in range(0, n, 512):
            w_ = min(512, n - off)
            ins_ = e.matmul(PS[:, bank * 512 + off: bank * 512 + off + w_],
                            onesr[:], row_ap[:, off:off + w_], start=True, stop=True)
        C.pe.inc(ins_)
    pn = C.pe.n

    @block.scalar
    def _(e):
        e.wait_ge(C.pe.h, pn)
        C.ac.inc(e.copy(out=out_ap, in_=PS[:, bank * 512: bank * 512 + n]))
    return C.ac.n



AM_HOIST = True
L1F_SPLITQ = True
L1F_DBGQ = False


def absmean_alloc(C):
    block = C.block
    b = {}
    b["acc"] = C.sbuf("am_acc", [P, 2], F32)
    b["accv"] = C.sbuf("am_accv", [P, 1], F32)
    b["sc2"] = C.sbuf("am_sc2", [1, 2], F32)
    b["tbc"] = C.sbuf("am_tbc", [P, 2], F32)
    b["ntbc"] = C.sbuf("am_ntbc", [P, 2], F32)
    b["onescf"] = C.sbuf("am_onesc", [P, 1], F32)
    @block.sync
    def _(e):
        C.dma(e, "ds", out=b["onescf"][:], in_=C.ins["ones_col"])
    b["ds0"] = C.ds.n
    @block.vector
    def _(e):
        e.memset(b["acc"][:], 0.0)
        C.dv.inc(e.memset(b["accv"][:], 0.0))
        e.wait_ge(C.dv.h, C.dv.n)
    b["dz"] = C.dv.n
    b["st"] = {"i": 0, "ld": 0, "ac2": [0, 0], "dv": 0}
    return b


def absmean_step(C, stage_ap, first_guard=None):
    am = getattr(C, "am", None)
    if am is None or am["st"]["i"] >= 16:
        return
    block = C.block
    s = am["st"]
    i = s["i"]
    s["i"] = i + 1
    col = 0 if i < 8 else 1
    @block.scalar
    def _(e, i=i, ga=s["ac2"][i % 2], fg=first_guard):
        if fg:
            e.wait_ge(fg[0].h, fg[1])
        if ga:
            e.wait_ge(C.ac.h, ga)
        s["ld"] = C.dma(e, "da", out=stage_ap, in_=C.ins["wpart"][i])
    @block.scalar
    def _(e, ld=s["ld"], dz=am["dz"], i=i, gdv=s["dv"]):
        e.wait_ge(C.da.h, ld)
        if i == 0:
            e.wait_ge(C.dv.h, dz)
        if gdv:
            e.wait_ge(C.dv.h, gdv)
        C.ac.inc(e.activation(out=stage_ap, in_=stage_ap, func=AF.Abs,
                              accum_out=am["accv"][:]))
    s["ac2"][i % 2] = C.ac.n
    @block.vector
    def _(e, an=s["ac2"][i % 2], col=col):
        e.wait_ge(C.ac.h, an)
        C.dv.inc(e.tensor_tensor(out=am["acc"][:, col:col + 1], in0=am["acc"][:, col:col + 1],
                                 in1=am["accv"][:], op=ALU.add))
        e.wait_ge(C.dv.h, C.dv.n)
    s["dv"] = C.dv.n


def absmean_finish_a(C):
    am = C.am
    block = C.block
    assert am["st"]["i"] == 16, am["st"]["i"]
    ccin, ccag = C.scr["f_ccin"], C.scr["f_ccag"]
    @block.tensor
    def _(e, da=am["st"]["dv"], ds0=am["ds0"]):
        e.wait_ge(C.dv.h, da)
        e.wait_ge(C.ds.h, ds0)
        C.pe.inc(e.matmul(C.PS[0:1, 0:2], am["onescf"][:], am["acc"][:],
                          start=True, stop=True))
    pn = C.pe.n
    @block.scalar
    def _(e, pn=pn):
        e.wait_ge(C.pe.h, pn)
        C.ac.inc(e.copy(out=am["sc2"][:], in_=C.PS[0:1, 0:2]))
    am["ac_sc"] = C.ac.n
    @block.sync
    def _(e, an=am["ac_sc"]):
        e.wait_ge(C.ac.h, an)
        C.dma(e, "ds", out=ccin, in_=am["sc2"][:])
    ds_cc = C.ds.n
    @block.gpsimd
    def _(e, ds_cc=ds_cc):
        e.wait_ge(C.ds.h, ds_cc)
        C.cc.inc(e.collective_compute("AllReduce", ALU.add,
                                      replica_groups=[list(range(NCORES))],
                                      ins=[ccin.opt()], outs=[ccag.opt()]))
    am["cc"] = C.cc.n


def absmean_finish_b(C, onesrf_ap, ds_need):
    am = C.am
    block = C.block
    ccag = C.scr["f_ccag"]
    @block.scalar
    def _(e, cc_n=am["cc"]):
        e.wait_ge(C.cc.h, cc_n)
        am["da_sc"] = C.dma(e, "da", out=am["sc2"][:], in_=ccag)
    @block.vector
    def _(e):
        e.wait_ge(C.da.h, am["da_sc"])
        C.dv.inc(e.tensor_scalar(out=am["sc2"][:], in0=am["sc2"][:],
                                 scalar1=0.7 / (8192.0 * 2048.0), scalar2=None, op0=ALU.mult))
        e.wait_ge(C.dv.h, C.dv.n)
    dv_t = C.dv.n
    @block.tensor
    def _(e, dv_t=dv_t, ds_need=ds_need):
        e.wait_ge(C.dv.h, dv_t)
        e.wait_ge(C.ds.h, ds_need)
        C.pe.inc(e.matmul(C.PS[:, 0:2], onesrf_ap, am["sc2"][:], start=True, stop=True))
    pn_t = C.pe.n
    @block.scalar
    def _(e, pn_t=pn_t):
        e.wait_ge(C.pe.h, pn_t)
        C.ac.inc(e.copy(out=am["tbc"][:], in_=C.PS[:, 0:2]))
    am["ac_t"] = C.ac.n
    @block.vector
    def _(e, ac_t=am["ac_t"]):
        e.wait_ge(C.ac.h, ac_t)
        C.dv.inc(e.tensor_scalar(out=am["ntbc"][:], in0=am["tbc"][:], scalar1=-1.0,
                                 scalar2=None, op0=ALU.mult))
        e.wait_ge(C.dv.h, C.dv.n)
    am["dv_nt"] = C.dv.n


def emit_l0attn(C, X, PS):
    nc, block = C.nc, C.block
    ins = C.ins
    C.push_scope()
    kT = C.sbuf("l0kT", [P, 4 * 1536], BF16)
    vtok = C.sbuf("l0vtok", [P, LT * 512], BF16)
    identb = C.sbuf("identb", [P, P], BF16)
    pswapb = C.sbuf("pswapb", [P, P], BF16)
    onesr = C.sbuf("onesr", [1, P], BF16)
    C.push_scope()

    big0 = C.sbuf("big0", [P, NCH * 1536], BF16)
    wst = dbuf(C, "l0wst", NCH * P, BF16)
    csf = C.sbuf("l0csf", [P, 1536], BF16)
    snf = C.sbuf("l0snf", [P, 1536], BF16)
    lnwb = C.sbuf("l0lnwb", [P, D], BF16)
    lnwrow = C.sbuf("lnwrow", [1, D], BF16)
    xhalo = dbuf(C, "l0xhalo", D, F32)
    xn2 = dbuf(C, "l0xn2", D, BF16)
    qsb = dbuf(C, "l0qsb", 512, BF16)
    rsw = dbuf(C, "l0rsw", 512, BF16)
    rcs = dbuf(C, "l0rcs", 512, BF16)
    qasm = dbuf(C, "l0qasm", TOK, BF16)
    scr2 = mk_ln_scr(C, "l0ln")

    qT_d = C.scr["l0_qT"]

    @block.gpsimd
    def _(e):
        C.dma(e, "dg", out=csf[:], in_=ins["cs_fm"])
        C.dma(e, "dg", out=snf[:], in_=ins["sn_fm"])
        C.dma(e, "dg", out=identb[:], in_=ins["ident"])
        C.dma(e, "dg", out=pswapb[:], in_=ins["pswap"])
        C.dma(e, "dg", out=onesr[:], in_=ins["ones_row"])
        C.dma(e, "dg", out=lnwrow[:], in_=ins["lnw_row"])
    dg_const = C.dg.n

    @block.tensor
    def _(e):
        e.wait_ge(C.dg.h, dg_const)
    @block.scalar
    def _(e):
        e.wait_ge(C.dg.h, dg_const)
    ac_lnw = emit_bcast_row(C, PS, 0, onesr, lnwrow, lnwb[:], D)

    halo_ld = []
    @block.sync
    def _(e):
        for i in range(2):
            halo_ld.append(C.dma(e, "ds", out=xhalo[i], in_=ins["x"][i * P:(i + 1) * P, :]))

    pe_tr = []
    ac_ev = []
    rdy_hist = []
    for lt in range(LT):
        if lt < 4:
            src = xhalo[lt % 2]
            pre = [(C.ds, halo_ld[lt])]
        else:
            src = X[:, (lt - 4) * D:(lt - 3) * D]
            pre = []
        if lt >= 2:
            k2, n2 = rdy_hist[lt - 2]
            pre.append(({"dv": C.dv, "gp": C.gp}[k2], n2))
            k1, n1 = rdy_hist[lt - 1]
            pre.append(({"dv": C.dv, "gp": C.gp}[k1], n1))
            pv = pe_tr[lt - 2]
            @block.vector
            def _(e, pv=pv):
                e.wait_ge(C.pe.h, pv)
            @block.gpsimd
            def _(e, pv=pv):
                e.wait_ge(C.pe.h, pv)
        kind, n = emit_ln_tile(C, src, xn2[lt % 2], scr2[lt % 2], l0_rms=True, lnw_bc=lnwb[:],
                               pre_wait=pre)
        rdy_hist.append((kind, n))
        sem = {"dv": C.dv, "gp": C.gp}[kind]
        if lt + 2 < 4:
            @block.sync
            def _(e, sem=sem, n=n, lt=lt):
                e.wait_ge(sem.h, n)
                halo_ld.append(C.dma(e, "ds", out=xhalo[lt % 2],
                                                    in_=ins["x"][(lt + 2) * P:(lt + 3) * P, :]))
        bank = (lt % 2) * 2
        prev_ev = ac_ev[lt - 2] if lt >= 2 else ac_lnw
        @block.tensor
        def _(e, sem=sem, n=n, bank=bank, prev_ev=prev_ev, lt=lt):
            e.wait_ge(sem.h, n)
            e.wait_ge(C.ac.h, prev_ev)
            for dc in range(NCH):
                bb = bank + dc // 8
                view = PS[:, bb * 512:(bb + 1) * 512].bitcast(BF16)[:, (dc % 8) * P:(dc % 8 + 1) * P]
                ins_ = e.transpose(view, xn2[lt % 2][:, dc * P:(dc + 1) * P], identb[:])
            C.pe.inc(ins_)
        pe_tr.append(C.pe.n)
        pn = C.pe.n

        @block.scalar
        def _(e, bank=bank, lt=lt, pn=pn):
            e.wait_ge(C.pe.h, pn)
            for half in range(2):
                bb = bank + half
                out = big0[:].rearrange("p (c n) -> p c n", c=NCH)[:, half * 8:(half + 1) * 8,
                                                                   lt * P:(lt + 1) * P]
                i2 = PS[:, bb * 512:(bb + 1) * 512].bitcast(BF16).rearrange("p (c n) -> p c n", c=8)
                ins_ = e.copy(out=out, in_=i2)
            C.ac.inc(ins_)
        ac_ev.append(C.ac.n)
        if lt >= 5 and C.am is not None:
            for _k in range(2):
                i = C.am["st"]["i"]
                fg = None
                if i < 2:
                    kk, nn = rdy_hist[2 + i]
                    fg = ({"dv": C.dv, "gp": C.gp}[kk], nn)
                absmean_step(C, xhalo[i % 2][:], first_guard=fg)
    ac_xnT = C.ac.n

    stream_ld = []
    bank_guard = {b: ("ac", 0) for b in range(8)}

    def guard_wait(e, b):
        kind, n = bank_guard[b]
        if n:
            e.wait_ge({"ac": C.ac, "dv": C.dv, "gp": C.gp, "ds": C.ds, "da": C.da}[kind].h, n)

    jobs = []
    for oc in range(4):
        jobs.append(("k", "wk_s", oc, 0, 1536))
    for oc in range(4):
        jobs.append(("v", "wv_s", oc, 0, 1536))
    for oc in range(NCH):
        jobs.append(("q", "wq_s", oc, HALO, TOK))

    @block.gpsimd
    def _(e):
        e.wait_ge(C.ac.h, ac_xnT)
        for i in range(2):
            tag, wname, oc, _, _ = jobs[i]
            stream_ld.append(C.dma(e, "dg", out=wst[i % 2], in_=ins[wname][oc]))

    vT_sb = dbuf(C, "l0vTsb", 1536, BF16)
    git = [0]
    rope_gp_hist = []
    qasm_spill = {0: 0, 1: 0}

    for ji, (tag, wname, oc, off, cols) in enumerate(jobs):
        need = stream_ld[ji]
        ngrp = cols // 512
        for g in range(ngrp):
            it = git[0]; git[0] += 1
            pb = 4 + (it % 2)
            @block.tensor
            def _(e, ji=ji, need=need, oc=oc, off=off, g=g, pb=pb, tag=tag):
                e.wait_ge(C.dg.h, need)
                if ji == 0 and g == 0:
                    e.wait_ge(C.ac.h, ac_xnT)
                guard_wait(e, pb)
                for dc in range(NCH):
                    lhsT = wst[ji % 2][:, dc * P:(dc + 1) * P]
                    rhs = big0[:, dc * 1536 + off + g * 512: dc * 1536 + off + (g + 1) * 512]
                    ins_ = e.matmul(PS[:, pb * 512:(pb + 1) * 512], lhsT, rhs,
                                    start=(dc == 0), stop=(dc == NCH - 1))
                C.pe.inc(ins_)
            pn = C.pe.n

            if tag == "v":
                @block.scalar
                def _(e, pn=pn, pb=pb, g=g, oc=oc):
                    e.wait_ge(C.pe.h, pn)
                    C.ac.inc(e.copy(out=vT_sb[oc % 2][:, g * 512:(g + 1) * 512],
                                    in_=PS[:, pb * 512:(pb + 1) * 512]))
                bank_guard[pb] = ("ac", C.ac.n)
            else:
                sb2 = 6 + (it % 2)
                ropehist = rope_gp_hist
                gguard = ropehist[-2] if len(ropehist) >= 2 else 0
                @block.scalar
                def _(e, pn=pn, pb=pb, it=it, gguard=gguard):
                    e.wait_ge(C.pe.h, pn)
                    if gguard:
                        e.wait_ge(C.gp.h, gguard)
                    C.ac.inc(e.copy(out=qsb[it % 2], in_=PS[:, pb * 512:(pb + 1) * 512]))
                an = C.ac.n
                bank_guard[pb] = ("ac", an)

                @block.tensor
                def _(e, an=an, sb2=sb2, it=it):
                    e.wait_ge(C.ac.h, an)
                    guard_wait(e, sb2)
                    C.pe.inc(e.matmul(PS[:, sb2 * 512:(sb2 + 1) * 512], pswapb[:],
                                      qsb[it % 2], start=True, stop=True))
                pn2 = C.pe.n

                cs_off = off + g * 512
                @block.vector
                def _(e, pn2=pn2, sb2=sb2, it=it, cs_off=cs_off, gguard=gguard):
                    e.wait_ge(C.pe.h, pn2)
                    if gguard:
                        e.wait_ge(C.gp.h, gguard)
                    C.dv.inc(e.tensor_tensor(out=rsw[it % 2],
                                             in0=PS[:, sb2 * 512:(sb2 + 1) * 512],
                                             in1=snf[:, cs_off:cs_off + 512], op=ALU.mult))
                dvn = C.dv.n
                bank_guard[sb2] = ("dv", dvn)

                if tag == "q":
                    out = qasm[oc % 2][:, g * 512:(g + 1) * 512]
                else:
                    out = kT[:, oc * 1536 + g * 512: oc * 1536 + (g + 1) * 512]
                qg = qasm_spill[oc % 2] if tag == "q" else 0
                @block.gpsimd
                def _(e, it=it, cs_off=cs_off, dvn=dvn, out=out, an=an, qg=qg):
                    e.wait_ge(C.ac.h, an)
                    if qg:
                        e.wait_ge(C.ds.h, qg)
                    e.tensor_tensor(out=rcs[it % 2], in0=qsb[it % 2],
                                    in1=csf[:, cs_off:cs_off + 512], op=ALU.mult)
                    e.wait_ge(C.dv.h, dvn)
                    C.gp.inc(e.tensor_tensor(out=out, in0=rcs[it % 2],
                                             in1=rsw[it % 2], op=ALU.add))
                rope_gp_hist.append(C.gp.n)

        if tag == "v":
            an_v = C.ac.n
            @block.tensor
            def _(e, an_v=an_v, oc=oc):
                e.wait_ge(C.ac.h, an_v)
                guard_wait(e, 2)
                guard_wait(e, 3)
                for lt in range(LT):
                    bb = 2 + lt // 8
                    view = PS[:, bb * 512:(bb + 1) * 512].bitcast(BF16)[:, (lt % 8) * P:(lt % 8 + 1) * P]
                    ins_ = e.transpose(view, vT_sb[oc % 2][:, lt * P:(lt + 1) * P], identb[:])
                C.pe.inc(ins_)
            pnv = C.pe.n

            @block.scalar
            def _(e, pnv=pnv, oc=oc):
                e.wait_ge(C.pe.h, pnv)
                out = vtok[:].rearrange("p (t n) -> p t n", t=LT)[:, 0:8, oc * P:(oc + 1) * P]
                C.ac.inc(e.copy(out=out, in_=PS[:, 2 * 512:3 * 512].bitcast(BF16).rearrange(
                    "p (t n) -> p t n", t=8)))
                out2 = vtok[:].rearrange("p (t n) -> p t n", t=LT)[:, 8:12, oc * P:(oc + 1) * P]
                C.ac.inc(e.copy(out=out2, in_=PS[:, 3 * 512:3 * 512 + 512].bitcast(BF16)[:, 0:4 * P]
                                .rearrange("p (t n) -> p t n", t=4)))
            bank_guard[2] = ("ac", C.ac.n)
            bank_guard[3] = ("ac", C.ac.n)
        elif tag == "q":
            gn = C.gp.n
            @block.sync
            def _(e, gn=gn, oc=oc):
                e.wait_ge(C.gp.h, gn)
                C.dma(e, "ds", out=qT_d[oc], in_=qasm[oc % 2])
            qasm_spill[oc % 2] = C.ds.n

        if ji + 2 < len(jobs):
            tag2, wname2, oc2, _, _ = jobs[ji + 2]
            pn_free = C.pe.n
            @block.gpsimd
            def _(e, pn_free=pn_free, wname2=wname2, oc2=oc2, ji=ji):
                e.wait_ge(C.pe.h, pn_free)
                stream_ld.append(C.dma(e, "dg", out=wst[ji % 2], in_=ins[wname2][oc2]))
        if C.am is not None:
            i = C.am["st"]["i"]
            absmean_step(C, xhalo[i % 2][:])

    if L0_STAGE < 1.5 and "dbgk" in C.ins:
        ac_all0 = C.ac.n
        @block.gpsimd
        def _(e):
            e.wait_ge(C.ac.h, ac_all0)
            C.dma(e, "dg", out=C.ins["dbgq"].rearrange("p (c n) -> p c n", c=NCH),
                  in_=big0[:].rearrange("p (c n) -> p c n", c=NCH)[:, :, 0:TOK])
            C.dma(e, "dg", out=C.ins["dbgk"], in_=big0[:, 0:4 * 1536])
    if L0_STAGE < 2 and "dbgk" in C.ins:
        gn_all = C.gp.n; ac_all = C.ac.n
        @block.gpsimd
        def _(e):
            e.wait_ge(C.gp.h, gn_all)
            e.wait_ge(C.ac.h, ac_all)
            C.dma(e, "dg", out=C.ins["dbgk"], in_=kT[:])
            C.dma(e, "dg", out=C.ins["dbgv"], in_=vtok[:])
            e.wait_ge(C.ds.h, C.ds.n)
            C.dma(e, "dg", out=C.ins["dbgq"].rearrange("p (c n) -> p c n", c=NCH), in_=C.scr["l0_qT"].rearrange("c p n -> p c n"))
    C.barrier()
    C.pop_scope()
    if L0_STAGE < 2:
        return

    C.push_scope()
    aoT = C.sbuf("l0aoT", [P, NCH * TOK], BF16)
    qt_sb = dbuf(C, "l0qt", NCH * P, BF16)
    masks = dbuf(C, "l0masks", KW, F32)
    ssc = dbuf(C, "l0ssc", KW, F32)
    sat = dbuf(C, "l0sat", KW, BF16)
    satT = dbuf(C, "l0satT", KW, BF16)
    oev = dbuf(C, "l0oev", 1024, BF16)
    wst2 = dbuf(C, "l0wst2", NCH * P, BF16)
    nmax = dbuf(C, "l0nmax", 1, F32)
    ssum = dbuf(C, "l0ssum", 1, F32)
    srec = dbuf(C, "l0srec", 1, F32)

    qt_ld = []
    mk_ld = []
    @block.scalar
    def _(e):
        for t in range(2):
            qt_ld.append(C.dma(e, "da", 
                out=qt_sb[t % 2].rearrange("p (c n) -> p c n", c=NCH),
                in_=C.scr["l0_qT"].rearrange("c p n -> p c n")[:, :, t * P:(t + 1) * P]))
    @block.sync
    def _(e):
        for t in range(2):
            mk_ld.append(C.dma(e, "ds", out=masks[t % 2],
                                              in_=ins["mask0"][t]))

    if C.am is not None:
        absmean_finish_a(C)
    am_sc = C.am["ac_sc"] if C.am is not None else 0

    bank_guard2 = {b: ("ac", 0) for b in range(8)}
    ac_exp_hist = []
    gp_nrm_hist = []
    pe_trans_hist = []
    def guard2(e, b):
        kind, n = bank_guard2[b]
        if n:
            e.wait_ge({"ac": C.ac, "dv": C.dv, "gp": C.gp}[kind].h, n)

    p0_pend = {}

    def p0_A(t, h):
        it = t * 16 + h
        sb = (it % 2) * 2
        @block.tensor
        def _(e, t=t, h=h, sb=sb, qn=qt_ld[t], mn=mk_ld[t], it=it):
            e.wait_ge(C.da.h, qn)
            if it == 0 and am_sc:
                e.wait_ge(C.ac.h, am_sc)
            guard2(e, sb)
            guard2(e, sb + 1)
            lhsT = qt_sb[t % 2][:, h * P:(h + 1) * P]
            kvh = h // 4
            base = kvh * 1536 + t * P
            e.matmul(PS[:, sb * 512:(sb + 1) * 512], lhsT, kT[:, base:base + 512],
                     start=True, stop=True)
            C.pe.inc(e.matmul(PS[:, (sb + 1) * 512:(sb + 1) * 512 + P], lhsT,
                              kT[:, base + 512:base + 640], start=True, stop=True))
        pn = C.pe.n

        acg = ac_exp_hist[it - 2] if it >= 2 else 0
        gng = gp_nrm_hist[it - 2] if it >= 2 else 0
        @block.vector
        def _(e, t=t, it=it, sb=sb, pn=pn, mn=mk_ld[t], acg=acg, gng=gng):
            e.wait_ge(C.pe.h, pn)
            e.wait_ge(C.ds.h, mn)
            if acg:
                e.wait_ge(C.ac.h, acg)
            if gng:
                e.wait_ge(C.dv.h, gng)
            C.dv.inc(e.scalar_tensor_tensor(out=ssc[it % 2], in0=PS[:, sb * 512: sb * 512 + KW],
                                   scalar=INV_SQRT_HD, in1=masks[t % 2],
                                   op0=ALU.mult, op1=ALU.add))
            e.wait_ge(C.dv.h, C.dv.n)
            C.dv.inc(e.reduce_max(nmax[it % 2], ssc[it % 2], axis=AX.X, negate=True))
        dvn = C.dv.n
        bank_guard2[sb] = ("dv", dvn)
        bank_guard2[sb + 1] = ("dv", dvn)

        peg = pe_trans_hist[it - 2] if it >= 2 else 0
        @block.scalar
        def _(e, it=it, dvn=dvn, peg=peg):
            e.wait_ge(C.dv.h, dvn)
            if peg:
                e.wait_ge(C.pe.h, peg)
            C.ac.inc(e.activation(out=sat[it % 2], in_=ssc[it % 2], func=AF.Exp,
                                  bias=nmax[it % 2], scale=1.0, accum_out=ssum[it % 2]))
        acn = C.ac.n
        ac_exp_hist.append(acn)

        @block.vector
        def _(e, acn=acn, it=it):
            e.wait_ge(C.ac.h, acn)
            C.dv.inc(e.reciprocal(srec[it % 2], ssum[it % 2]))

        @block.vector
        def _(e, it=it):
            C.dv.inc(e.tensor_scalar(out=sat[it % 2], in0=sat[it % 2],
                                     scalar1=srec[it % 2], scalar2=None, op0=ALU.mult))
            e.wait_ge(C.dv.h, C.dv.n)
        gn = C.dv.n
        gp_nrm_hist.append(gn)
        p0_pend[it] = (pn, dvn, gn)

    def p0_B(t, h):
        it = t * 16 + h
        atb = 4 + it % 2
        avb = 6 + (it // 4) % 2
        pn, dvn, gn = p0_pend[it]

        @block.tensor
        def _(e, it=it, atb=atb, gn=gn):
            e.wait_ge(C.dv.h, gn)
            guard2(e, atb)
            for kc in range(5):
                view = PS[:, atb * 512:(atb + 1) * 512].bitcast(BF16)[:, kc * P:(kc + 1) * P]
                ins_ = e.transpose(view, sat[it % 2][:, kc * P:(kc + 1) * P], identb[:])
            C.pe.inc(ins_)
        pn2 = C.pe.n
        pe_trans_hist.append(pn2)

        @block.vector
        def _(e, it=it, atb=atb, pn2=pn2):
            e.wait_ge(C.pe.h, pn2)
            C.dv.inc(e.tensor_copy(satT[it % 2],
                                   PS[:, atb * 512:(atb + 1) * 512].bitcast(BF16)[:, 0:KW]))
        bank_guard2[atb] = ("dv", C.dv.n)
        an = C.dv.n

        @block.tensor
        def _(e, t=t, h=h, it=it, avb=avb, an=an):
            e.wait_ge(C.dv.h, an)
            if h % 4 == 0:
                guard2(e, avb)
            kvh = h // 4
            for kc in range(5):
                lhsT = vtok[:, (t + kc) * 512 + kvh * P: (t + kc) * 512 + (kvh + 1) * P]
                ins_ = e.matmul(PS[:, avb * 512 + (h % 4) * P: avb * 512 + (h % 4 + 1) * P],
                                lhsT, satT[it % 2][:, kc * P:(kc + 1) * P],
                                start=(kc == 0), stop=(kc == 4))
            C.pe.inc(ins_)
        pn3 = C.pe.n

        if h % 4 == 3:
            @block.vector
            def _(e, t=t, h=h, avb=avb, pn3=pn3):
                e.wait_ge(C.pe.h, pn3)
                hb = h - 3
                out = aoT[:].rearrange("p (c n) -> p c n", c=NCH)[:, hb:hb + 4, t * P:(t + 1) * P]
                C.dv.inc(e.tensor_copy(out, PS[:, avb * 512:(avb + 1) * 512].rearrange(
                    "p (c n) -> p c n", c=4)))
            bank_guard2[avb] = ("dv", C.dv.n)
        if h == 15 and t + 2 <= NT - 1:
            @block.scalar
            def _(e, t=t, pn=pn, dvn=dvn):
                e.wait_ge(C.pe.h, pn)
                qt_ld.append(C.dma(e, "da",
                    out=qt_sb[t % 2].rearrange("p (c n) -> p c n", c=NCH),
                    in_=C.scr["l0_qT"].rearrange("c p n -> p c n")[:, :, (t + 2) * P:(t + 3) * P]))
            @block.sync
            def _(e, t=t, dvn=dvn):
                e.wait_ge(C.dv.h, dvn)
                mk_ld.append(C.dma(e, "ds", out=masks[t % 2], in_=ins["mask0"][t + 2]))

    its = [(t, h) for t in range(NT) for h in range(16)]
    L0_PIPE = 1
    for i in range(len(its) + L0_PIPE):
        if i < len(its):
            p0_A(*its[i])
        if i >= L0_PIPE:
            p0_B(*its[i - L0_PIPE])
    ac_attn = C.ac.n
    dv_attn = C.dv.n
    wo_ld = []
    @block.gpsimd
    def _(e):
        e.wait_ge(C.dv.h, dv_attn)
        for i in range(2):
            wo_ld.append(C.dma(e, "dg", out=wst2[i % 2], in_=ins["wo_s"][i]))

    bg = {b: ("ac", 0) for b in range(8)}
    def guard3(e, b):
        kind, n = bg[b]
        if n:
            e.wait_ge({"ac": C.ac, "dv": C.dv}[kind].h, n)

    for oc in range(NCH):
        need = wo_ld[oc]
        pb = (oc % 2) * 2
        tb = 4 + (oc % 2)
        @block.tensor
        def _(e, oc=oc, need=need, pb=pb):
            e.wait_ge(C.dg.h, need)
            if oc == 0:
                e.wait_ge(C.dv.h, dv_attn)
            guard3(e, pb)
            guard3(e, pb + 1)
            for dc in range(NCH):
                lhsT = wst2[oc % 2][:, dc * P:(dc + 1) * P]
                for g in range(2):
                    rhs = aoT[:, dc * TOK + g * 512: dc * TOK + (g + 1) * 512]
                    ins_ = e.matmul(PS[:, (pb + g) * 512:(pb + g + 1) * 512], lhsT, rhs,
                                    start=(dc == 0), stop=(dc == NCH - 1))
            C.pe.inc(ins_)
        pn = C.pe.n
        if oc + 2 < NCH:
            @block.gpsimd
            def _(e, oc=oc, pn=pn):
                e.wait_ge(C.pe.h, pn)
                wo_ld.append(C.dma(e, "dg", out=wst2[oc % 2], in_=ins["wo_s"][oc + 2]))

        @block.scalar
        def _(e, oc=oc, pb=pb, pn=pn):
            e.wait_ge(C.pe.h, pn)
            C.ac.inc(e.copy(out=oev[oc % 2], in_=PS[:, pb * 512: pb * 512 + 1024]))
        bg[pb] = ("ac", C.ac.n)
        bg[pb + 1] = ("ac", C.ac.n)
        an = C.ac.n

        if L0_STAGE < 2.8:
            bg[pb] = ("ac", C.ac.n)
            continue
        @block.tensor
        def _(e, oc=oc, tb=tb, an=an):
            e.wait_ge(C.ac.h, an)
            guard3(e, tb)
            for tt in range(NT):
                view = PS[:, tb * 512:(tb + 1) * 512].bitcast(BF16)[:, tt * P:(tt + 1) * P]
                ins_ = e.transpose(view, oev[oc % 2][:, tt * P:(tt + 1) * P], identb[:])
            C.pe.inc(ins_)
        pn2 = C.pe.n

        @block.vector
        def _(e, oc=oc, tb=tb, pn2=pn2):
            e.wait_ge(C.pe.h, pn2)
            out = X[:].rearrange("p (t n) -> p t n", t=NT)[:, :, oc * P:(oc + 1) * P]
            src = PS[:, tb * 512:(tb + 1) * 512].bitcast(BF16).rearrange("p (t n) -> p t n", t=NT)
            if L0_STAGE < 2.9:
                C.dv.inc(e.tensor_copy(scr["xtmp"][:, 0:TOK], src.rearrange("p t n -> p (t n)")))
            else:
                C.dv.inc(e.tensor_tensor(out=out, in0=out, in1=src, op=ALU.add))
        bg[tb] = ("dv", C.dv.n)

    C.barrier()
    C.pop_scope()
    C.pop_scope()



def emit_moe(C, X, PS):
    nc, block = C.nc, C.block
    ins = C.ins
    C.push_scope()

    XNT1 = C.sbuf("m_xnT", [P, NCH * TOK], BF16)
    xnf = C.sbuf("m_xnf", [P, D], F32)
    identb = C.sbuf("m_ident", [P, P], BF16)
    identf = C.sbuf("m_identf", [P, P], F32)
    gwt = C.sbuf("m_gwt", [P, NCH * 32], F32)
    gwh = C.sbuf("m_gwh", [P, NCH * 32], BF16)
    gwl = C.sbuf("m_gwl", [P, NCH * 32], BF16)
    lg = C.sbuf("m_lg", [P, 32], F32)
    cur = C.sbuf("m_cur", [P, 32], F32)
    pex = C.sbuf("m_pex", [P, 32], F32)
    tbig = C.sbuf("m_tbig", [P, 32], F32)
    mi = C.sbuf("m_mi", [P, 1], F32)
    m1n = C.sbuf("m_m1n", [P, 1], F32)
    ssum = C.sbuf("m_ssum", [P, 1], F32)
    srec = C.sbuf("m_srec", [P, 1], F32)
    mw_all = C.sbuf("m_mw", [P, NT * 32], F32)
    mwT = C.sbuf("m_mwT", [32, NT * P], BF16)
    wg_sb = C.sbuf("m_wg", [P, NCH * 768], BF16)
    wu_sb = C.sbuf("m_wu", [P, NCH * 768], BF16)
    wd_sb = C.sbuf("m_wd", [P, 6 * D], BF16)
    sg = C.sbuf("m_sg", [P, 6 * 512], BF16)
    uev = C.sbuf("m_uev", [P, 6 * 512], BF16)
    hT = C.sbuf("m_hT", [P, 6 * 512], BF16)
    wb = C.sbuf("m_wb", [P, TOK], BF16)
    xnb = [sg[:, 0:D], sg[:, 0:D]]
    xnlo = hT[:, 0:D]
    xnloT = wd_sb[:, D:2 * D]
    scr2 = mk_ln_scr(C, "m_ln", dump_ap=wd_sb[:, 0:D], need_xtmp=False)
    mw_d = C.scr["moe_mw"]

    @block.gpsimd
    def _(e):
        C.dma(e, "dg", out=identb[:], in_=ins["ident"])
        C.dma(e, "dg", out=gwh[:], in_=ins["gwt"])
    @block.sync
    def _(e):
        C.dma(e, "ds", out=gwt[:], in_=ins["gwt"])
        C.dma(e, "ds", out=identf[:], in_=ins["ident"])
    dsn0 = C.ds.n
    dgn0 = C.dg.n

    @block.vector
    def _(e):
        e.wait_ge(C.ds.h, dsn0)
        e.wait_ge(C.dg.h, dgn0)
        C.dv.inc(e.tensor_tensor(out=gwl[:], in0=gwt[:], in1=gwh[:], op=ALU.subtract))
    dv_gw = C.dv.n

    rdy_hist = []
    pe_tr = []
    ac_ev = []
    pe_lo = [0] * NT
    ac_lgev = 0
    ac_mwT = 0
    dv_mw_prev = 0
    for t in range(NT):
        pre = []
        if t >= 2:
            k2, n2 = rdy_hist[t - 2]
            pre.append((C.dv, n2))
            pv = pe_tr[t - 2]
            @block.vector
            def _(e, pv=pv):
                e.wait_ge(C.pe.h, pv)
        kind, n = emit_ln_tile(C, X[:, t * D:(t + 1) * D], xnf[:], scr2[t % 2],
                               pre_wait=pre)
        rdy_hist.append((kind, n))
        plo = pe_lo[t - 1] if t >= 1 else 0
        @block.vector
        def _(e, t=t, plo=plo):
            if plo:
                e.wait_ge(C.pe.h, plo)
            C.dv.inc(e.tensor_copy(xnb[t % 2], xnf[:]))
            e.wait_ge(C.dv.h, C.dv.n)
            C.dv.inc(e.tensor_tensor(out=xnlo, in0=xnf[:], in1=xnb[t % 2], op=ALU.subtract))
        dv_hilo = C.dv.n

        prev_ev = ac_ev[t - 1] if t >= 1 else 0
        @block.tensor
        def _(e, t=t, dv_hilo=dv_hilo, prev_ev=prev_ev):
            e.wait_ge(C.dv.h, dv_hilo)
            e.wait_ge(C.ac.h, prev_ev)
            for dc in range(NCH):
                bb = dc // 8
                view = PS[:, bb * 512:(bb + 1) * 512].bitcast(BF16)[:, (dc % 8) * P:(dc % 8 + 1) * P]
                ins_ = e.transpose(view, xnb[t % 2][:, dc * P:(dc + 1) * P], identb[:])
            C.pe.inc(ins_)
            for dc in range(NCH):
                bb = 2 + dc // 8
                view = PS[:, bb * 512:(bb + 1) * 512].bitcast(BF16)[:, (dc % 8) * P:(dc % 8 + 1) * P]
                ins_ = e.transpose(view, xnlo[:, dc * P:(dc + 1) * P], identb[:])
            C.pe.inc(ins_)
        pe_tr.append(C.pe.n - 1)
        pe_lo[t] = C.pe.n
        pn = C.pe.n

        @block.scalar
        def _(e, t=t, pn=pn):
            e.wait_ge(C.pe.h, pn)
            for half in range(2):
                out = XNT1[:].rearrange("p (c n) -> p c n", c=NCH)[:, half * 8:(half + 1) * 8,
                                                                   t * P:(t + 1) * P]
                i2 = PS[:, half * 512:(half + 1) * 512].bitcast(BF16).rearrange("p (c n) -> p c n", c=8)
                ins_ = e.copy(out=out, in_=i2)
            for half in range(2):
                out = xnloT.rearrange("p (c n) -> p c n", c=NCH)[:, half * 8:(half + 1) * 8, :]
                i2 = PS[:, (2 + half) * 512:(3 + half) * 512].bitcast(BF16).rearrange("p (c n) -> p c n", c=8)
                ins_ = e.copy(out=out, in_=i2)
            C.ac.inc(ins_)
        ac_ev.append(C.ac.n)
        an_t = C.ac.n

        @block.tensor
        def _(e, t=t, an_t=an_t, dv_gw=dv_gw, ac_lgev=ac_lgev):
            e.wait_ge(C.ac.h, an_t)
            if t == 0:
                e.wait_ge(C.dv.h, dv_gw)
            if ac_lgev:
                e.wait_ge(C.ac.h, ac_lgev)
            combos = [(0, gwh), (0, gwl), (1, gwh)]
            nmm = len(combos) * NCH
            k = 0
            for (which, gw) in combos:
                for dc in range(NCH):
                    if which == 0:
                        lhsT = XNT1[:, dc * TOK + t * P: dc * TOK + (t + 1) * P]
                    else:
                        lhsT = xnloT[:, dc * P:(dc + 1) * P]
                    ins_ = e.matmul(PS[:, 4 * 512:4 * 512 + 32], lhsT, gw[:, dc * 32:(dc + 1) * 32],
                                    start=(k == 0), stop=(k == nmm - 1))
                    k += 1
            C.pe.inc(ins_)
        pn_r = C.pe.n

        @block.scalar
        def _(e, pn_r=pn_r, dvp=dv_mw_prev):
            e.wait_ge(C.pe.h, pn_r)
            if dvp:
                e.wait_ge(C.dv.h, dvp)
            C.ac.inc(e.copy(out=lg[:], in_=PS[:, 4 * 512:4 * 512 + 32]))
        ac_lgev = C.ac.n
        an_lg = C.ac.n

        def step(e, instr):
            C.dv.inc(instr)
            e.wait_ge(C.dv.h, C.dv.n)

        @block.vector
        def _(e, t=t, an_lg=an_lg):
            e.wait_ge(C.ac.h, an_lg)
            step(e, e.tensor_copy(cur[:], lg[:]))
            for i in range(8):
                step(e, e.reduce_max(mi[:], cur[:], axis=AX.X))
                if i == 0:
                    step(e, e.tensor_scalar(out=m1n[:], in0=mi[:], scalar1=-1.0, scalar2=None,
                                            op0=ALU.mult))
                if i < 7:
                    step(e, e.tensor_scalar(out=tbig[:], in0=cur[:], scalar1=mi[:],
                                            scalar2=1e30, op0=ALU.is_ge, op1=ALU.mult))
                    step(e, e.tensor_tensor(out=cur[:], in0=cur[:], in1=tbig[:], op=ALU.subtract))
        dv_t8 = C.dv.n

        @block.scalar
        def _(e, dv_t8=dv_t8):
            e.wait_ge(C.dv.h, dv_t8)
            C.ac.inc(e.activation(out=pex[:], in_=lg[:], func=AF.Exp, bias=m1n[:], scale=1.0))
        an_p = C.ac.n

        @block.vector
        def _(e, t=t, an_p=an_p):
            e.wait_ge(C.ac.h, an_p)
            C.dv.inc(e.scalar_tensor_tensor(out=mw_all[:, t * 32:(t + 1) * 32], in0=lg[:],
                                            scalar=mi[:], in1=pex[:], op0=ALU.is_ge,
                                            op1=ALU.mult, accum_out=ssum[:]))
            e.wait_ge(C.dv.h, C.dv.n)
            C.dv.inc(e.reciprocal(srec[:], ssum[:]))
            e.wait_ge(C.dv.h, C.dv.n)
            C.dv.inc(e.tensor_scalar(out=mw_all[:, t * 32:(t + 1) * 32],
                                     in0=mw_all[:, t * 32:(t + 1) * 32],
                                     scalar1=srec[:], scalar2=None, op0=ALU.mult))
            e.wait_ge(C.dv.h, C.dv.n)
        dv_mw = C.dv.n

        dv_mw_prev = dv_mw
        @block.tensor
        def _(e, t=t, dv_mw=dv_mw, ac_mwT=ac_mwT):
            e.wait_ge(C.dv.h, dv_mw)
            if ac_mwT:
                e.wait_ge(C.ac.h, ac_mwT)
            C.pe.inc(e.transpose(PS[0:32, 5 * 512:5 * 512 + P], mw_all[:, t * 32:(t + 1) * 32],
                                 identf[:]))
        pn_mw = C.pe.n

        @block.scalar
        def _(e, t=t, pn_mw=pn_mw):
            e.wait_ge(C.pe.h, pn_mw)
            C.ac.inc(e.copy(out=mwT[:, t * P:(t + 1) * P], in_=PS[0:32, 5 * 512:5 * 512 + P]))
        ac_mwT = C.ac.n

    @block.sync
    def _(e, ac_mwT=ac_mwT):
        e.wait_ge(C.ac.h, ac_mwT)
        C.dma(e, "ds", out=mw_d, in_=mwT[:])
    ds_mw = C.ds.n

    if MOE_STAGE < 2:
        if "dbg" in C.ins:
            @block.gpsimd
            def _(e, am=ac_mwT, ds_mw=ds_mw):
                e.wait_ge(C.ac.h, am)
                e.wait_ge(C.ds.h, ds_mw)
                C.dma(e, "dg", out=C.ins["dbg"], in_=XNT1[:])
                C.dma(e, "dg", out=C.ins["dbgk"].rearrange("p n -> p n")[0:32, 0:NT * P], in_=mwT[:])
        C.barrier()
        C.pop_scope()
        return

    wl = {}
    @block.gpsimd
    def _(e, ds_mw=ds_mw):
        e.wait_ge(C.ds.h, ds_mw)
        wl[("wg", 0)] = C.dma(e, "dg", out=wg_sb[:], in_=ins["wg_t"][0])
        wl[("wu", 0)] = C.dma(e, "dg", out=wu_sb[:], in_=ins["wu_t"][0])
        wl[("wd", 0)] = C.dma(e, "dg", out=wd_sb[:], in_=ins["wd_t"][0])
    wb_ld = {}
    @block.scalar
    def _(e):
        e.wait_ge(C.ds.h, ds_mw)
        wb_ld[0] = C.dma(e, "da", out=wb[:], in_=mw_d[0:1, :].partition_broadcast(P).squeeze(1))

    bank_dv = {b: 0 for b in range(8)}
    bank_ac = {b: 0 for b in range(6)}
    gp_hT = 0
    pe_g = {}
    pe_u = {}
    pe_dn = 0
    n_experts = 32 if MOE_STAGE >= 3 else 1
    for ei in range(n_experts):
        for blk in range(2):
            for mc in range(6):
                acfirst = ac_mwT if (ei == 0 and blk == 0) else 0
                @block.tensor
                def _(e, ei=ei, blk=blk, mc=mc, need=wl[("wg", ei)], g1=bank_ac[mc],
                      gh=gp_hT, acfirst=acfirst):
                    e.wait_ge(C.dg.h, need)
                    if acfirst:
                        e.wait_ge(C.ac.h, acfirst)
                    if g1:
                        e.wait_ge(C.ac.h, g1)
                    for dc in range(NCH):
                        lhsT = wg_sb[:, dc * 768 + mc * P: dc * 768 + (mc + 1) * P]
                        rhs = XNT1[:, dc * TOK + blk * 512: dc * TOK + (blk + 1) * 512]
                        ins_ = e.matmul(PS[:, mc * 512:(mc + 1) * 512], lhsT, rhs,
                                        start=(dc == 0), stop=(dc == NCH - 1))
                    C.pe.inc(ins_)
                pe_g[mc] = C.pe.n
                @block.scalar
                def _(e, mc=mc, pn=pe_g[mc], gh=gp_hT):
                    e.wait_ge(C.pe.h, pn)
                    if gh:
                        e.wait_ge(C.gp.h, gh)
                    C.ac.inc(e.activation(out=sg[:, mc * 512:(mc + 1) * 512],
                                          in_=PS[:, mc * 512:(mc + 1) * 512], func=AF.Silu))
                bank_ac[mc] = C.ac.n
            for mc in range(6):
                @block.tensor
                def _(e, ei=ei, blk=blk, mc=mc, need=wl[("wu", ei)], ga=bank_ac[mc]):
                    e.wait_ge(C.dg.h, need)
                    e.wait_ge(C.ac.h, ga)
                    for dc in range(NCH):
                        lhsT = wu_sb[:, dc * 768 + mc * P: dc * 768 + (mc + 1) * P]
                        rhs = XNT1[:, dc * TOK + blk * 512: dc * TOK + (blk + 1) * 512]
                        ins_ = e.matmul(PS[:, mc * 512:(mc + 1) * 512], lhsT, rhs,
                                        start=(dc == 0), stop=(dc == NCH - 1))
                    C.pe.inc(ins_)
                pe_u[mc] = C.pe.n
                @block.scalar
                def _(e, mc=mc, pn=pe_u[mc], gh=gp_hT):
                    e.wait_ge(C.pe.h, pn)
                    if gh:
                        e.wait_ge(C.gp.h, gh)
                    C.ac.inc(e.copy(out=uev[:, mc * 512:(mc + 1) * 512],
                                    in_=PS[:, mc * 512:(mc + 1) * 512]))
                bank_ac[mc] = C.ac.n
                bank_dv[mc] = 0
            pn_u = C.pe.n
            ac_uev = C.ac.n
            @block.vector
            def _(e, ac_uev=ac_uev):
                e.wait_ge(C.ac.h, ac_uev)
                C.dv.inc(e.tensor_tensor(out=sg[:], in0=sg[:], in1=uev[:], op=ALU.mult))
            dv_h1 = C.dv.n

            wbn = wb_ld[ei]
            @block.gpsimd
            def _(e, dv_h1=dv_h1, blk=blk, wbn=wbn, pdn=pe_dn):
                e.wait_ge(C.dv.h, dv_h1)
                e.wait_ge(C.da.h, wbn)
                if pdn:
                    e.wait_ge(C.pe.h, pdn)
                C.gp.inc(e.tensor_tensor(
                    out=hT[:].rearrange("p (c n) -> p c n", c=6),
                    in0=sg[:].rearrange("p (c n) -> p c n", c=6),
                    in1=wb[:, blk * 512:(blk + 1) * 512].unsqueeze(1).broadcast_to((P, 6, 512)),
                    op=ALU.mult))
            gp_hT = C.gp.n

            if blk == 1 and ei + 1 < n_experts:
                @block.gpsimd
                def _(e, ei=ei, pg=pe_u[5], pu=pe_u[5]):
                    e.wait_ge(C.pe.h, pu)
                    wl[("wg", ei + 1)] = C.dma(e, "dg", out=wg_sb[:], in_=ins["wg_t"][ei + 1])
                    wl[("wu", ei + 1)] = C.dma(e, "dg", out=wu_sb[:], in_=ins["wu_t"][ei + 1])
                @block.scalar
                def _(e, ei=ei, gh=gp_hT):
                    e.wait_ge(C.gp.h, gh)
                    wb_ld[ei + 1] = C.dma(e, "da", out=wb[:],
                                          in_=mw_d[ei + 1:ei + 2, :].partition_broadcast(P).squeeze(1))

            for ti, (s, n_) in enumerate([(s, n_) for s in range(4) for n_ in range(4)]):
                bk = 6 + ti % 2
                @block.tensor
                def _(e, ei=ei, blk=blk, s=s, n_=n_, bk=bk, gh=gp_hT, need=wl[("wd", ei)],
                      gd=bank_dv[bk]):
                    e.wait_ge(C.gp.h, gh)
                    e.wait_ge(C.dg.h, need)
                    if gd:
                        e.wait_ge(C.dv.h, gd)
                    for mc in range(6):
                        lhsT = hT[:, mc * 512 + s * P: mc * 512 + (s + 1) * P]
                        rhs = wd_sb[:, mc * D + n_ * 512: mc * D + (n_ + 1) * 512]
                        ins_ = e.matmul(PS[:, bk * 512:(bk + 1) * 512], lhsT, rhs,
                                        start=(mc == 0), stop=(mc == 5))
                    C.pe.inc(ins_)
                pn_t = C.pe.n

                @block.vector
                def _(e, blk=blk, s=s, n_=n_, bk=bk, pn_t=pn_t):
                    e.wait_ge(C.pe.h, pn_t)
                    xs = X[:, (blk * 4 + s) * D + n_ * 512: (blk * 4 + s) * D + (n_ + 1) * 512]
                    C.dv.inc(e.tensor_tensor(out=xs, in0=xs, in1=PS[:, bk * 512:(bk + 1) * 512],
                                             op=ALU.add))
                bank_dv[bk] = C.dv.n
            pe_dn = C.pe.n
            if blk == 1 and ei + 1 < n_experts:
                @block.gpsimd
                def _(e, ei=ei, pdn=pe_dn):
                    e.wait_ge(C.pe.h, pdn)
                    wl[("wd", ei + 1)] = C.dma(e, "dg", out=wd_sb[:], in_=ins["wd_t"][ei + 1])

    C.barrier()
    C.pop_scope()




def emit_l1attn(C, X, PS):
    nc, block = C.nc, C.block
    ins = C.ins
    C.push_scope()

    XNT2 = C.sbuf("a_xnT", [P, NCH * TOK], BF16)
    identb = C.sbuf("a_ident", [P, P], BF16)
    onesc = C.sbuf("a_onesc", [P, 1], BF16)
    onesr = C.sbuf("a_onesr", [1, P], BF16)
    onesrf = C.sbuf("a_onesrf", [1, P], F32)
    xn2 = dbuf(C, "a_xn2", D, BF16)
    wst = dbuf(C, "a_wst", NCH * P, BF16)
    pstage = dbuf(C, "a_pstage", TOK, BF16)
    slope_sb = C.sbuf("a_slope", [1, 16], F32)
    sbc = C.sbuf("a_sbc", [P, 16], F32)
    arq = C.sbuf("a_arq", [1, 256], F32)
    ark = C.sbuf("a_ark", [P, 2], F32)
    arkp = C.sbuf("a_arkp", [P, 2], F32)
    tcn = C.sbuf("a_tcn", [1, 8], F32)
    kdec = C.sbuf("a_kdec", [P, 32], F32)
    expk = C.sbuf("a_expk", [P, 32], F32)
    causal = C.sbuf("a_causal", [P, 512], BF16)
    bdh = C.sbuf("a_bdh", [P, 16], F32)
    tmp256 = C.sbuf("a_t256", [1, 256], F32)
    qdec_bc = C.sbuf("a_qdecbc", [P, 256], BF16)
    coefs = C.sbuf("a_coefs", [1, P], F32)
    coef_bc = C.sbuf("a_coefbc", [P, P], F32)
    kv = C.sbuf("a_kv", [P, 16 * P], F32)
    S_out = C.sbuf("a_Sout", [P, 16 * P], F32)
    S_in = C.sbuf("a_Sin", [P, 16 * P], F32)
    qblk = C.sbuf("a_qblk", [P, 16 * 256], BF16)
    kblk = C.sbuf("a_kblk", [P, 16 * 256], BF16)
    vblk = C.sbuf("a_vblk", [P, 16 * 256], BF16)
    gblk = C.sbuf("a_gblk", [P, 16 * 256], BF16)
    S_r = qblk[:].bitcast(F32)
    outb = C.sbuf("a_outb", [P, 16 * 256], BF16)
    prodT = outb
    ktok2p = dbuf(C, "a_ktok", 256, BF16)
    vtok2p = dbuf(C, "a_vtok", 256, BF16)
    kdki2p = dbuf(C, "a_kdki", 256, BF16)
    kvbf2p = dbuf(C, "a_kvbf", P, BF16)
    qkd2p = dbuf(C, "a_qkd", 512, BF16)
    ktok, vtok2, kdki, kvbf, qkd = (ktok2p[0], vtok2p[0], kdki2p[0], kvbf2p[0], qkd2p[0])
    sq_sb = C.sbuf("a_sq", [P, 256], BF16)
    sq_sb2 = C.sbuf("a_sq2", [P, 256], BF16)
    sqp = [sq_sb, sq_sb2]
    rr = C.sbuf("a_rr", [1, 256], F32)
    rstd_bc = C.sbuf("a_rstdbc", [P, 256], F32)
    normw = C.sbuf("a_normw", [P, 16], F32)
    wrow = C.sbuf("a_wrow", [P, D], BF16)
    wrow2 = C.sbuf("a_wrow2", [P, D], BF16)
    scr2 = mk_ln_scr(C, "a_ln", need_xtmp=False)

    qd_d, kd_d, vd_d, gd_d = (C.scr["l1_q"], C.scr["l1_k"], C.scr["l1_v"], C.scr["l1_g"])
    wot_bf = C.scr["l1_wotbf"]
    ccin, ccag = C.scr["l1_ccin"], C.scr["l1_ccag"]

    @block.gpsimd
    def _(e):
        C.dma(e, "dg", out=identb[:], in_=ins["ident"])
        C.dma(e, "dg", out=onesc[:], in_=ins["ones_col"])
        C.dma(e, "dg", out=onesr[:], in_=ins["ones_row"])
        C.dma(e, "dg", out=causal[:], in_=ins["causal01"])
    @block.sync
    def _(e):
        C.dma(e, "ds", out=slope_sb[:], in_=ins["slope"])
        C.dma(e, "ds", out=onesrf[:], in_=ins["ones_row"])
        C.dma(e, "ds", out=arq[:], in_=ins["arqneg"])
        C.dma(e, "ds", out=ark[:], in_=ins["arkneg"])
        C.dma(e, "ds", out=arkp[:], in_=ins["arkpos"])
        C.dma(e, "ds", out=tcn[:], in_=ins["tcneg"])
        C.dma(e, "ds", out=normw[:], in_=ins["normw_col"])
    dsn0 = C.ds.n
    dgn0 = C.dg.n

    if C.am is not None:
        absmean_finish_b(C, onesrf[:], dsn0)

    @block.tensor
    def _(e):
        e.wait_ge(C.ds.h, dsn0)
        e.wait_ge(C.dg.h, dgn0)
        if C.am is not None:
            e.wait_ge(C.ac.h, C.am["ac_t"])
        C.pe.inc(e.matmul(PS[:, 0:16], onesrf[:], slope_sb[:], start=True, stop=True))
    pn0 = C.pe.n
    @block.scalar
    def _(e, pn0=pn0):
        e.wait_ge(C.pe.h, pn0)
        C.ac.inc(e.copy(out=sbc[:], in_=PS[:, 0:16]))
    ac_sbc = C.ac.n

    for h in range(16):
        @block.vector
        def _(e, h=h, ac_sbc=ac_sbc):
            if h == 0:
                e.wait_ge(C.ac.h, ac_sbc)
            C.dv.inc(e.tensor_scalar(out=kdec[:, h * 2:(h + 1) * 2], in0=ark[:],
                                     scalar1=sbc[:, h:h + 1], scalar2=None, op0=ALU.mult))
            e.wait_ge(C.dv.h, C.dv.n)
            C.dv.inc(e.tensor_scalar(out=expk[:, h * 2:(h + 1) * 2], in0=arkp[:],
                                     scalar1=sbc[:, h:h + 1], scalar2=None, op0=ALU.mult))
            e.wait_ge(C.dv.h, C.dv.n)
            C.dv.inc(e.tensor_scalar(out=bdh[:, h:h + 1], in0=sbc[:, h:h + 1], scalar1=-256.0,
                                     scalar2=None, op0=ALU.mult))
            e.wait_ge(C.dv.h, C.dv.n)
        dvh = C.dv.n
        @block.scalar
        def _(e, h=h, dvh=dvh):
            e.wait_ge(C.dv.h, dvh)
            e.activation(out=kdec[:, h * 2:(h + 1) * 2], in_=kdec[:, h * 2:(h + 1) * 2], func=AF.Exp)
            e.activation(out=expk[:, h * 2:(h + 1) * 2], in_=expk[:, h * 2:(h + 1) * 2], func=AF.Exp)
            C.ac.inc(e.activation(out=bdh[:, h:h + 1], in_=bdh[:, h:h + 1], func=AF.Exp))
    ac_tab = C.ac.n

    @block.vector
    def _(e, ac_tab=ac_tab):
        e.wait_ge(C.ac.h, ac_tab)
        for h in range(16):
            ins_ = e.tensor_scalar(out=coefs[0:1, h * 8:(h + 1) * 8], in0=tcn[:],
                                   scalar1=slope_sb[0:1, h:h + 1], scalar2=None, op0=ALU.mult)
        C.dv.inc(ins_)
    dv_cf = C.dv.n
    @block.scalar
    def _(e, dv_cf=dv_cf):
        e.wait_ge(C.dv.h, dv_cf)
        C.ac.inc(e.activation(out=coefs[:], in_=coefs[:], func=AF.Exp))
    ac_cf = C.ac.n
    @block.tensor
    def _(e, ac_cf=ac_cf):
        e.wait_ge(C.ac.h, ac_cf)
        C.pe.inc(e.matmul(PS[:, 512:640], onesrf[:], coefs[:], start=True, stop=True))
    pn_cf = C.pe.n
    @block.scalar
    def _(e, pn_cf=pn_cf):
        e.wait_ge(C.pe.h, pn_cf)
        C.ac.inc(e.copy(out=coef_bc[:], in_=PS[:, 512:640]))
    ac_cfb = C.ac.n

    @block.vector
    def _(e):
        e.memset(S_out[:], 0.0)
        C.dv.inc(e.memset(S_in[:], 0.0))
    dv_z = C.dv.n

    wot_ld0 = {}
    wot_sp0 = {}

    def emit_precast(oc):
        @block.gpsimd
        def _(e, oc=oc):
            if oc >= 2:
                e.wait_ge(C.ds.h, wot_sp0[oc - 2])
            wot_ld0[oc] = C.dma(e, "dg", out=wrow[:] if oc % 2 == 0 else wrow2[:],
                                in_=ins["wot_r"][oc])
        @block.sync
        def _(e, oc=oc):
            e.wait_ge(C.dg.h, wot_ld0[oc])
            C.dma(e, "ds", out=wot_bf[oc], in_=wrow[:] if oc % 2 == 0 else wrow2[:])
            wot_sp0[oc] = C.ds.n

    rdy_hist = []
    pe_tr = []
    ac_ev = []
    for t in range(NT):
        pre = []
        if t >= 2:
            k2, n2 = rdy_hist[t - 2]
            pre.append((C.dv, n2))
            pv = pe_tr[t - 2]
            @block.vector
            def _(e, pv=pv):
                e.wait_ge(C.pe.h, pv)
        kind, n = emit_ln_tile(C, X[:, t * D:(t + 1) * D], xn2[t % 2], scr2[t % 2],
                               pre_wait=pre)
        rdy_hist.append((kind, n))
        prev_ev = ac_ev[t - 1] if t >= 1 else 0
        @block.tensor
        def _(e, t=t, n=n, prev_ev=prev_ev):
            e.wait_ge(C.dv.h, n)
            e.wait_ge(C.ac.h, prev_ev)
            for dc in range(NCH):
                bb = 2 + dc // 8
                view = PS[:, bb * 512:(bb + 1) * 512].bitcast(BF16)[:, (dc % 8) * P:(dc % 8 + 1) * P]
                ins_ = e.transpose(view, xn2[t % 2][:, dc * P:(dc + 1) * P], identb[:])
            C.pe.inc(ins_)
        pe_tr.append(C.pe.n)
        pn = C.pe.n
        @block.scalar
        def _(e, t=t, pn=pn):
            e.wait_ge(C.pe.h, pn)
            for half in range(2):
                out = XNT2[:].rearrange("p (c n) -> p c n", c=NCH)[:, half * 8:(half + 1) * 8,
                                                                   t * P:(t + 1) * P]
                i2 = PS[:, (2 + half) * 512:(3 + half) * 512].bitcast(BF16).rearrange(
                    "p (c n) -> p c n", c=8)
                ins_ = e.copy(out=out, in_=i2)
            C.ac.inc(ins_)
        ac_ev.append(C.ac.n)
    ac_xnT = C.ac.n

    wjobs = [("wqkv_s", oc, qd_d if oc < 16 else (kd_d if oc < 32 else vd_d),
              oc % 16, AF.Silu) for oc in range(48)]
    wjobs += [("wgate_s", oc, gd_d, oc, AF.Sigmoid) for oc in range(16)]
    gp_pq = {}
    prev_pnq = [0]
    wld = []
    @block.gpsimd
    def _(e):
        for i in range(2):
            wname, oc, _, _, _ = wjobs[i]
            wld.append(C.dma(e, "dg", out=wst[i % 2], in_=ins[wname][oc]))
    ac_pst = {}
    ds_spill = {}

    def emit_wjob(ji):
        wname, oc, dst, slot, func = wjobs[ji]
        pb = 4 + ji % 2
        guard = ac_pst.get(ji - 2, 0)
        @block.tensor
        def _(e, ji=ji, oc=oc, pb=pb, need=wld[ji], guard=guard):
            e.wait_ge(C.dg.h, need)
            if ji == 0:
                e.wait_ge(C.ac.h, ac_xnT)
            if guard:
                e.wait_ge(C.ac.h, guard)
            for blk in range(4):
                for dc in range(NCH):
                    lhsT = wst[ji % 2][:, dc * P:(dc + 1) * P]
                    rhs = XNT2[:, dc * TOK + blk * 256: dc * TOK + (blk + 1) * 256]
                    ins_ = e.matmul(PS[:, pb * 512 + blk * 256: pb * 512 + blk * 256 + 256]
                                    if blk < 2 else
                                    PS[:, (pb + 2) * 512 + (blk - 2) * 256:
                                       (pb + 2) * 512 + (blk - 2) * 256 + 256],
                                    lhsT, rhs, start=(dc == 0), stop=(dc == NCH - 1))
            C.pe.inc(ins_)
        pn = C.pe.n
        if ji + 2 < len(wjobs):
            @block.gpsimd
            def _(e, ji=ji, pn=pn):
                e.wait_ge(C.pe.h, pn)
                wname2, oc2, _, _, _ = wjobs[ji + 2]
                wld.append(C.dma(e, "dg", out=wst[ji % 2], in_=ins[wname2][oc2]))
        is_q = (wname == "wqkv_s" and oc < 16)
        if is_q:
            @block.vector
            def _(e, oc=oc, pq=prev_pnq[0]):
                if pq:
                    e.wait_ge(C.pe.h, pq)
                C.dv.inc(e.tensor_scalar(out=tmp256[:], in0=arq[:],
                                         scalar1=slope_sb[0:1, oc:oc + 1], scalar2=None,
                                         op0=ALU.mult))
                e.wait_ge(C.dv.h, C.dv.n)
            dvq = C.dv.n
            @block.scalar
            def _(e, dvq=dvq):
                e.wait_ge(C.dv.h, dvq)
                C.ac.inc(e.activation(out=tmp256[:], in_=tmp256[:], func=AF.Exp))
            acq = C.ac.n
            @block.tensor
            def _(e, acq=acq, gq=gp_pq.get(ji - 1, 0)):
                e.wait_ge(C.ac.h, acq)
                if gq:
                    e.wait_ge(C.gp.h, gq)
                C.pe.inc(e.matmul(PS[:, 0:256], onesrf[:], tmp256[:], start=True, stop=True))
            pnq = C.pe.n
            prev_pnq[0] = pnq
            @block.scalar
            def _(e, pnq=pnq):
                e.wait_ge(C.pe.h, pnq)
                C.ac.inc(e.copy(out=qdec_bc[:], in_=PS[:, 0:256]))
            ac_qd = C.ac.n

        @block.scalar
        def _(e, ji=ji, pb=pb, pn=pn, func=func, sp=ds_spill.get(ji - 2, 0)):
            e.wait_ge(C.pe.h, pn)
            if sp:
                e.wait_ge(C.ds.h, sp)
            e.activation(out=pstage[ji % 2][:, 0:512], in_=PS[:, pb * 512:(pb + 1) * 512], func=func)
            C.ac.inc(e.activation(out=pstage[ji % 2][:, 512:1024],
                                  in_=PS[:, (pb + 2) * 512:(pb + 3) * 512], func=func))
        ac_pst[ji] = C.ac.n
        an = C.ac.n
        if is_q:
            @block.gpsimd
            def _(e, ji=ji, an=an, ac_qd=ac_qd):
                e.wait_ge(C.ac.h, an)
                e.wait_ge(C.ac.h, ac_qd)
                C.gp.inc(e.tensor_tensor(
                    out=pstage[ji % 2][:].rearrange("p (b n) -> p b n", b=4),
                    in0=pstage[ji % 2][:].rearrange("p (b n) -> p b n", b=4),
                    in1=qdec_bc[:].unsqueeze(1).broadcast_to((P, 4, 256)), op=ALU.mult))
            gp_pq[ji] = C.gp.n
            an = ("gp", C.gp.n)
        else:
            an = ("ac", an)
        @block.sync
        def _(e, ji=ji, dst=dst, slot=slot, an=an):
            e.wait_ge({"ac": C.ac, "gp": C.gp}[an[0]].h, an[1])
            C.dma(e, "ds", out=dst[slot], in_=pstage[ji % 2])
        ds_spill[ji] = C.ds.n

    for ji in range(48):
        emit_wjob(ji)
        if ji % 3 == 0 and ji // 3 < NCH:
            emit_precast(ji // 3)
    ds_wotbf = C.ds.n

    ld_blk = {}
    @block.scalar
    def _(e):
        e.wait_ge(C.ds.h, ds_spill[47])
        ld_blk[("k", 0)] = C.dma(e, "da", out=kblk[:].rearrange("p (c n) -> p c n", c=16),
                                 in_=kd_d.rearrange("c p n -> p c n")[:, :, 0:256])
        ld_blk[("v", 0)] = C.dma(e, "da", out=vblk[:].rearrange("p (c n) -> p c n", c=16),
                                 in_=vd_d.rearrange("c p n -> p c n")[:, :, 0:256])
    p1_kt_ac = [0, 0]
    p1_kd_dv = [0, 0]
    p1_stt_dv = [0, 0]
    p1_pend = {}
    p1_dv = 0

    def p1_A(blk, h):
        par = h % 2
        ktok, vtok2, kdki = ktok2p[par], vtok2p[par], kdki2p[par]
        TB = par * 512
        @block.tensor
        def _(e, blk=blk, h=h, par=par, need=ld_blk[("v", blk)], dz=dv_z, TB=TB,
              ga=p1_kt_ac[par], cfb=ac_cfb):
            e.wait_ge(C.da.h, need)
            if ga:
                e.wait_ge(C.ac.h, ga)
            if blk == 0 and h == 0:
                e.wait_ge(C.dv.h, dz)
            if blk == 0 and h < 2:
                e.wait_ge(C.ac.h, cfb)
            for sub in range(2):
                view = PS[:, TB:TB + 512].bitcast(BF16)[:, sub * P:(sub + 1) * P]
                ins_ = e.transpose(view, kblk[:, h * 256 + sub * P: h * 256 + (sub + 1) * P],
                                   identb[:])
            for sub in range(2):
                view = PS[:, TB:TB + 512].bitcast(BF16)[:, (2 + sub) * P:(3 + sub) * P]
                ins_ = e.transpose(view, vblk[:, h * 256 + sub * P: h * 256 + (sub + 1) * P],
                                   identb[:])
            C.pe.inc(ins_)
        pn = C.pe.n
        @block.scalar
        def _(e, pn=pn, gd=p1_kd_dv[par], ktok=ktok, vtok2=vtok2, TB=TB):
            e.wait_ge(C.pe.h, pn)
            if gd:
                e.wait_ge(C.dv.h, gd)
            e.copy(out=ktok[:], in_=PS[:, TB:TB + 512].bitcast(BF16)[:, 0:256])
            C.ac.inc(e.copy(out=vtok2[:], in_=PS[:, TB:TB + 512].bitcast(BF16)[:, 256:512]))
        p1_kt_ac[par] = C.ac.n
        an = C.ac.n
        @block.vector
        def _(e, h=h, an=an, ktok=ktok, kdki=kdki):
            e.wait_ge(C.ac.h, an)
            for sub in range(2):
                ins_ = e.tensor_scalar(out=kdki[:, sub * P:(sub + 1) * P],
                                       in0=ktok[:, sub * P:(sub + 1) * P],
                                       scalar1=kdec[:, h * 2 + sub:h * 2 + sub + 1],
                                       scalar2=None, op0=ALU.mult)
            C.dv.inc(ins_)
        p1_kd_dv[par] = C.dv.n
        p1_pend[h] = C.dv.n

    def p1_B(blk, h):
        par = h % 2
        vtok2, kdki = vtok2p[par], kdki2p[par]
        CB = (2 + par) * 512
        @block.tensor
        def _(e, h=h, dvk=p1_pend[h], gd=p1_stt_dv[par], vtok2=vtok2, kdki=kdki, CB=CB):
            e.wait_ge(C.dv.h, dvk)
            if gd:
                e.wait_ge(C.dv.h, gd)
            for sub in range(2):
                ins_ = e.matmul(PS[:, CB:CB + P], kdki[:, sub * P:(sub + 1) * P],
                                vtok2[:, sub * P:(sub + 1) * P], start=(sub == 0), stop=(sub == 1))
            C.pe.inc(ins_)
        pnc = C.pe.n
        @block.vector
        def _(e, h=h, pnc=pnc, CB=CB):
            e.wait_ge(C.pe.h, pnc)
            C.dv.inc(e.scalar_tensor_tensor(out=S_out[:, h * P:(h + 1) * P],
                                            in0=S_out[:, h * P:(h + 1) * P],
                                            scalar=bdh[:, h:h + 1], in1=PS[:, CB:CB + P],
                                            op0=ALU.mult, op1=ALU.add))
        p1_stt_dv[par] = C.dv.n

    for blk in range(4):
        for h in range(17):
            if h < 16:
                p1_A(blk, h)
            if h >= 1:
                p1_B(blk, h - 1)
        p1_dv = C.dv.n
        if blk + 1 < 4:
            @block.scalar
            def _(e, blk=blk, pa=C.pe.n):
                e.wait_ge(C.pe.h, pa)
                ld_blk[("k", blk + 1)] = C.dma(e, "da", out=kblk[:].rearrange("p (c n) -> p c n", c=16),
                                               in_=kd_d.rearrange("c p n -> p c n")[:, :, (blk + 1) * 256:(blk + 2) * 256])
                ld_blk[("v", blk + 1)] = C.dma(e, "da", out=vblk[:].rearrange("p (c n) -> p c n", c=16),
                                               in_=vd_d.rearrange("c p n -> p c n")[:, :, (blk + 1) * 256:(blk + 2) * 256])

    @block.sync
    def _(e, pd=p1_dv):
        e.wait_ge(C.dv.h, pd)
        C.dma(e, "ds", out=ccin.rearrange("(h p) d -> p h d", p=P),
              in_=S_out[:].rearrange("p (h d) -> p h d", h=16))
    ds_cc = C.ds.n
    cc_box = [0]
    sin_box = [0]

    def emit_cc():
        @block.gpsimd
        def _(e, ds_cc=ds_cc):
            e.wait_ge(C.ds.h, ds_cc)
            C.cc.inc(e.collective_compute("AllGather", ALU.bypass,
                                          replica_groups=[list(range(NCORES))],
                                          ins=[ccin.opt()], outs=[ccag.opt()]))
        cc_box[0] = C.cc.n

    def emit_sin_round(r):
        @block.gpsimd
        def _(e, r=r, cc_n=cc_box[0], pd=sin_box[0]):
            if r == 0:
                e.wait_ge(C.cc.h, cc_n)
            if pd:
                e.wait_ge(C.dv.h, pd)
            C.dma(e, "dg", out=S_r[:].rearrange("p (h d) -> p h d", h=16),
                  in_=ccag[r * 2048:(r + 1) * 2048, :].rearrange("(h p) d -> p h d", p=P))
        dg_r = C.dg.n
        @block.vector
        def _(e, r=r, dg_r=dg_r, ac_cfb=ac_cfb):
            e.wait_ge(C.dg.h, dg_r)
            if r == 0:
                e.wait_ge(C.ac.h, ac_cfb)
            for h in range(16):
                C.dv.inc(e.scalar_tensor_tensor(out=S_in[:, h * P:(h + 1) * P],
                                                in0=S_r[:, h * P:(h + 1) * P],
                                                scalar=coef_bc[:, h * 8 + r:h * 8 + r + 1],
                                                in1=S_in[:, h * P:(h + 1) * P],
                                                op0=ALU.mult, op1=ALU.add))
        sin_box[0] = C.dv.n

    for ji in range(48, 64):
        emit_wjob(ji)
        if ji == 49:
            emit_cc()
        if ji >= 56:
            emit_sin_round(ji - 56)
    dv_sin = sin_box[0]

    @block.vector
    def _(e, dv_sin=dv_sin):
        C.dv.inc(e.tensor_copy(kv[:], S_in[:]))
        e.wait_ge(C.dv.h, C.dv.n)
    dv_kv0 = C.dv.n

    ld2 = {}
    @block.scalar
    def _(e, dv_kv0=dv_kv0, sp63=ds_spill[63]):
        e.wait_ge(C.dv.h, dv_kv0)
        e.wait_ge(C.ds.h, sp63)
        for nm, dd in (("q", qd_d), ("k", kd_d), ("v", vd_d), ("g", gd_d)):
            ld2[(nm, 0)] = C.dma(e, "da", out={"q": qblk, "k": kblk, "v": vblk, "g": gblk}[nm][:]
                                 .rearrange("p (c n) -> p c n", c=16),
                                 in_=dd.rearrange("c p n -> p c n")[:, :, 0:256])
    wo_ld = {}
    @block.scalar
    def _(e, ds_wotbf=ds_wotbf):
        e.wait_ge(C.ds.h, ds_wotbf)
        wo_ld[0] = (C.da, C.dma(e, "da", out=wrow[:], in_=wot_bf[0]))
    @block.gpsimd
    def _(e, ds_wotbf=ds_wotbf):
        e.wait_ge(C.ds.h, ds_wotbf)
        wo_ld[1] = (C.dg, C.dma(e, "dg", out=wrow2[:], in_=wot_bf[1]))

    g_out_ac = [0, 0]
    g_qkd_dv = [0, 0]
    g_kt_ac = [0, 0]
    kv_stt_n = [dv_kv0] * 16
    stt_par = [0, 0]
    pe_h_hist = [0, 0]
    dv_add_prev = 0
    an_rb = 0
    p2_pend = {}

    def p2_A(blk, h, an_rb_cur, dv_add):
        par = h % 2
        ktok, vtok2, kdki, kvbf, qkd = (ktok2p[par], vtok2p[par], kdki2p[par],
                                        kvbf2p[par], qkd2p[par])
        QB = (1 if par == 0 else 6) * 512
        TB = (4 if par == 0 else 3) * 512

        @block.scalar
        def _(e, h=h, kvn=kv_stt_n[h], ph=pe_h_hist[par], kvbf=kvbf):
            if kvn:
                e.wait_ge(C.dv.h, kvn)
            if ph:
                e.wait_ge(C.pe.h, ph)
            C.ac.inc(e.copy(out=kvbf[:], in_=kv[:, h * P:(h + 1) * P]))
        ac_kvbf = C.ac.n

        @block.tensor
        def _(e, blk=blk, h=h, gq=g_qkd_dv[par], needk=ld2[("k", blk)],
              needq=ld2[("q", blk)], gadd=dv_add, QB=QB):
            e.wait_ge(C.da.h, needk)
            e.wait_ge(C.da.h, needq)
            if h < 2 and gadd:
                e.wait_ge(C.dv.h, gadd)
            if gq:
                e.wait_ge(C.dv.h, gq)
            for sub in range(2):
                lhsT = kblk[:, h * 256 + sub * P: h * 256 + (sub + 1) * P]
                ins_ = e.matmul(PS[:, QB + sub * 256: QB + (sub + 1) * 256], lhsT,
                                qblk[:, h * 256:(h + 1) * 256], start=True, stop=True)
            C.pe.inc(ins_)
        pn_qk = C.pe.n

        @block.vector
        def _(e, h=h, pn_qk=pn_qk, qkd=qkd, QB=QB):
            e.wait_ge(C.pe.h, pn_qk)
            for sub in range(2):
                ins_ = e.scalar_tensor_tensor(
                    out=qkd[:, sub * 256:(sub + 1) * 256],
                    in0=PS[:, QB + sub * 256: QB + (sub + 1) * 256],
                    scalar=expk[:, h * 2 + sub:h * 2 + sub + 1],
                    in1=causal[:, sub * 256:(sub + 1) * 256],
                    op0=ALU.mult, op1=ALU.mult)
            C.dv.inc(ins_)
        g_qkd_dv[par] = C.dv.n
        dv_qkd = C.dv.n

        rbg = an_rb_cur if (par == 1 and h == 1) else 0
        @block.tensor
        def _(e, blk=blk, h=h, needv=ld2[("v", blk)], ga=g_kt_ac[par], TB=TB, rbg=rbg,
              gadd=dv_add):
            e.wait_ge(C.da.h, needv)
            if ga:
                e.wait_ge(C.ac.h, ga)
            if h < 2 and gadd:
                e.wait_ge(C.dv.h, gadd)
            if rbg:
                e.wait_ge(C.ac.h, rbg)
            for sub in range(2):
                view = PS[:, TB:TB + 512].bitcast(BF16)[:, sub * P:(sub + 1) * P]
                e.transpose(view, kblk[:, h * 256 + sub * P: h * 256 + (sub + 1) * P], identb[:])
            for sub in range(2):
                view = PS[:, TB:TB + 512].bitcast(BF16)[:, (2 + sub) * P:(3 + sub) * P]
                ins_ = e.transpose(view, vblk[:, h * 256 + sub * P: h * 256 + (sub + 1) * P],
                                   identb[:])
            C.pe.inc(ins_)
        pn_t = C.pe.n
        @block.scalar
        def _(e, pn_t=pn_t, ktok=ktok, vtok2=vtok2, TB=TB, gd=p2_kd_dv[par]):
            e.wait_ge(C.pe.h, pn_t)
            if gd:
                e.wait_ge(C.dv.h, gd)
            e.copy(out=ktok[:], in_=PS[:, TB:TB + 512].bitcast(BF16)[:, 0:256])
            C.ac.inc(e.copy(out=vtok2[:], in_=PS[:, TB:TB + 512].bitcast(BF16)[:, 256:512]))
        g_kt_ac[par] = C.ac.n
        an_kt = C.ac.n

        @block.vector
        def _(e, h=h, an_kt=an_kt, ktok=ktok, kdki=kdki):
            e.wait_ge(C.ac.h, an_kt)
            for sub in range(2):
                ins_ = e.tensor_scalar(out=kdki[:, sub * P:(sub + 1) * P],
                                       in0=ktok[:, sub * P:(sub + 1) * P],
                                       scalar1=kdec[:, h * 2 + sub:h * 2 + sub + 1],
                                       scalar2=None, op0=ALU.mult)
            C.dv.inc(ins_)
        p2_kd_dv[par] = C.dv.n
        p2_pend[h] = (ac_kvbf, dv_qkd, C.dv.n)

    def p2_B(blk, h):
        par = h % 2
        vtok2, kdki, kvbf, qkd = vtok2p[par], kdki2p[par], kvbf2p[par], qkd2p[par]
        OB = (0 if par == 0 else 5) * 512
        CB = (2 if par == 0 else 7) * 512
        ac_kvbf, dv_qkd, dv_kdki = p2_pend[h]

        @block.tensor
        def _(e, h=h, dv_qkd=dv_qkd, dv_kdki=dv_kdki, gkv=stt_par[par], ac_kvbf=ac_kvbf,
              go=g_out_ac[par], kvbf=kvbf, vtok2=vtok2, kdki=kdki, qkd=qkd, OB=OB, CB=CB):
            e.wait_ge(C.dv.h, dv_qkd)
            e.wait_ge(C.dv.h, dv_kdki)
            e.wait_ge(C.ac.h, ac_kvbf)
            if go:
                e.wait_ge(C.ac.h, go)
            e.matmul(PS[:, OB:OB + 256], kvbf[:], qblk[:, h * 256:(h + 1) * 256],
                     start=True, stop=False)
            for sub in range(2):
                ins_ = e.matmul(PS[:, OB:OB + 256], vtok2[:, sub * P:(sub + 1) * P],
                                qkd[:, sub * 256:(sub + 1) * 256],
                                start=False, stop=(sub == 1))
            if gkv:
                e.wait_ge(C.dv.h, gkv)
            for sub in range(2):
                ins2 = e.matmul(PS[:, CB:CB + P], kdki[:, sub * P:(sub + 1) * P],
                                vtok2[:, sub * P:(sub + 1) * P], start=(sub == 0), stop=(sub == 1))
            C.pe.inc(ins_)
            C.pe.inc(ins2)
        pn_av = C.pe.n
        pe_h_hist[par] = C.pe.n

        @block.scalar
        def _(e, blk=blk, h=h, pn_av=pn_av, OB=OB):
            e.wait_ge(C.pe.h, pn_av)
            C.ac.inc(e.copy(out=outb[:, h * 256:(h + 1) * 256], in_=PS[:, OB:OB + 256]))
        g_out_ac[par] = C.ac.n

        @block.vector
        def _(e, h=h, pn_av=pn_av, CB=CB):
            e.wait_ge(C.pe.h, pn_av)
            C.dv.inc(e.scalar_tensor_tensor(out=kv[:, h * P:(h + 1) * P],
                                            in0=kv[:, h * P:(h + 1) * P],
                                            scalar=bdh[:, h:h + 1],
                                            in1=PS[:, CB:CB + P],
                                            op0=ALU.mult, op1=ALU.add))
        kv_stt_n[h] = C.dv.n
        stt_par[par] = C.dv.n

    p2_kd_dv = [0, 0]
    for blk in range(4):
        for h in range(17):
            if h < 16:
                p2_A(blk, h, an_rb, dv_add_prev)
            if h >= 1:
                p2_B(blk, h - 1)

        if blk + 1 < 4:
            @block.scalar
            def _(e, blk=blk, ph=max(pe_h_hist)):
                e.wait_ge(C.pe.h, ph)
                for nm, dd in (("q", qd_d), ("k", kd_d), ("v", vd_d)):
                    ld2[(nm, blk + 1)] = C.dma(e, "da",
                        out={"q": qblk, "k": kblk, "v": vblk}[nm][:]
                        .rearrange("p (c n) -> p c n", c=16),
                        in_=dd.rearrange("c p n -> p c n")[:, :, (blk + 1) * 256:(blk + 2) * 256])

        if L1_STAGE < 2:
            break
        pe_sq = [0, 0]
        for h in range(16):
            @block.scalar
            def _(e, h=h, g=pe_sq[h % 2]):
                if g:
                    e.wait_ge(C.pe.h, g)
                C.ac.inc(e.activation(out=sqp[h % 2][:], in_=outb[:, h * 256:(h + 1) * 256],
                                      func=AF.Square))
            an_sq = C.ac.n
            @block.tensor
            def _(e, h=h, an_sq=an_sq):
                e.wait_ge(C.ac.h, an_sq)
                C.pe.inc(e.matmul(PS[0:1, 3 * 512:3 * 512 + 256], onesc[:], sqp[h % 2][:],
                                  start=(h == 0), stop=(h == 15)))
            pe_sq[h % 2] = C.pe.n
        pn_sq = C.pe.n
        @block.vector
        def _(e, pn_sq=pn_sq):
            e.wait_ge(C.pe.h, pn_sq)
            C.dv.inc(e.tensor_scalar(out=rr[:], in0=PS[0:1, 3 * 512:3 * 512 + 256],
                                     scalar1=1.0 / D, scalar2=1e-6, op0=ALU.mult, op1=ALU.add))
            e.wait_ge(C.dv.h, C.dv.n)
        dv_rr = C.dv.n
        @block.scalar
        def _(e, dv_rr=dv_rr):
            e.wait_ge(C.dv.h, dv_rr)
            C.ac.inc(e.activation(out=rr[:], in_=rr[:], func=AF.Sqrt))
        an_rr = C.ac.n
        @block.vector
        def _(e, an_rr=an_rr):
            e.wait_ge(C.ac.h, an_rr)
            C.dv.inc(e.reciprocal(rr[:], rr[:]))
            e.wait_ge(C.dv.h, C.dv.n)
        dv_rc = C.dv.n
        @block.tensor
        def _(e, dv_rc=dv_rc):
            e.wait_ge(C.dv.h, dv_rc)
            C.pe.inc(e.matmul(PS[:, 3 * 512:3 * 512 + 256], onesrf[:], rr[:], start=True, stop=True))
        pn_rb = C.pe.n
        @block.scalar
        def _(e, pn_rb=pn_rb):
            e.wait_ge(C.pe.h, pn_rb)
            C.ac.inc(e.copy(out=rstd_bc[:], in_=PS[:, 3 * 512:3 * 512 + 256]))
        an_rb = C.ac.n

        dv_pr = 0
        gp_pr = 0
        gp_n2 = [0, 0]
        for h in range(16):
            @block.vector
            def _(e, h=h, an_rb=an_rb, needg=ld2[("g", blk)], g=gp_n2[h % 2]):
                if h == 0:
                    e.wait_ge(C.ac.h, an_rb)
                    e.wait_ge(C.da.h, needg)
                if g:
                    e.wait_ge(C.gp.h, g)
                C.dv.inc(e.scalar_tensor_tensor(out=sqp[h % 2][:], in0=outb[:, h * 256:(h + 1) * 256],
                                                scalar=normw[:, h:h + 1], in1=rstd_bc[:],
                                                op0=ALU.mult, op1=ALU.mult))
            dv_pr = C.dv.n
            @block.gpsimd
            def _(e, h=h, dv_pr=dv_pr):
                e.wait_ge(C.dv.h, dv_pr)
                C.gp.inc(e.tensor_tensor(out=prodT[:, h * 256:(h + 1) * 256], in0=sqp[h % 2][:],
                                         in1=gblk[:, h * 256:(h + 1) * 256], op=ALU.mult))
            gp_n2[h % 2] = C.gp.n
            gp_pr = C.gp.n

        if blk + 1 < 4:
            @block.scalar
            def _(e, blk=blk, gp_pr=gp_pr):
                e.wait_ge(C.gp.h, gp_pr)
                ld2[("g", blk + 1)] = C.dma(e, "da",
                    out=gblk[:].rearrange("p (c n) -> p c n", c=16),
                    in_=gd_d.rearrange("c p n -> p c n")[:, :, (blk + 1) * 256:(blk + 2) * 256])

        for dc in range(NCH):
            wbuf = wrow if dc % 2 == 0 else wrow2
            need = wo_ld[blk * NCH + dc]
            @block.tensor
            def _(e, dc=dc, need=need, gp_pr=gp_pr, dv_add=dv_add_prev,
                  wbuf=wbuf, anr=an_rb):
                e.wait_ge(need[0].h, need[1])
                if dc == 0:
                    e.wait_ge(C.gp.h, gp_pr)
                    e.wait_ge(C.ac.h, anr)
                    if dv_add:
                        e.wait_ge(C.dv.h, dv_add)
                for sub in range(2):
                    lhsT = prodT[:, dc * 256 + sub * P: dc * 256 + (sub + 1) * P]
                    for nn_ in range(4):
                        ins_ = e.matmul(PS[:, (sub * 4 + nn_) * 512:(sub * 4 + nn_ + 1) * 512],
                                        lhsT, wbuf[:, nn_ * 512:(nn_ + 1) * 512],
                                        start=(dc == 0), stop=(dc == NCH - 1))
                C.pe.inc(ins_)
            nxt = blk * NCH + dc + 2
            if nxt < 4 * NCH and nxt not in wo_ld:
                pn2 = C.pe.n
                if nxt % 2 == 0:
                    @block.scalar
                    def _(e, nxt=nxt, pn2=pn2):
                        e.wait_ge(C.pe.h, pn2)
                        wo_ld[nxt] = (C.da, C.dma(e, "da", out=wrow[:], in_=wot_bf[nxt % NCH]))
                else:
                    @block.gpsimd
                    def _(e, nxt=nxt, pn2=pn2):
                        e.wait_ge(C.pe.h, pn2)
                        wo_ld[nxt] = (C.dg, C.dma(e, "dg", out=wrow2[:], in_=wot_bf[nxt % NCH]))
        pn_op = C.pe.n
        @block.vector
        def _(e, blk=blk, pn_op=pn_op):
            e.wait_ge(C.pe.h, pn_op)
            for sub in range(2):
                xs = X[:, (blk * 2 + sub) * D:(blk * 2 + sub + 1) * D]
                ins_ = e.tensor_tensor(out=xs, in0=xs,
                                       in1=PS[:, sub * 4 * 512:(sub * 4 + 4) * 512], op=ALU.add)
            C.dv.inc(ins_)
        dv_add_prev = C.dv.n
    C.barrier()
    C.pop_scope()




def emit_l1ffn(C, X, PS):
    nc, block = C.nc, C.block
    ins = C.ins
    C.push_scope()

    XNT3 = C.sbuf("f_xnT", [P, NCH * TOK], BF16)
    hT = C.sbuf("f_hT", [P, 64 * 512], BF16)
    identb = C.sbuf("f_ident", [P, P], BF16)
    onescf = C.sbuf("f_onescf", [P, 1], F32)
    onesrf = C.sbuf("f_onesrf", [1, P], F32)
    xn2 = dbuf(C, "f_xn2", D, BF16)
    wfp = dbuf(C, "f_wfp", D, F32)
    wq = dbuf(C, "f_wq", D, BF16)
    b1c = C.sbuf("f_b1c", [P, 64], F32)
    b2r = C.sbuf("f_b2r", [1, D], BF16)
    b2bc = C.sbuf("f_b2bc", [P, D], BF16)
    qa = C.sbuf("f_qa", [P, D], BF16)
    qa_ap = qa[:]
    if C.am is None:
        acc = C.sbuf("f_acc", [P, 2], F32)
        accv = C.sbuf("f_accv", [P, 1], F32)
        sc2 = C.sbuf("f_sc2", [1, 2], F32)
        tbc = C.sbuf("f_tbc", [P, 2], F32)
        ntbc = C.sbuf("f_ntbc", [P, 2], F32)
    else:
        tbc, ntbc = C.am["tbc"], C.am["ntbc"]
    scr2 = mk_ln_scr(C, "f_ln", dump_ap=xn2[0], need_xtmp=False)
    w1q_d, w2q_d = C.scr["f_w1q"], C.scr["f_w2q"]
    ccin, ccag = C.scr["f_ccin"], C.scr["f_ccag"]

    @block.gpsimd
    def _(e):
        C.dma(e, "dg", out=identb[:], in_=ins["ident"])
        C.dma(e, "dg", out=b2r[:], in_=ins["b2row"])
    @block.sync
    def _(e):
        C.dma(e, "ds", out=onescf[:], in_=ins["ones_col"])
        C.dma(e, "ds", out=onesrf[:], in_=ins["ones_row"])
        C.dma(e, "ds", out=b1c[:], in_=ins["b1col"])
    dsn0 = C.ds.n
    dgn0 = C.dg.n

    onesrb = C.sbuf("f_onesrb", [1, P], BF16)
    @block.gpsimd
    def _(e):
        C.dma(e, "dg", out=onesrb[:], in_=ins["ones_row"])
    dgn1 = C.dg.n
    @block.tensor
    def _(e):
        e.wait_ge(C.dg.h, dgn1)
        e.wait_ge(C.dg.h, dgn0)
        for i in range(4):
            ins_ = e.matmul(PS[:, (2 + i) * 512:(3 + i) * 512], onesrb[:],
                            b2r[:, i * 512:(i + 1) * 512], start=True, stop=True)
        C.pe.inc(ins_)
    pn_b2 = C.pe.n
    @block.scalar
    def _(e, pn_b2=pn_b2):
        e.wait_ge(C.pe.h, pn_b2)
        C.ac.inc(e.copy(out=b2bc[:], in_=PS[:, 2 * 512:6 * 512]))
    ac_b2 = C.ac.n

    if C.am is None:
        dv_acc = 0
        @block.vector
        def _(e):
            e.memset(acc[:], 0.0)
            C.dv.inc(e.memset(accv[:], 0.0))
        dv_z = C.dv.n
        part_ld = []
        @block.sync
        def _(e):
            part_ld.append(C.dma(e, "ds", out=wfp[0], in_=ins["wpart"][0]))
        ac_abs = 0
        for i in range(16):
            @block.scalar
            def _(e, i=i, need=part_ld[i], dz=dv_z):
                e.wait_ge(C.ds.h, need)
                C.ac.inc(e.activation(out=qa[:], in_=wfp[0], func=AF.Abs, accum_out=accv[:]))
            ac_abs = C.ac.n
            @block.vector
            def _(e, i=i, ac_abs=ac_abs, dz=dv_z):
                e.wait_ge(C.ac.h, ac_abs)
                if i == 0:
                    e.wait_ge(C.dv.h, dz)
                col = 0 if i < 8 else 1
                C.dv.inc(e.tensor_tensor(out=acc[:, col:col + 1], in0=acc[:, col:col + 1],
                                         in1=accv[:], op=ALU.add))
                e.wait_ge(C.dv.h, C.dv.n)
            dv_acc = C.dv.n
            if i + 1 < 16:
                @block.sync
                def _(e, i=i, aa=ac_abs):
                    e.wait_ge(C.ac.h, aa)
                    part_ld.append(C.dma(e, "ds", out=wfp[0], in_=ins["wpart"][i + 1]))
        @block.tensor
        def _(e, da=dv_acc):
            e.wait_ge(C.dv.h, da)
            C.pe.inc(e.matmul(PS[0:1, 0:2], onescf[:], acc[:], start=True, stop=True))
        pn_acc = C.pe.n
        @block.scalar
        def _(e, pn_acc=pn_acc):
            e.wait_ge(C.pe.h, pn_acc)
            C.ac.inc(e.copy(out=sc2[:], in_=PS[0:1, 0:2]))
        ac_sc = C.ac.n
        @block.sync
        def _(e, ac_sc=ac_sc):
            e.wait_ge(C.ac.h, ac_sc)
            C.dma(e, "ds", out=ccin, in_=sc2[:])
        ds_cc = C.ds.n
        @block.gpsimd
        def _(e, ds_cc=ds_cc):
            e.wait_ge(C.ds.h, ds_cc)
            C.cc.inc(e.collective_compute("AllReduce", ALU.add,
                                          replica_groups=[list(range(NCORES))],
                                          ins=[ccin.opt()], outs=[ccag.opt()]))
        cc_n = C.cc.n
        @block.scalar
        def _(e, cc_n=cc_n):
            e.wait_ge(C.cc.h, cc_n)
            C.dma(e, "da", out=sc2[:], in_=ccag)
        da_sc = C.da.n
        @block.vector
        def _(e, da_sc=da_sc):
            e.wait_ge(C.da.h, da_sc)
            C.dv.inc(e.tensor_scalar(out=sc2[:], in0=sc2[:], scalar1=0.7 / (8192.0 * 2048.0),
                                     scalar2=None, op0=ALU.mult))
            e.wait_ge(C.dv.h, C.dv.n)
        dv_t = C.dv.n
        @block.tensor
        def _(e, dv_t=dv_t):
            e.wait_ge(C.dv.h, dv_t)
            C.pe.inc(e.matmul(PS[:, 0:2], onesrf[:], sc2[:], start=True, stop=True))
        pn_t = C.pe.n
        @block.scalar
        def _(e, pn_t=pn_t):
            e.wait_ge(C.pe.h, pn_t)
            C.ac.inc(e.copy(out=tbc[:], in_=PS[:, 0:2]))
        ac_t = C.ac.n
        @block.vector
        def _(e, ac_t=ac_t):
            e.wait_ge(C.ac.h, ac_t)
            C.dv.inc(e.tensor_scalar(out=ntbc[:], in0=tbc[:], scalar1=-1.0, scalar2=None,
                                     op0=ALU.mult))
            e.wait_ge(C.dv.h, C.dv.n)
        dv_nt = C.dv.n
    else:
        dv_nt = C.am["dv_nt"]

    rdy_hist = []
    pe_tr = []
    ac_ev = []
    for t in range(NT):
        pre = []
        if t >= 1:
            pre.append((C.pe, pe_tr[t - 1]))
        if t >= 2:
            k2, n2 = rdy_hist[t - 2]
            pre.append((C.dv, n2))
            pv = pe_tr[t - 2]
            @block.vector
            def _(e, pv=pv):
                e.wait_ge(C.pe.h, pv)
        scr2[t % 2]["dump"] = xn2[(t + 1) % 2]
        kind, n = emit_ln_tile(C, X[:, t * D:(t + 1) * D], xn2[t % 2], scr2[t % 2],
                               pre_wait=pre)
        rdy_hist.append((kind, n))
        prev_ev = ac_ev[t - 1] if t >= 1 else 0
        @block.tensor
        def _(e, t=t, n=n, prev_ev=prev_ev):
            e.wait_ge(C.dv.h, n)
            e.wait_ge(C.ac.h, prev_ev)
            for dc in range(NCH):
                bb = 6 + dc // 8
                view = PS[:, bb * 512:(bb + 1) * 512].bitcast(BF16)[:, (dc % 8) * P:(dc % 8 + 1) * P]
                ins_ = e.transpose(view, xn2[t % 2][:, dc * P:(dc + 1) * P], identb[:])
            C.pe.inc(ins_)
        pe_tr.append(C.pe.n)
        pn = C.pe.n
        @block.scalar
        def _(e, t=t, pn=pn):
            e.wait_ge(C.pe.h, pn)
            for half in range(2):
                out = XNT3[:].rearrange("p (c n) -> p c n", c=NCH)[:, half * 8:(half + 1) * 8,
                                                                   t * P:(t + 1) * P]
                i2 = PS[:, (6 + half) * 512:(7 + half) * 512].bitcast(BF16).rearrange(
                    "p (c n) -> p c n", c=8)
                ins_ = e.copy(out=out, in_=i2)
            C.ac.inc(ins_)
        ac_ev.append(C.ac.n)
    ac_xnT = C.ac.n

    def emit_quant(src_ap, dst_ap, guards):
        col = guards["col"]
        box = [0]
        @block.vector
        def _(e, col=col, need=guards.get("need"), gpe=guards.get("pe", 0),
              gda=guards.get("da", 0)):
            if need:
                e.wait_ge(need[0].h, need[1])
            C.dv.inc(e.tensor_scalar(out=qa_ap, in0=src_ap, scalar1=ntbc[:, col:col + 1],
                                     scalar2=None, op0=ALU.is_le))
            if gpe:
                e.wait_ge(C.pe.h, gpe)
            if gda:
                e.wait_ge(gda[0].h, gda[1])
            C.dv.inc(e.tensor_scalar(out=dst_ap, in0=src_ap, scalar1=tbc[:, col:col + 1],
                                     scalar2=None, op0=ALU.is_ge))
            box[0] = C.dv.n
            e.wait_ge(C.dv.h, C.dv.n)
            C.dv.inc(e.tensor_tensor(out=dst_ap, in0=dst_ap, in1=qa_ap, op=ALU.subtract))
        return box[0], C.dv.n

    bank_h = {0: 0, 1: 0}
    dv_down_prev = 0
    for half in range(2):
        hq_ld = []
        @block.sync
        def _(e, half=half, dnt=dv_nt, gpe=C.pe.n, gda=C.da.n):
            e.wait_ge(C.dv.h, dnt)
            if half == 0:
                hq_ld.append((C.ds, C.dma(e, "ds", out=wfp[0], in_=ins["w1_s"][0])))
            else:
                e.wait_ge(C.pe.h, gpe)
                e.wait_ge(C.da.h, gda)
                for i in range(2):
                    hq_ld.append((C.ds, C.dma(e, "ds", out=wq[i], in_=w1q_d[i])))
        da_w1sp = {}
        dv_abh = {}
        pe_prevq = {}
        for f in range(64):
            if half == 0:
                dv_ab, dv_sub = emit_quant(wfp[f % 2], wq[f % 2][:],
                    {"col": 0, "need": hq_ld[f], "pe": pe_prevq.get(f - 2, 0),
                     "da": da_w1sp.get(f - 2, 0)})
                dv_abh[f] = dv_ab
                if f + 1 < 64:
                    ga = dv_abh.get(f - 1, 0)
                    if (f + 1) % 2 == 0 or not L1F_SPLITQ:
                        @block.sync
                        def _(e, f=f, ga=ga):
                            if ga:
                                e.wait_ge(C.dv.h, ga)
                            hq_ld.append((C.ds, C.dma(e, "ds", out=wfp[(f + 1) % 2],
                                                      in_=ins["w1_s"][f + 1])))
                    else:
                        @block.gpsimd
                        def _(e, f=f, ga=ga):
                            if ga:
                                e.wait_ge(C.dv.h, ga)
                            hq_ld.append((C.dg, C.dma(e, "dg", out=wfp[(f + 1) % 2],
                                                      in_=ins["w1_s"][f + 1])))
                if L1F_SPLITQ:
                    @block.scalar
                    def _(e, f=f, dv_sub=dv_sub):
                        e.wait_ge(C.dv.h, dv_sub)
                        C.dma(e, "da", out=w1q_d[f], in_=wq[f % 2][:])
                    da_w1sp[f] = (C.da, C.da.n)
                else:
                    @block.sync
                    def _(e, f=f, dv_sub=dv_sub):
                        e.wait_ge(C.dv.h, dv_sub)
                        C.dma(e, "ds", out=w1q_d[f], in_=wq[f % 2][:])
                    da_w1sp[f] = (C.ds, C.ds.n)
                ready = (C.dv, dv_sub)
            else:
                ready = hq_ld[f]
            bk = f % 2
            @block.tensor
            def _(e, f=f, half=half, bk=bk, ready=ready, ga=bank_h[bk], axn=ac_xnT,
                  gdp=dv_down_prev):
                e.wait_ge(ready[0].h, ready[1])
                if f == 0:
                    e.wait_ge(C.ac.h, axn)
                    if gdp:
                        e.wait_ge(C.dv.h, gdp)
                if ga:
                    e.wait_ge(C.ac.h, ga)
                for dc in range(NCH):
                    lhsT = wq[f % 2][:, dc * P:(dc + 1) * P]
                    rhs = XNT3[:, dc * TOK + half * 512: dc * TOK + (half + 1) * 512]
                    ins_ = e.matmul(PS[:, bk * 512:(bk + 1) * 512], lhsT, rhs,
                                    start=(dc == 0), stop=(dc == NCH - 1))
                C.pe.inc(ins_)
            pn = C.pe.n
            pe_prevq[f] = pn
            @block.scalar
            def _(e, f=f, bk=bk, pn=pn):
                e.wait_ge(C.pe.h, pn)
                C.ac.inc(e.activation(out=hT[:, f * 512:(f + 1) * 512],
                                      in_=PS[:, bk * 512:(bk + 1) * 512], func=AF.Gelu,
                                      bias=b1c[:, f:f + 1], scale=1.0))
            bank_h[bk] = C.ac.n
            if half == 1 and f + 2 < 64:
                if f % 2 == 0 or not L1F_SPLITQ:
                    @block.sync
                    def _(e, f=f, pn=pn):
                        e.wait_ge(C.pe.h, pn)
                        hq_ld.append((C.ds, C.dma(e, "ds", out=wq[f % 2], in_=w1q_d[f + 2])))
                else:
                    @block.gpsimd
                    def _(e, f=f, pn=pn):
                        e.wait_ge(C.pe.h, pn)
                        hq_ld.append((C.dg, C.dma(e, "dg", out=wq[f % 2], in_=w1q_d[f + 2])))
        ac_h = C.ac.n

        for sg in range(2):
            dq_ld = []
            quantize_now = (half == 0 and sg == 0)
            @block.sync
            def _(e, sg=sg, quantize_now=quantize_now, gdv=C.dv.n, gpe=C.pe.n, gda=C.da.n):
                if quantize_now:
                    e.wait_ge(C.dv.h, gdv)
                    dq_ld.append((C.ds, C.dma(e, "ds", out=wfp[0], in_=ins["w2_s"][0])))
                else:
                    e.wait_ge(C.pe.h, gpe)
                    e.wait_ge(C.da.h, gda)
                    for i in range(2):
                        dq_ld.append((C.ds, C.dma(e, "ds", out=wq[i], in_=w2q_d[i])))
            da_w2sp = {}
            dv_abd = {}
            pe_prevq = {}
            if quantize_now:
                pe_prevq[-2] = pe_prevq[-1] = C.pe.n
                sq = C.da if L1F_SPLITQ else C.ds
                da_w2sp[-2] = da_w2sp[-1] = (sq, sq.n)
                dv_abd[-1] = dv_abh.get(63, 0)
            for f in range(64):
                if quantize_now:
                    dv_ab, dv_sub = emit_quant(wfp[f % 2], wq[f % 2][:],
                        {"col": 1, "need": dq_ld[f], "pe": pe_prevq.get(f - 2, 0),
                         "da": da_w2sp.get(f - 2, 0)})
                    dv_abd[f] = dv_ab
                    if f + 1 < 64:
                        ga = dv_abd.get(f - 1, 0)
                        if (f + 1) % 2 == 0 or not L1F_SPLITQ:
                            @block.sync
                            def _(e, f=f, ga=ga):
                                if ga:
                                    e.wait_ge(C.dv.h, ga)
                                dq_ld.append((C.ds, C.dma(e, "ds", out=wfp[(f + 1) % 2],
                                                          in_=ins["w2_s"][f + 1])))
                        else:
                            @block.gpsimd
                            def _(e, f=f, ga=ga):
                                if ga:
                                    e.wait_ge(C.dv.h, ga)
                                dq_ld.append((C.dg, C.dma(e, "dg", out=wfp[(f + 1) % 2],
                                                          in_=ins["w2_s"][f + 1])))
                    if L1F_SPLITQ:
                        @block.scalar
                        def _(e, f=f, dv_sub=dv_sub):
                            e.wait_ge(C.dv.h, dv_sub)
                            C.dma(e, "da", out=w2q_d[f], in_=wq[f % 2][:])
                        da_w2sp[f] = (C.da, C.da.n)
                    else:
                        @block.sync
                        def _(e, f=f, dv_sub=dv_sub):
                            e.wait_ge(C.dv.h, dv_sub)
                            C.dma(e, "ds", out=w2q_d[f], in_=wq[f % 2][:])
                        da_w2sp[f] = (C.ds, C.ds.n)
                    ready = (C.dv, dv_sub)
                else:
                    ready = dq_ld[f]
                @block.tensor
                def _(e, f=f, half=half, sg=sg, ready=ready, ach=ac_h, gdp=dv_down_prev):
                    e.wait_ge(ready[0].h, ready[1])
                    if f == 0:
                        e.wait_ge(C.ac.h, ach)
                        if gdp:
                            e.wait_ge(C.dv.h, gdp)
                    for si in range(2):
                        s = sg * 2 + si
                        lhsT = hT[:, f * 512 + s * P: f * 512 + (s + 1) * P]
                        for n_ in range(4):
                            ins_ = e.matmul(PS[:, (si * 4 + n_) * 512:(si * 4 + n_ + 1) * 512],
                                            lhsT, wq[f % 2][:, n_ * 512:(n_ + 1) * 512],
                                            start=(f == 0), stop=(f == 63))
                    C.pe.inc(ins_)
                pn = C.pe.n
                pe_prevq[f] = pn
                if (not quantize_now) and f + 2 < 64:
                    if f % 2 == 0 or not L1F_SPLITQ:
                        @block.sync
                        def _(e, f=f, pn=pn):
                            e.wait_ge(C.pe.h, pn)
                            dq_ld.append((C.ds, C.dma(e, "ds", out=wq[f % 2], in_=w2q_d[f + 2])))
                    else:
                        @block.gpsimd
                        def _(e, f=f, pn=pn):
                            e.wait_ge(C.pe.h, pn)
                            dq_ld.append((C.dg, C.dma(e, "dg", out=wq[f % 2], in_=w2q_d[f + 2])))
            pn_dn = C.pe.n
            @block.vector
            def _(e, half=half, sg=sg, pn_dn=pn_dn):
                e.wait_ge(C.pe.h, pn_dn)
                for si in range(2):
                    s = sg * 2 + si
                    tile = half * 4 + s
                    xs = X[:, tile * D:(tile + 1) * D]
                    ins_ = e.tensor_tensor(out=xs, in0=xs,
                                           in1=PS[:, si * 4 * 512:(si * 4 + 4) * 512], op=ALU.add)
                C.dv.inc(ins_)
            dv_down_prev = C.dv.n

    @block.vector
    def _(e, ac_b2=ac_b2):
        e.wait_ge(C.ac.h, ac_b2)
        e.wait_ge(C.dv.h, C.dv.n)
        for t in range(NT):
            xs = X[:, t * D:(t + 1) * D]
            ins_ = e.tensor_tensor(out=xs, in0=xs, in1=b2bc[:], op=ALU.add)
        C.dv.inc(ins_)

    C.barrier()
    C.pop_scope()


def build_program(blocks=("l0",)):
    nc = bass.Bass("TRN2", target_bir_lowering=False)
    es = ExitStack()
    ins = {}

    def din(name, shape, dtype=F32):
        ins[name] = nc.dram_tensor(name, list(shape), dtype, kind="ExternalInput").ap()

    din("x", (TOK + HALO, D) if "l0" in blocks else (TOK, D))
    xout = nc.dram_tensor("xout", [TOK, D], F32, kind="ExternalOutput").ap()
    if "l0" in blocks:
        din("wq_s", (NCH, P, D)); din("wk_s", (4, P, D)); din("wv_s", (4, P, D))
        din("wo_s", (NCH, P, D)); din("lnw_row", (1, D))
        din("cs_fm", (P, 1536)); din("sn_fm", (P, 1536)); din("mask0", (NT, P, KW))
    if "moe" in blocks:
        din("gwt", (P, NCH * 32)); din("wg_t", (32, P, NCH * 768))
        din("wu_t", (32, P, NCH * 768)); din("wd_t", (32, P, 6 * D))
    if "l1a" in blocks:
        din("wqkv_s", (48, P, D)); din("wgate_s", (NCH, P, D)); din("wot_r", (NCH, P, D))
        din("normw_col", (P, 16)); din("slope", (1, 16)); din("causal01", (P, 512))
        din("arqneg", (1, 256)); din("arkneg", (P, 2)); din("arkpos", (P, 2)); din("tcneg", (1, 8))
    if "l1f" in blocks:
        din("w1_s", (64, P, D)); din("w2_s", (64, P, D)); din("wpart", (16, P, D))
        din("b1col", (P, 64)); din("b2row", (1, D))
    din("ident", (P, P)); din("pswap", (P, P))
    din("ones_row", (1, P)); din("ones_col", (P, 1))
    if DBG_TAP:
        ins["dbg"] = nc.dram_tensor("dbg", [P, NCH * TOK], BF16, kind="ExternalOutput").ap()
        ins["dbgk"] = nc.dram_tensor("dbgk", [P, 4 * 1536], BF16, kind="ExternalOutput").ap()
        ins["dbgv"] = nc.dram_tensor("dbgv", [P, LT * 512], BF16, kind="ExternalOutput").ap()
        ins["dbgq"] = nc.dram_tensor("dbgq", [P, NCH * TOK], BF16, kind="ExternalOutput").ap()

    scr = {}
    if "l0" in blocks:
        scr["l0_qT"] = nc.dram_tensor("l0_qT", [NCH, P, TOK], BF16).ap()
    if "moe" in blocks:
        scr["moe_mw"] = nc.dram_tensor("moe_mw", [32, NT * P], BF16).ap()
    if "l1a" in blocks:
        for nm in ("l1_q", "l1_k", "l1_v", "l1_g"):
            scr[nm] = nc.dram_tensor(nm, [NCH, P, TOK], BF16).ap()
        scr["l1_wotbf"] = nc.dram_tensor("l1_wotbf", [NCH, P, D], BF16).ap()
        scr["l1_ccin"] = nc.dram_tensor("l1_ccin", [16 * P, P], F32).ap()
        scr["l1_ccag"] = nc.dram_tensor("l1_ccag", [NCORES * 16 * P, P], F32).ap()
    if "l1f" in blocks:
        qkind = {"kind": "ExternalOutput"} if L1F_DBGQ else {}
        scr["f_w1q"] = nc.dram_tensor("f_w1q", [64, P, D], BF16, **qkind).ap()
        scr["f_w2q"] = nc.dram_tensor("f_w2q", [64, P, D], BF16, **qkind).ap()
        scr["f_ccin"] = nc.dram_tensor("f_ccin", [1, 2], F32).ap()
        scr["f_ccag"] = nc.dram_tensor("f_ccag", [1, 2], F32).ap()

    block = es.enter_context(nc.Block())
    C = Ctx(nc, es, block)
    C.ins = ins
    C.scr = scr

    X = C.sbuf("X", [P, NT * D], F32)
    PS = es.enter_context(nc.psum_tensor("PS", [P, 4096], F32))[:]
    C.PS = PS
    C.am = absmean_alloc(C) if (AM_HOIST and "l0" in blocks and "l1f" in blocks) else None

    xl = []
    off = HALO if "l0" in blocks else 0
    @block.sync
    def _(e):
        for t in range(NT):
            xl.append(C.dma(e, "ds", out=X[:, t * D:(t + 1) * D],
                                           in_=ins["x"][off + t * P: off + (t + 1) * P, :]))
    for eng in ("scalar", "vector", "tensor", "gpsimd"):
        def mk(eng=eng):
            def f(e):
                e.wait_ge(C.ds.h, xl[-1])
            return f
        getattr(block, eng)(mk())

    if "l0" in blocks:
        emit_l0attn(C, X, PS)
    if "moe" in blocks:
        emit_moe(C, X, PS)
    if "l1a" in blocks:
        emit_l1attn(C, X, PS)
    if "l1f" in blocks:
        emit_l1ffn(C, X, PS)

    stn = []
    @block.sync
    def _(e):
        e.wait_ge(C.dv.h, C.dv.n)
        e.wait_ge(C.ac.h, C.ac.n)
        e.wait_ge(C.gp.h, C.gp.n)
        e.wait_ge(C.pe.h, C.pe.n)
        for t in range(NT):
            stn.append(C.dma(e, "ds", out=xout[t * P:(t + 1) * P, :],
                                            in_=X[:, t * D:(t + 1) * D]))
        e.wait_ge(C.ds.h, stn[-1])

    es.close()
    return nc


def kernel(**inputs):
    per_core, shared = make_host_inputs(inputs)
    nc = build_program(blocks=("l0", "moe", "l1a", "l1f"))
    in_maps = []
    for c in range(NCORES):
        m = dict(shared)
        m.update(per_core[c])
        in_maps.append(m)
    res = run_bass_kernel_spmd(nc, in_maps, core_ids=list(range(NCORES)))
    x = [res.results[c]["xout"] for c in range(NCORES)]
    return np.concatenate(x, axis=0).reshape(2, 4096, D)
